# version 25
# baseline (speedup 1.0000x reference)
import numpy as np
from contextlib import ExitStack

import concourse.bass as bass
import concourse.mybir as mybir
from concourse.bass_utils import run_bass_kernel_spmd

F32 = mybir.dt.float32
BF16 = mybir.dt.bfloat16
AF = mybir.ActivationFunctionType
ALU = mybir.AluOpType
AX = mybir.AxisListType

P = 128
G = 8
D = 1024
SP = 2048
TS1 = 512
NS1 = SP // TS1
TS2 = 512
NS2 = SP // TS2
NCORES = 8
FD = 4096
NPV = 16

ENGINES = ("sync", "vector", "scalar", "tensor", "gpsimd")


def _key(x):
    if isinstance(x, str):
        return x
    t = getattr(x, "tensor", x)
    return t.name


class Trk:
    NDMA = 6

    def __init__(self, nc, stack):
        self.nc = nc
        self.stack = stack
        self.sems = {}
        self.count = {}
        for e in ENGINES:
            self._mksem(e)
        self.dma_uses = {}
        self.dma_next = {}
        self.waited = {}
        for e in ENGINES:
            self.waited[e] = {}
        self.writer = {}
        self.readers = {}

    def _mksem(self, name):
        self.sems[name] = self.stack.enter_context(self.nc.semaphore(f"S_{name}"))
        self.count[name] = 0
        return self.sems[name]

    def _wait(self, engine, e, sem_name, val):
        if self.waited[engine].get(sem_name, 0) >= val:
            return
        e.wait_ge(self.sems[sem_name], val)
        self.waited[engine][sem_name] = val

    def op(self, engine, fn, reads=(), writes=(), dma=False, inc=None,
           async_sem=None):
        nc = self.nc
        e = getattr(nc, engine)
        needs = {}

        def need(sem_name, c):
            if needs.get(sem_name, 0) < c:
                needs[sem_name] = c
        for r in reads:
            w = self.writer.get(_key(r))
            if w:
                need(*w)
        for wr in writes:
            k = _key(wr)
            w = self.writer.get(k)
            if w:
                need(*w)
            for sem_name, c in self.readers.get(k, {}).items():
                need(sem_name, c)
        for sem_name, c in sorted(needs.items()):
            self._wait(engine, e, sem_name, c)
        if async_sem is not None:
            if async_sem not in self.sems:
                self._mksem(async_sem)
            sem_name = async_sem
            step = inc if inc is not None else 1
            inst = fn(e)
            inst.then_inc(self.sems[sem_name], step)
            self.count[sem_name] += step
        elif dma:
            j = self.dma_next.get(engine, 0)
            self.dma_next[engine] = (j + 1) % self.NDMA
            sem_name = f"{engine}_d{j}"
            if sem_name not in self.sems:
                self._mksem(sem_name)
            uses = self.dma_uses.get(sem_name, 0)
            if uses > 0:
                self._wait(engine, e, sem_name, 16 * uses)
            inst = fn(e)
            inst.then_inc(self.sems[sem_name], 16)
            self.dma_uses[sem_name] = uses + 1
            self.count[sem_name] = 16 * (uses + 1)
        else:
            sem_name = engine
            inst = fn(e)
            step = inc if inc is not None else 1
            inst.then_inc(self.sems[sem_name], step)
            self.count[sem_name] += step
        cnt = self.count[sem_name]
        for wr in writes:
            k = _key(wr)
            self.writer[k] = (sem_name, cnt)
            self.readers[k] = {}
        for r in reads:
            k = _key(r)
            self.readers.setdefault(k, {})
            if self.readers[k].get(sem_name, 0) < cnt:
                self.readers[k][sem_name] = cnt
        return inst

    def full_barrier(self):
        for cons in ENGINES:
            e = getattr(self.nc, cons)
            for sem_name, c in list(self.count.items()):
                if sem_name == cons or not c or sem_name.startswith("cc"):
                    continue
                self._wait(cons, e, sem_name, c)


def ilv(*gens):
    its = [iter(g) for g in gens]
    while its:
        for it in list(its):
            try:
                next(it)
            except StopIteration:
                its.remove(it)


def drain(gen):
    for _ in gen:
        pass


def fm_in(dram_ap, rows, row_stride, col0, ncols):
    t = dram_ap.tensor
    return bass.AP(t, col0, [[row_stride, P], [P * row_stride, G], [1, ncols]])


def build_program(debug=0):
    nc = bass.Bass("TRN2", target_bir_lowering=False)
    dt_in = {}

    def din(name, shape, dtype=F32):
        dt_in[name] = nc.dram_tensor(name, shape, dtype, kind="ExternalInput").ap()
        return dt_in[name]

    x_d = din("x", [D, SP])
    wk_d = din("wk", [D, D]); wv_d = din("wv", [D, D])
    wr_d = din("wr", [D, D]); wo_d = din("wo", [D, D])
    fwk_d = din("fwk", [D, FD]); fwr_d = din("fwr", [D, D])
    fwv_d = din("fwv", [FD, D])
    bw_d = din("bw", [D, D]); la_d = din("la", [D, P]); lb_d = din("lb", [P, D])
    rw_d = din("rw", [D, G]); rb_d = din("rb", [1, G])
    pv_d = din("pvec", [D, NPV])
    cns_d = din("cns", [P, 8])
    ident_d = din("ident", [P, P], BF16)
    expand_d = din("expand", [G, P], BF16)
    ones_d = din("onesrow", [1, P])
    msel0_d = din("msel0", [P, G])
    selA_d = din("selA", [P, 3 * G])
    selB_d = din("selB", [P, 3 * G])
    mbit_d = din("mbits", [P, 8])
    mhalo_d = din("mhalo", [P, 2])

    y_d = nc.dram_tensor("y", [D, SP], F32, kind="ExternalOutput").ap()

    def scr(name, cols, dtype=F32):
        return nc.dram_tensor(name, [P, cols], dtype, kind="Internal")

    x1_sp = scr("x1_sp", NS1 * G * TS1)
    h0_sp = scr("h0_sp", NS1 * G * TS1)
    ek_sp = scr("ek_sp", NS1 * G * TS1)
    ekv_sp = scr("ekv_sp", NS1 * G * TS1)
    sr_sp = scr("sr_sp", NS1 * G * TS1, BF16)
    h1_sp = scr("h1_sp", NS1 * G * TS1)
    mfk_sp = scr("mfk_sp", NS1 * G * TS1, BF16)
    sgf_sp = scr("sgf_sp", NS1 * G * TS1)
    x2p_sp = scr("x2p_sp", NS1 * G * TS1)
    bc_sc = nc.dram_tensor("bc_sc", [1, 4096], F32, kind="Internal")
    ag0_in = nc.dram_tensor("ag0_in", [P, G], F32, kind="Internal")
    ag0_out = nc.dram_tensor("ag0_out", [NCORES * P, G], F32, kind="Internal")
    ag1_in = nc.dram_tensor("ag1_in", [P, 64], F32, kind="Internal")
    ag1_out = nc.dram_tensor("ag1_out", [NCORES * P, 64], F32, kind="Internal")

    dbg = {}
    for i in range(debug):
        dbg[i] = nc.dram_tensor(f"dbg{i}", [P, G * TS1], F32, kind="ExternalOutput").ap()

    with ExitStack() as outer:
        sbuf = lambda *a: outer.enter_context(nc.sbuf_tensor(*a))
        psum = lambda *a: outer.enter_context(nc.psum_tensor(*a))
        trk = Trk(nc, outer)
        op = trk.op

        FW = {}
        rbt = sbuf("rbt", [P, G], F32)
        exp_b = sbuf("exp_b", [G, P], BF16)
        id_b = sbuf("id_b", [P, P], BF16)
        ones_r = sbuf("ones_r", [1, P], F32)
        ones_c = sbuf("ones_c", [P, 2], BF16)
        cns = sbuf("cns_s", [P, 8], F32)
        pv = sbuf("pv_s", [P, G * NPV], F32)
        dv = sbuf("dv", [P, G * 8], F32)
        SS = {}
        msel0 = sbuf("msel0_s", [P, G], F32)
        selA = sbuf("selA_s", [P, 3 * G], F32)
        selB = sbuf("selB_s", [P, 3 * G], F32)
        mbit = sbuf("mbit_s", [P, 8], F32)
        mhalo = sbuf("mhalo_s", [P, 2], F32)
        xn_last = sbuf("xn_last", [P, G], F32)
        xn2_last = sbuf("xn2_last", [P, G], F32)
        h0_end = sbuf("h0_end", [P, G], F32)
        stA = sbuf("stA", [P, G], F32)
        stB = sbuf("stB", [P, G], F32)
        penA = sbuf("penA", [P, G], F32)
        penB = sbuf("penB", [P, G], F32)
        eke = sbuf("eke", [P, G], F32)
        ekve = sbuf("ekve", [P, G], F32)
        sre = sbuf("sre", [P, G], F32)
        carA = sbuf("carA", [P, G], F32)
        carB = sbuf("carB", [P, G], F32)
        st2A = sbuf("st2A", [P, G], F32)
        st2B = sbuf("st2B", [P, G], F32)
        srh_p = sbuf("srh_p", [P, 8], BF16)
        h0sel_p = sbuf("h0sel_p", [P, 8], F32)

        pm0 = psum("pm0", [P, TS1], F32)
        pm1 = psum("pm1", [P, TS1], F32)
        pa0 = psum("pa0", [P, TS1], F32)
        pa1 = psum("pa1", [P, TS1], F32)
        prs = psum("prs", [64, TS1], F32)
        pb = psum("pb", [P, TS1], F32)
        pq = psum("pq", [P, TS1], F32)
        ptb = psum("ptb", [P, P], BF16)

        def pvc(g, j):
            return pv[:, g * NPV + j:g * NPV + j + 1]

        def dvc(g, j):
            return dv[:, g * 8 + j:g * 8 + j + 1]

        def dvrow(j):
            t = dv.tensor if hasattr(dv, "tensor") else dv
            return bass.AP(t, j, [[G * 8, P], [8, G]])

        def pvrow(j):
            t = pv.tensor if hasattr(pv, "tensor") else pv
            return bass.AP(t, j, [[G * NPV, P], [NPV, G]])

        CN = {"half": cns[:, 0:1], "neghalf": cns[:, 1:2], "negone": cns[:, 2:3],
              "eps": cns[:, 3:4], "two": cns[:, 4:5], "one": cns[:, 5:6]}
        nc.const_aps.register(F32, 0.5, cns[:, 0:1])
        nc.const_aps.register(F32, -0.5, cns[:, 1:2])
        nc.const_aps.register(F32, -1.0, cns[:, 2:3])
        nc.const_aps.register(F32, 1e-5, cns[:, 3:4])
        nc.const_aps.register(F32, 2.0, cns[:, 4:5])
        nc.const_aps.register(F32, 1.0, cns[:, 5:6])

        AH = {}

        def ag0_p1(sb):
            xe = sb("a0_xe", [P, G], F32)
            r2b = sb("a0_r2b", [P, 2], F32)
            op("tensor", lambda e: e.matmul(
                pq[:, 8:9], ones_r[:, :], SS["t"][0:1, SP - 1:SP],
                start=True, stop=True), reads=["srstd2", ones_r], writes=[pq])
            op("vector", lambda e: e.tensor_copy(r2b[:, 0:1], pq[:, 8:9]),
                reads=[pq], writes=[r2b])
            op("vector", lambda e: e.tensor_scalar(
                xe[:, :], h0_end[:, :], r2b[:, 0:1], None, ALU.mult),
                reads=[h0_end, r2b], writes=[xe])
            op("vector", lambda e: e.tensor_tensor(
                xe[:, :], xe[:, :], pvrow(0), ALU.mult),
                reads=[xe, pv], writes=[xe])
            op("vector", lambda e: e.tensor_tensor(
                xe[:, :], xe[:, :], pvrow(1), ALU.add),
                reads=[xe, pv], writes=[xe])
            op("gpsimd", lambda e: e.dma_start(out=ag0_in.ap(), in_=xe[:, :]),
                reads=[xe], writes=["ag0_in"], dma=True)
            op("gpsimd", lambda e: e.collective_compute(
                "AllGather", ALU.bypass, replica_groups=[list(range(NCORES))],
                ins=[ag0_in.ap().opt()], outs=[ag0_out.ap().opt()]),
                reads=["ag0_in"], writes=["ag0_out"], async_sem="cc0", inc=1)

        def ag0_p2(sb):
            sag = sb("a0_sag", [P, NCORES * G], F32)
            tse = sb("a0_tse", [P, NCORES * G], F32)
            op("sync", lambda e: e.dma_start(
                out=sag[:, :].rearrange("p (r c) -> p r c", r=NCORES),
                in_=bass.AP(ag0_out, 0, [[G, P], [P * G, NCORES], [1, G]])),
                reads=["ag0_out"], writes=[sag], dma=True)
            op("vector", lambda e: e.tensor_tensor(
                tse[:, :].rearrange("p (r c) -> p r c", r=NCORES),
                sag[:, :].rearrange("p (r c) -> p r c", r=NCORES),
                msel0[:, :].rearrange("p (r o) -> p r o", o=1)
                    .broadcast_to([P, NCORES, G]), ALU.mult),
                reads=[sag, msel0], writes=[tse])
            tset = tse.tensor if hasattr(tse, "tensor") else tse
            op("vector", lambda e: e.reduce_sum(
                xn_last[:, :],
                bass.AP(tset, 0, [[NCORES * G, P], [1, G], [G, NCORES]]),
                axis=AX.X), reads=[tse], writes=[xn_last])


        def emit_all():
            with nc.named_scope("p0"):
                p0_load()
            with nc.sbuf_tensor("srstd2", [1, SP], F32) as _ss:
                SS["t"] = _ss
                for name, fn in (("p1f", p1f), ("p1a", p1a)):
                    with nc.named_scope(name):
                        fn()
                trk.full_barrier()
            for name, fn in (("b1", b1), ("b2", b2), ("b3", b3)):
                with nc.named_scope(name):
                    fn()
            trk.full_barrier()

        def load_fm(dst, dram, rows, cols, dtype_tmp=None, scale=None, tmp=None,
                    tmpkey=None, tmp_cols=None):
            ng = rows // P
            srcap = bass.AP(dram.tensor, 0, [[cols, P], [P * cols, ng], [1, cols]])
            op("gpsimd", lambda e: e.dma_start(
                out=dst[:, 0:ng * cols].rearrange("p (g t) -> p g t", g=ng),
                in_=srcap),
                reads=[dram], writes=[dst], dma=True)
            if scale is not None:
                op("scalar", lambda e: e.activation(
                    dst[:, 0:ng * cols], dst[:, 0:ng * cols], AF.Copy, scale=scale),
                    reads=[dst], writes=[dst])

        def p0_load():
            for dst, src in ((cns, cns_d), (exp_b, expand_d), (id_b, ident_d),
                             (ones_r, ones_d), (msel0, msel0_d), (selA, selA_d),
                             (selB, selB_d), (mbit, mbit_d), (mhalo, mhalo_d)):
                op("sync", lambda e, d=dst, s=src: e.dma_start(out=d[:, :], in_=s),
                    reads=[src], writes=[dst], dma=True)
            op("sync", lambda e: e.dma_start(
                out=pv[:, :].rearrange("p (g t) -> p g t", g=G),
                in_=bass.AP(pv_d.tensor, 0, [[NPV, P], [P * NPV, G], [1, NPV]])),
                reads=[pv_d], writes=[pv], dma=True)
            op("vector", lambda e: e.memset(ones_c[:, :], 1.0), writes=[ones_c])
            op("vector", lambda e: e.memset(xn_last[:, :], 0.0), writes=[xn_last])
            op("vector", lambda e: e.memset(stA[:, :], 0.0), writes=[stA])
            op("vector", lambda e: e.memset(stB[:, :], 0.0), writes=[stB])
            wrow = dvrow(5)
            op("scalar", lambda e: e.activation(wrow, pvrow(4), AF.Exp),
                reads=[pv], writes=[dv])
            op("scalar", lambda e: e.activation(dvrow(0), dvrow(5), AF.Exp, scale=-1.0),
                reads=[dv], writes=[dv])
            op("scalar", lambda e: e.activation(dvrow(1), pvrow(5), AF.Exp, scale=-1.0),
                reads=[pv], writes=[dv])
            op("vector", lambda e: e.tensor_scalar(
                dvrow(6), dvrow(5), -2048.0, None, ALU.mult), reads=[dv], writes=[dv])
            op("scalar", lambda e: e.activation(dvrow(3), dvrow(6), AF.Exp),
                reads=[dv], writes=[dv])
            op("vector", lambda e: e.tensor_scalar(
                dvrow(6), dvrow(5), -2047.0, None, ALU.mult), reads=[dv], writes=[dv])
            op("scalar", lambda e: e.activation(dvrow(4), dvrow(6), AF.Exp),
                reads=[dv], writes=[dv])
            op("vector", lambda e: e.tensor_scalar(
                dvrow(2), pvrow(11), 0.5, None, ALU.mult), reads=[pv], writes=[dv])
            with nc.sbuf_tensor("wtmp0", [1, G], F32) as wtmp:
                op("sync", lambda e: e.dma_start(out=wtmp[0:1, 0:G], in_=rb_d),
                    reads=[rb_d], writes=[wtmp], dma=True)
                op("tensor", lambda e: e.matmul(pq[:, 0:G], ones_r[:, :],
                                                wtmp[0:1, 0:G], start=True, stop=True),
                    reads=[ones_r, wtmp], writes=[pq])
                op("vector", lambda e: e.tensor_copy(rbt[:, :], pq[:, 0:G]),
                    reads=[pq], writes=[rbt])
                trk.full_barrier()

        def ln_stats2(xbuf, nt, xb_tmp, sq_tmp, rows, keep_var=False):
            rowm = rows[0:1, 0:nt]
            rowr = rows[0:1, nt:2 * nt]
            rowt = rows[0:1, 2 * nt:3 * nt]
            KM, KR, KT = (rows.name + "_m", rows.name + "_r", rows.name + "_t")
            op("scalar", lambda e: e.copy(xb_tmp[:, 0:G * nt], xbuf[:, 0:G * nt]),
                reads=[xbuf], writes=[xb_tmp])
            op("scalar", lambda e: e.activation(
                sq_tmp[:, 0:G * nt], xbuf[:, 0:G * nt], AF.Square),
                reads=[xbuf], writes=[sq_tmp])

            def mm_sum(e, src, bank):
                last = None
                for kg in range(G):
                    last = e.matmul(bank, ones_c[:, 0:1],
                                    src[:, kg * nt:(kg + 1) * nt],
                                    start=(kg == 0), stop=(kg == G - 1))
                return last
            yield
            op("tensor", lambda e: mm_sum(e, xb_tmp, prs[0:1, 0:TS1]),
                reads=[xb_tmp, ones_c], writes=[prs])
            op("tensor", lambda e: mm_sum(e, sq_tmp, prs[32:33, 0:TS1]),
                reads=[sq_tmp, ones_c], writes=[prs])
            op("vector", lambda e: e.tensor_scalar(
                rowm, prs[0:1, 0:nt], 1.0 / D, None, ALU.mult),
                reads=[prs], writes=[KM])
            op("vector", lambda e: e.tensor_tensor(rowt, rowm, rowm, ALU.mult),
                reads=[KM], writes=[KT])
            op("vector", lambda e: e.scalar_tensor_tensor(
                rowr, prs[32:33, 0:nt], 1.0 / D, rowt, ALU.mult, ALU.subtract),
                reads=[prs, KT], writes=[KR])
            if keep_var:
                op("vector", lambda e: e.tensor_copy(rowt, rowr),
                    reads=[KR], writes=[KT])
            op("scalar", lambda e: e.activation(rowr, rowr, AF.Ln, bias=1e-5),
                reads=[KR], writes=[KR])
            op("scalar", lambda e: e.activation(rowr, rowr, AF.Exp, scale=-0.5),
                reads=[KR], writes=[KR])
            return rowm, rowr, rowt

        def bcast(row_ap, nt, bank, rows_key, dst=None):
            op("tensor", lambda e: e.matmul(bank[:, 0:nt], ones_r[:, :], row_ap,
                                            start=True, stop=True),
                reads=[rows_key, ones_r], writes=[bank])
            op("scalar", lambda e: e.copy(dst[:, 0:nt], bank[:, 0:nt]),
                reads=[bank], writes=[dst])

        def mm_group(bank, nt, lhs_fn, rhs_fn, nk, reads, extra=None, col0=0,
                     open_accum=False, close_accum=False):
            def go(e):
                last = None
                for k in range(nk):
                    last = e.matmul(bank[:, col0:col0 + nt], lhs_fn(k), rhs_fn(k),
                                    start=(k == 0 and not close_accum),
                                    stop=(k == nk - 1 and extra is None
                                          and not open_accum))
                if extra is not None:
                    last = e.matmul(bank[:, col0:col0 + nt], extra[0], extra[1],
                                    start=(nk == 0 and not close_accum),
                                    stop=True)
                return last
            op("tensor", go, reads=list(reads), writes=[bank])

        def alloc_ffn_weights(sb, tag):
            FW["bw"] = sb(f"ffw_bw{tag}", [P, G * D], BF16)
            FW["la"] = sb(f"ffw_la{tag}", [P, G * P], BF16)
            FW["lb"] = sb(f"ffw_lb{tag}", [P, D], BF16)
            FW["rw"] = sb(f"ffw_rw{tag}", [P, G * G], BF16)

        def load_ffn_weights(sb, tag):
            load_fm(FW["bw"], bw_d, D, D)
            yield
            load_fm(FW["la"], la_d, D, P)
            load_fm(FW["lb"], lb_d, P, D, scale=2.0)
            load_fm(FW["rw"], rw_d, D, G)
            yield

        def emit_ffn(y_bf, nt, resid_g, out_g, fb):
            ntile = (nt + P - 1) // P
            lg = fb["lg"]
            for mt in range(ntile):
                mw = min(P, nt - mt * P)
                B = mt * 48
                mm_group(pq, G, lambda k, mt=mt, mw=mw: y_bf[:, k * nt + mt * P:
                                                             k * nt + mt * P + mw],
                         lambda k: FW["rw"][:, k * G:(k + 1) * G], G,
                         [y_bf, FW["rw"]], col0=0)
                op("vector", lambda e, B=B, mw=mw: e.tensor_tensor(
                    lg[0:mw, B + 40:B + 48], pq[0:mw, 0:G],
                    rbt[0:mw, :], ALU.add),
                    reads=[pq, rbt], writes=[f"lg{mt}"])
                yield
            mm_group(pa0, nt, lambda k: FW["la"][:, k * P:(k + 1) * P],
                     lambda k: y_bf[:, k * nt:(k + 1) * nt], G, [FW["la"], y_bf])
            op("scalar", lambda e: e.copy(fb["axf"][:, 0:nt], pa0[:, 0:nt]),
                reads=[pa0], writes=["axf"])
            yield
            OB = (pm0, pm1, pq, pa0)
            for m in range(4):
                mm_group(OB[m], nt,
                         lambda k, m=m: FW["bw"][:, k * D + m * P:k * D + m * P + P],
                         lambda k: y_bf[:, k * nt:(k + 1) * nt], G,
                         [FW["bw"], y_bf] + (["axf"] if m == 3 else []),
                         open_accum=True)
                if m == 1:
                    yield
            yield
            wgb = fb["wgb"]

            def gate_op(opi, mt):
                mw = min(P, nt - mt * P)
                B = mt * 48
                lk = f"lg{mt}"
                if opi == 0:
                    op("vector", lambda e: e.max(
                        lg[0:mw, B + 8:B + 16], lg[0:mw, B + 40:B + 48]),
                        reads=[lk], writes=[lk])
                elif opi == 1:
                    op("vector", lambda e: e.tensor_scalar(
                        lg[0:mw, B + 16:B + 17], lg[0:mw, B + 8:B + 9], -1.0,
                        None, ALU.mult), reads=[lk], writes=[lk])
                elif opi == 2:
                    op("vector", lambda e: e.tensor_scalar(
                        lg[0:mw, B + 24:B + 32], lg[0:mw, B + 40:B + 48],
                        lg[0:mw, B + 9:B + 10], None, ALU.is_ge),
                        reads=[lk], writes=[lk])
                elif opi == 3:
                    op("scalar", lambda e: e.activation(
                        lg[0:mw, B + 32:B + 40], lg[0:mw, B + 40:B + 48], AF.Exp,
                        bias=lg[0:mw, B + 16:B + 17]),
                        reads=[lk], writes=[lk])
                elif opi == 4:
                    op("vector", lambda e: e.tensor_tensor(
                        lg[0:mw, B + 32:B + 40], lg[0:mw, B + 32:B + 40],
                        lg[0:mw, B + 24:B + 32], ALU.mult),
                        reads=[lk], writes=[lk])
                elif opi == 5:
                    op("vector", lambda e: e.reduce_sum(
                        lg[0:mw, B + 17:B + 18], lg[0:mw, B + 32:B + 40],
                        axis=AX.X), reads=[lk], writes=[lk])
                elif opi == 6:
                    op("vector", lambda e: e.reciprocal(
                        lg[0:mw, B + 18:B + 19], lg[0:mw, B + 17:B + 18]),
                        reads=[lk], writes=[lk])
                elif opi == 7:
                    op("vector", lambda e: e.tensor_scalar(
                        wgb[0:mw, mt * G:(mt + 1) * G], lg[0:mw, B + 32:B + 40],
                        lg[0:mw, B + 18:B + 19], None, ALU.mult),
                        reads=[lk], writes=[f"wgb{mt}"])
                elif opi == 8:
                    op("tensor", lambda e: e.transpose(
                        ptb[0:G, 0:P], wgb[:, mt * G:(mt + 1) * G], id_b[:, :]),
                        reads=[f"wgb{mt}", id_b], writes=[ptb])
                    op("scalar", lambda e: e.copy(
                        fb["wgT"][0:G, mt * P:mt * P + mw], ptb[0:G, 0:mw]),
                        reads=[ptb], writes=[f"wgT{mt}"])

            for opi in range(9):
                for mt in range(ntile):
                    gate_op(opi, mt)
                if opi in (3, 6):
                    yield
            yield
            wgTk = [f"wgT{mt}" for mt in range(ntile)]
            mm_group(pa1, nt, lambda k: exp_b[:, :],
                     lambda k: fb["wgT"][0:G, 0:nt], 1, [exp_b] + wgTk)
            op("scalar", lambda e: e.copy(fb["we"][:, 0:nt], pa1[:, 0:nt]),
                reads=[pa1], writes=[fb["we"]])
            op("vector", lambda e: e.tensor_tensor(
                fb["axw"][:, 0:nt], fb["axf"][:, 0:nt], fb["we"][:, 0:nt],
                ALU.mult), reads=["axf", fb["we"]], writes=[fb["axw"]])
            yield
            for m in range(G):
                bank = OB[m % 4]
                if m < 4:
                    mm_group(bank, nt, None, None, 0,
                             [FW["lb"], fb["axw"]],
                             extra=(FW["lb"][:, m * P:m * P + P],
                                    fb["axw"][:, 0:nt]), close_accum=True)
                else:
                    mm_group(bank, nt,
                             lambda k, m=m: FW["bw"][:, k * D + m * P:
                                                     k * D + m * P + P],
                             lambda k: y_bf[:, k * nt:(k + 1) * nt], G,
                             [FW["bw"], y_bf, fb["axw"], FW["lb"]],
                             extra=(FW["lb"][:, m * P:m * P + P],
                                    fb["axw"][:, 0:nt]))
                tb = fb["tep"][:, (m % 2) * nt:(m % 2) * nt + nt]
                tbk = f"tep{m % 2}"
                op("vector", lambda e, bank=bank, m=m, tb=tb: e.tensor_scalar(
                    tb, bank[:, 0:nt], 0.5, dvc(m, 2),
                    ALU.mult, ALU.add),
                    reads=[bank, dv], writes=[tbk])
                dst, dkey = out_g(m)
                op("gpsimd", lambda e, m=m, tb=tb: e.tensor_tensor(
                    dst, tb, resid_g(m), ALU.add),
                    reads=[tbk, fb["resid"]], writes=[dkey])
                yield

        def p1f():
            with ExitStack() as ph:
                sb = lambda *a: ph.enter_context(nc.sbuf_tensor(*a))
                alloc_ffn_weights(sb, "f")
                xsb = [sb("f_xsA", [P, G * TS1], F32),
                       sb("f_xsB", [P, G * TS1], F32)]
                xb = sb("f_xb", [P, G * TS1], BF16)
                sq = sb("f_sq", [P, G * TS1], BF16)
                xb2 = sb("f_xb2", [P, G * TS1], BF16)
                sq2 = sb("f_sq2", [P, G * TS1], BF16)
                rows = sb("f_rows", [1, 3 * TS1], F32)
                rows2 = sb("f_rows2", [1, 3 * TS1], F32)
                t1 = sb("f_t1", [P, G * TS1], F32)
                ybb = [sb("f_ybA", [P, G * TS1], BF16),
                       sb("f_ybB", [P, G * TS1], BF16)]
                h0 = sb("f_h0", [P, G * TS1], F32)
                x1b2 = [sb("f_x1A", [P, G * TS1], F32),
                        sb("f_x1B", [P, G * TS1], F32)]
                lg = sb("f_lg", [P, 192], F32)
                wgb = sb("f_wgb", [P, 4 * G], BF16)
                axf = sb("f_axf", [P, TS1], F32)
                wgT = sb("f_wgT", [G, TS1], BF16)
                we = sb("f_we", [P, TS1], BF16)
                axw = sb("f_axw", [P, TS1], BF16)
                tep = sb("f_tep", [P, 2 * TS1], F32)

                bbm = sb("f_bbm", [P, TS1], F32)
                bbr = sb("f_bbr", [P, TS1], F32)
                bbm2 = sb("f_bbm2", [P, TS1], F32)
                bbr2 = sb("f_bbr2", [P, TS1], F32)

                def f1(s):
                    xs, yb = xsb[s % 2], ybb[s % 2]
                    op("sync", lambda e, s=s: e.dma_start(
                        out=xs[:, :].rearrange("p (g t) -> p g t", g=G),
                        in_=fm_in(x_d, D, SP, s * TS1, TS1)),
                        reads=[x_d], writes=[xs], dma=True)
                    rowm, rowr, _ = yield from ln_stats2(xs, TS1, xb, sq, rows)
                    bcast(rowm, TS1, pb, rows.name + "_m", dst=bbm)
                    bcast(rowr, TS1, pa1, rows.name + "_r", dst=bbr)
                    yield
                    for g in range(G):
                        en = "gpsimd" if g < 3 else "vector"
                        op(en, lambda e, g=g, xs=xs: e.tensor_tensor(
                            t1[:, g * TS1:(g + 1) * TS1],
                            xs[:, g * TS1:(g + 1) * TS1], bbm[:, 0:TS1],
                            ALU.subtract), reads=[xs, bbm], writes=[t1])
                        op(en, lambda e, g=g, yb=yb: e.tensor_tensor(
                            yb[:, g * TS1:(g + 1) * TS1],
                            t1[:, g * TS1:(g + 1) * TS1], bbr[:, 0:TS1], ALU.mult),
                            reads=[t1, bbr], writes=[yb])
                        yield

                def f2(s):
                    xs, yb, x1 = xsb[s % 2], ybb[s % 2], x1b2[s % 2]
                    fb = {"lg": lg, "wgb": wgb, "wgT": wgT, "we": we, "axw": axw,
                          "axf": axf,
                          "tep": tep, "ptq": pa1, "resid": xs}
                    yield from emit_ffn(
                        yb, TS1, lambda g, xs=xs: xs[:, g * TS1:(g + 1) * TS1],
                        lambda g, x1=x1: (x1[:, g * TS1:(g + 1) * TS1], x1), fb)

                def f3(s):
                    x1 = x1b2[s % 2]
                    rowm, rowr, rowt = yield from ln_stats2(x1, TS1, xb2, sq2,
                                                            rows2, keep_var=True)
                    op("vector", lambda e: e.tensor_tensor(
                        rows2[0:1, 0:TS1], rowm, rowr, ALU.mult),
                        reads=[rows2.name + "_m", rows2.name + "_r"],
                        writes=[rows2.name + "_m"])
                    bcast(rows2[0:1, 0:TS1], TS1, pb, rows2.name + "_m",
                          dst=bbm2)
                    bcast(rowr, TS1, pa1, rows2.name + "_r", dst=bbr2)
                    yield
                    for g in range(G):
                        en = "gpsimd" if g < 3 else "vector"
                        op(en, lambda e, g=g, x1=x1: e.tensor_tensor(
                            h0[:, g * TS1:(g + 1) * TS1],
                            x1[:, g * TS1:(g + 1) * TS1], bbr2[:, 0:TS1],
                            ALU.mult), reads=[x1, bbr2], writes=[h0])
                        op(en, lambda e, g=g: e.tensor_tensor(
                            h0[:, g * TS1:(g + 1) * TS1],
                            h0[:, g * TS1:(g + 1) * TS1], bbm2[:, 0:TS1],
                            ALU.subtract),
                            reads=[h0, bbm2], writes=[h0])
                        yield
                    op("vector", lambda e: e.tensor_tensor(
                        rows2[0:1, 0:TS1], rowr, rowr, ALU.mult),
                        reads=[rows2.name + "_r"], writes=[rows2.name + "_m"])
                    op("vector", lambda e: e.tensor_tensor(
                        rows2[0:1, 0:TS1], rows2[0:1, 0:TS1], rowt, ALU.mult),
                        reads=[rows2.name + "_m", rows2.name + "_t"],
                        writes=[rows2.name + "_m"])
                    op("scalar", lambda e: e.activation(
                        rows2[0:1, 0:TS1], rows2[0:1, 0:TS1], AF.Ln, bias=1e-5),
                        reads=[rows2.name + "_m"], writes=[rows2.name + "_m"])
                    op("scalar", lambda e, s=s: e.activation(
                        SS["t"][0:1, s * TS1:(s + 1) * TS1], rows2[0:1, 0:TS1],
                        AF.Exp, scale=-0.5), reads=[rows2.name + "_m"],
                        writes=["srstd2"])
                    yield
                    op("gpsimd", lambda e, s=s, x1=x1: e.dma_start(
                        out=bass.AP(x1_sp, s * G * TS1, [[NS1 * G * TS1, P],
                                                         [1, G * TS1]]),
                        in_=x1[:, :]), reads=[x1], writes=["x1_sp"], dma=True)
                    op("gpsimd", lambda e, s=s: e.dma_start(
                        out=bass.AP(h0_sp, s * G * TS1, [[NS1 * G * TS1, P],
                                                         [1, G * TS1]]),
                        in_=h0[:, :]), reads=[h0], writes=["h0_sp"], dma=True)
                    yield
                    if s == NS1 - 1:
                        h0t = h0.tensor if hasattr(h0, "tensor") else h0
                        op("vector", lambda e, h0t=h0t: e.tensor_copy(
                            h0_end[:, :],
                            bass.AP(h0t, TS1 - 1, [[G * TS1, P], [TS1, G]])),
                            reads=[h0], writes=[h0_end])
                        yield

                ilv(f1(3), load_ffn_weights(sb, "f"))
                ilv(f2(3), f1(0))
                ilv(f3(3), f2(0), f1(1))
                ag0_p1(sb)
                ilv(f3(0), f2(1), f1(2))
                ilv(f3(1), f2(2))
                drain(f3(2))
                trk.full_barrier()

        def p1a():
            with ExitStack() as ph:
                sb = lambda *a: ph.enter_context(nc.sbuf_tensor(*a))
                wkb = sb("a_wk", [P, G * D], BF16)
                wvb = sb("a_wv", [P, G * D], BF16)
                wrb = sb("a_wr", [P, G * D], BF16)
                h0b = [sb("a_h0A", [P, G * TS1], F32),
                       sb("a_h0B", [P, G * TS1], F32)]
                xnb = [sb("a_xnA", [P, G * (TS1 + 1)], F32),
                       sb("a_xnB", [P, G * (TS1 + 1)], F32)]
                mkb = [sb("a_mkA", [P, G * TS1], BF16),
                       sb("a_mkB", [P, G * TS1], BF16)]
                mixb = sb("a_mix", [P, G * TS1], BF16)
                ekb = [sb("a_ekA", [P, G * TS1], F32),
                       sb("a_ekB", [P, G * TS1], F32)]
                ekvs = sb("a_ekv", [P, G * TS1], F32)
                srb = sb("a_sr", [P, G * TS1], BF16)
                scn = sb("a_scn", [P, TS1], F32)
                bb0 = sb("a_bb0", [P, TS1], F32)
                scn2 = pa1
                PITCH = TS1 + 1

                def e1(s):
                    p = s % 2
                    h0, xn, mixk = h0b[p], xnb[p], mkb[p]
                    xnt = xn.tensor if hasattr(xn, "tensor") else xn
                    op("sync", lambda e, s=s: e.dma_start(
                        out=h0[:, :],
                        in_=bass.AP(h0_sp, s * G * TS1,
                                    [[NS1 * G * TS1, P], [1, G * TS1]])),
                        reads=["h0_sp"], writes=[h0], dma=True)
                    bcast(SS["t"][0:1, s * TS1:(s + 1) * TS1], TS1, pb,
                          "srstd2", dst=bb0)
                    for g in range(G):
                        op("vector", lambda e, g=g: e.tensor_tensor(
                            xn[:, g * PITCH + 1:g * PITCH + 1 + TS1],
                            h0[:, g * TS1:(g + 1) * TS1], bb0[:, 0:TS1], ALU.mult),
                            reads=[h0, bb0], writes=[xn])
                    for g in range(G):
                        op("scalar", lambda e, g=g: e.activation(
                            xn[:, g * PITCH + 1:g * PITCH + 1 + TS1],
                            xn[:, g * PITCH + 1:g * PITCH + 1 + TS1],
                            AF.Identity, scale=pvc(g, 0), bias=pvc(g, 1)),
                            reads=[xn, pv], writes=[xn])
                    op("vector", lambda e: e.tensor_copy(
                        bass.AP(xnt, 0, [[G * PITCH, P], [PITCH, G]]),
                        xn_last[:, :]), reads=[xn_last], writes=[xn])
                    op("vector", lambda e: e.tensor_copy(
                        xn_last[:, :],
                        bass.AP(xnt, TS1, [[G * PITCH, P], [PITCH, G]])),
                        reads=[xn], writes=[xn_last])
                    op("vector", lambda e: e.tensor_tensor(
                        h0[:, :].rearrange("p (g t) -> p g t", g=G),
                        bass.AP(xnt, 1, [[G * PITCH, P], [PITCH, G], [1, TS1]]),
                        bass.AP(xnt, 0, [[G * PITCH, P], [PITCH, G], [1, TS1]]),
                        ALU.subtract), reads=[xn], writes=[h0])
                    for g in range(G):
                        op("vector", lambda e, g=g: e.scalar_tensor_tensor(
                            mixk[:, g * TS1:(g + 1) * TS1],
                            h0[:, g * TS1:(g + 1) * TS1], pvc(g, 6),
                            bass.AP(xnt, g * PITCH, [[G * PITCH, P], [1, TS1]]),
                            ALU.mult, ALU.add),
                            reads=[h0, pv, xn], writes=[mixk])

                def e2(s):
                    p = s % 2
                    h0, xn, mixk = h0b[p], xnb[p], mkb[p]
                    eks = ekb[p]
                    xnt = xn.tensor if hasattr(xn, "tensor") else xn
                    for m in range(G):
                        bank = pm0 if m % 2 == 0 else pm1
                        mm_group(bank, TS1,
                                 lambda k, m=m: wkb[:, k * D + m * P:
                                                    k * D + m * P + P],
                                 lambda k: mixk[:, k * TS1:(k + 1) * TS1], G,
                                 [wkb, mixk])
                        op("scalar", lambda e, bank=bank, m=m, eks=eks: e.activation(
                            eks[:, m * TS1:(m + 1) * TS1], bank[:, 0:TS1],
                            AF.Exp), reads=[bank], writes=[eks])
                        yield
                    for wb_, pj, kind in ((wvb, 7, "v"), (wrb, 8, "r")):
                        for g in range(G):
                            op("vector", lambda e, g=g, pj=pj, h0=h0, xnt=xnt:
                                e.scalar_tensor_tensor(
                                    mixb[:, g * TS1:(g + 1) * TS1],
                                    h0[:, g * TS1:(g + 1) * TS1], pvc(g, pj),
                                    bass.AP(xnt, g * PITCH,
                                            [[G * PITCH, P], [1, TS1]]),
                                    ALU.mult, ALU.add),
                                reads=[h0, pv, xn], writes=[mixb])
                        yield
                        for m in range(G):
                            bank = pm0 if m % 2 == 0 else pm1
                            mm_group(bank, TS1,
                                     lambda k, m=m, wb_=wb_: wb_[:, k * D + m * P:
                                                                 k * D + m * P + P],
                                     lambda k: mixb[:, k * TS1:(k + 1) * TS1], G,
                                     [wb_, mixb])
                            if kind == "v":
                                op("vector", lambda e, bank=bank, m=m, eks=eks:
                                    e.tensor_tensor(
                                        ekvs[:, m * TS1:(m + 1) * TS1],
                                        bank[:, 0:TS1],
                                        eks[:, m * TS1:(m + 1) * TS1], ALU.mult),
                                    reads=[bank, eks], writes=[ekvs])
                            else:
                                op("scalar", lambda e, bank=bank, m=m: e.activation(
                                    scn[:, 0:TS1], bank[:, 0:TS1], AF.Tanh,
                                    scale=0.5), reads=[bank], writes=[scn])
                                op("vector", lambda e, m=m: e.tensor_scalar(
                                    srb[:, m * TS1:(m + 1) * TS1], scn[:, 0:TS1],
                                    0.5, 0.5, ALU.mult, ALU.add),
                                    reads=[scn], writes=[srb])
                            yield
                    op("gpsimd", lambda e, s=s, eks=eks: e.dma_start(
                        out=bass.AP(ek_sp, s * G * TS1, [[NS1 * G * TS1, P],
                                                         [1, G * TS1]]),
                        in_=eks[:, :]), reads=[eks], writes=["ek_sp"], dma=True)
                    op("gpsimd", lambda e, s=s: e.dma_start(
                        out=bass.AP(ekv_sp, s * G * TS1, [[NS1 * G * TS1, P],
                                                          [1, G * TS1]]),
                        in_=ekvs[:, :]), reads=[ekvs], writes=["ekv_sp"], dma=True)
                    op("gpsimd", lambda e, s=s: e.dma_start(
                        out=bass.AP(sr_sp, s * G * TS1, [[NS1 * G * TS1, P],
                                                         [1, G * TS1]]),
                        in_=srb[:, :]), reads=[srb], writes=["sr_sp"], dma=True)
                    yield

                def e3(s):
                    eks = ekb[s % 2]
                    for arr, st_, pen in ((ekvs, stA, penA), (eks, stB, penB)):
                        for g in range(G):
                            op("vector", lambda e, g=g, arr=arr, st_=st_:
                                e.tensor_tensor_scan(
                                    scn2[:, 0:TS1], dvc(g, 0).broadcast_to([P, TS1]),
                                    arr[:, g * TS1:(g + 1) * TS1], st_[:, g:g + 1],
                                    ALU.mult, ALU.add),
                                reads=[dv, arr, st_], writes=[scn2])
                            if s == NS1 - 1:
                                op("vector", lambda e, g=g, pen=pen: e.tensor_copy(
                                    pen[:, g:g + 1], scn2[:, TS1 - 2:TS1 - 1]),
                                    reads=[scn2], writes=[pen])
                            op("vector", lambda e, g=g, st_=st_: e.tensor_copy(
                                st_[:, g:g + 1], scn2[:, TS1 - 1:TS1]),
                                reads=[scn2], writes=[st_])
                            yield
                    if s == NS1 - 1:
                        for arr, dst in ((eks, eke), (ekvs, ekve), (srb, sre)):
                            at = arr.tensor if hasattr(arr, "tensor") else arr
                            op("vector", lambda e, at=at, dst=dst, arr=arr:
                                e.tensor_copy(
                                    dst[:, :],
                                    bass.AP(at, TS1 - 1, [[G * TS1, P], [TS1, G]])),
                                reads=[arr], writes=[dst])
                        yield

                def lw_a():
                    load_fm(wkb, wk_d, D, D)
                    yield
                    load_fm(wvb, wv_d, D, D)
                    yield
                    load_fm(wrb, wr_d, D, D)
                    yield
                drain(lw_a())
                ag0_p2(sb)
                e1(0)
                e1(1)
                drain(e2(0))
                e1(2)
                ilv(e3(0), e2(1))
                e1(3)
                ilv(e3(1), e2(2))
                ilv(e3(2), e2(3))
                drain(e3(3))
                ag1_p1(sb)
                trk.full_barrier()

        def sel_from(sag, msk, col0, width, dst, dkey, tse):
            op("vector", lambda e: e.tensor_tensor(
                tse[:, 0:NCORES * width].rearrange("p (r c) -> p r c", r=NCORES),
                sag[:, :].rearrange("p (r c) -> p r c", r=NCORES)[:, :, col0:col0 + width],
                msk.rearrange("p (r o) -> p r o", o=1).broadcast_to([P, NCORES, width]),
                ALU.mult), reads=[sag, msk], writes=[tse])
            tset = tse.tensor if hasattr(tse, "tensor") else tse
            op("vector", lambda e: e.reduce_sum(
                dst, bass.AP(tset, 0, [[NCORES * 64, P], [1, width], [width, NCORES]]),
                axis=AX.X), reads=[tse], writes=[dkey])

        def ag1_p1(sb):
            pk = sb("g_pk", [P, 64], F32)
            for i, src_ in enumerate((stA, stB, penA, penB, eke, ekve, sre,
                                      h0_end)):
                op("vector", lambda e, i=i, src_=src_: e.tensor_copy(
                    pk[:, i * 8:(i + 1) * 8], src_[:, :]),
                    reads=[src_], writes=[pk])
            op("gpsimd", lambda e: e.dma_start(out=ag1_in.ap(), in_=pk[:, :]),
                reads=[pk], writes=["ag1_in"], dma=True)
            op("gpsimd", lambda e: e.collective_compute(
                "AllGather", ALU.bypass, replica_groups=[list(range(NCORES))],
                ins=[ag1_in.ap().opt()], outs=[ag1_out.ap().opt()]),
                reads=["ag1_in"], writes=["ag1_out"], async_sem="cc1", inc=1)

        def ag1_p2(sb):
            if True:
                pkunused = None
                sag = sb("g_sag", [P, NCORES * 64], F32)

                tse = sb("g_tse", [P, NCORES * 64], F32)
                dsp2 = sb("g_dsp2", [P, 16], F32)
                SAB = sb("g_sab", [P, 16], F32)
                SAB2 = sb("g_sab2", [P, 16], F32)
                selbuf = sb("g_selb", [P, 16], F32)
                dm = sb("g_dm", [P, 16], F32)
                hl = sb("g_hl", [P, 64], F32)
                op("gpsimd", lambda e: e.dma_start(
                    out=sag[:, :].rearrange("p (r c) -> p r c", r=NCORES),
                    in_=bass.AP(ag1_out, 0, [[64, P], [P * 64, NCORES], [1, 64]])),
                    reads=["ag1_out"], writes=[sag], dma=True)
                op("vector", lambda e: e.tensor_copy(dsp2[:, 0:8], dvrow(3)),
                    reads=[dv], writes=[dsp2])
                op("vector", lambda e: e.tensor_copy(dsp2[:, 8:16], dvrow(3)),
                    reads=[dv], writes=[dsp2])

                def fold(dst, selm, mb_off):
                    sel_from(sag, selm[:, 0:G], 0, 16, dst[:, :], dst, tse)
                    for j in (1, 2):
                        op("vector", lambda e, j=j: e.tensor_scalar(
                            dm[:, :], dsp2[:, :],
                            mbit[:, mb_off + 2 * (j - 1):mb_off + 2 * (j - 1) + 1],
                            None, ALU.mult), reads=[dsp2, mbit], writes=[dm])
                        op("vector", lambda e, j=j: e.tensor_scalar(
                            dm[:, :], dm[:, :],
                            mbit[:, mb_off + 2 * (j - 1) + 1:mb_off + 2 * (j - 1) + 2],
                            None, ALU.add), reads=[dm, mbit], writes=[dm])
                        op("vector", lambda e: e.tensor_tensor(
                            dst[:, :], dst[:, :], dm[:, :], ALU.mult),
                            reads=[dst, dm], writes=[dst])
                        sel_from(sag, selm[:, j * G:(j + 1) * G], 0, 16,
                                 selbuf[:, :], selbuf, tse)
                        op("vector", lambda e: e.tensor_tensor(
                            dst[:, :], dst[:, :], selbuf[:, :], ALU.add),
                            reads=[dst, selbuf], writes=[dst])
                fold(SAB, selA, 0)
                fold(SAB2, selB, 4)
                op("vector", lambda e: e.tensor_copy(carA[:, :], SAB[:, 0:8]),
                    reads=[SAB], writes=[carA])
                op("vector", lambda e: e.tensor_copy(carB[:, :], SAB[:, 8:16]),
                    reads=[SAB], writes=[carB])
                sel_from(sag, msel0[:, :], 16, 16, hl[:, 0:16], hl, tse)
                sel_from(sag, msel0[:, :], 32, 32, hl[:, 16:48], hl, tse)
                op("vector", lambda e: e.tensor_copy(dm[:, 0:8], dvrow(4)),
                    reads=[dv], writes=[dm])
                op("vector", lambda e: e.tensor_copy(dm[:, 8:16], dvrow(4)),
                    reads=[dv], writes=[dm])
                op("vector", lambda e: e.tensor_tensor(
                    SAB2[:, :], SAB2[:, :], dm[:, :], ALU.mult),
                    reads=[SAB2, dm], writes=[SAB2])
                op("vector", lambda e: e.tensor_tensor(
                    hl[:, 0:16], hl[:, 0:16], SAB2[:, :], ALU.add),
                    reads=[hl, SAB2], writes=[hl])
                op("vector", lambda e: e.tensor_tensor(
                    hl[:, 48:56], hl[:, 0:8], dvrow(1), ALU.mult),
                    reads=[hl, dv], writes=[hl])
                op("vector", lambda e: e.tensor_tensor(
                    hl[:, 48:56], hl[:, 48:56], hl[:, 24:32], ALU.add),
                    reads=[hl], writes=[hl])
                op("vector", lambda e: e.tensor_tensor(
                    hl[:, 56:64], hl[:, 8:16], dvrow(1), ALU.mult),
                    reads=[hl, dv], writes=[hl])
                op("vector", lambda e: e.tensor_tensor(
                    hl[:, 56:64], hl[:, 56:64], hl[:, 16:24], ALU.add),
                    reads=[hl], writes=[hl])
                op("vector", lambda e: e.tensor_scalar(
                    hl[:, 56:64], hl[:, 56:64], mhalo[:, 1:2], None, ALU.add),
                    reads=[hl, mhalo], writes=[hl])
                op("vector", lambda e: e.reciprocal(hl[:, 56:64], hl[:, 56:64]),
                    reads=[hl], writes=[hl])
                op("vector", lambda e: e.tensor_tensor(
                    hl[:, 48:56], hl[:, 48:56], hl[:, 56:64], ALU.mult),
                    reads=[hl], writes=[hl])
                srh = sb("g_srh", [P, 8], BF16)
                op("vector", lambda e: e.tensor_tensor(
                    srh[:, :], hl[:, 48:56], hl[:, 32:40], ALU.mult),
                    reads=[hl], writes=[srh])
                op("vector", lambda e: e.tensor_copy(srh_p[:, :], srh[:, :]),
                    reads=[srh], writes=[srh_p])
                op("vector", lambda e: e.tensor_copy(h0sel_p[:, :], hl[:, 40:48]),
                    reads=[hl], writes=[h0sel_p])

        def b1():
            with ExitStack() as ph:
                sb = lambda *a: ph.enter_context(nc.sbuf_tensor(*a))
                wob = sb("b1_wo", [P, G * D], BF16)
                frb_w = sb("b1_fwr", [P, G * D], BF16)
                eks = sb("b1_ek", [P, G * TS1], F32)
                ekvs = sb("b1_ekv", [P, G * TS1], F32)
                srs = sb("b1_sr", [P, G * TS1], BF16)
                h0 = sb("b1_h0", [P, G * TS1], F32)
                scA = sb("b1_scA", [P, G * (TS1 + 1)], F32)
                scB = sb("b1_scB", [P, G * (TS1 + 1)], F32)
                numb = sb("b1_num", [P, G * TS1], F32)
                denb = sb("b1_den", [P, G * TS1], F32)
                srout = sb("b1_srout", [P, G * TS1], BF16)
                xn2 = sb("b1_xn2", [P, G * (TS1 + 1)], F32)
                rows = sb("b1_rows", [1, 3 * TS1], F32)
                mixb = sb("b1_mix", [P, G * TS1], BF16)
                bb0 = sb("b1_bb0", [P, TS1], F32)
                bb1 = sb("b1_bb1", [P, TS1], F32)
                xb = srout
                sq = mixb
                load_fm(wob, wo_d, D, D)
                load_fm(frb_w, fwr_d, D, D)
                ag1_p2(sb)
                PITCH = TS1 + 1
                scAt = scA.tensor if hasattr(scA, "tensor") else scA
                scBt = scB.tensor if hasattr(scB, "tensor") else scB
                xn2t = xn2.tensor if hasattr(xn2, "tensor") else xn2

                def go_projh(e):
                    last = None
                    for m in range(G):
                        for k in range(G):
                            last = e.matmul(pq[:, m:m + 1],
                                            wob[:, k * D + m * P:k * D + m * P + P],
                                            srh_p[:, k:k + 1], start=(k == 0),
                                            stop=(k == G - 1))
                    return last
                op("tensor", go_projh, reads=[wob, srh_p], writes=[pq])
                hh = sb("b1_hh", [P, 16], F32)
                op("vector", lambda e: e.tensor_tensor(
                    hh[:, 0:8], pq[:, 0:8], h0sel_p[:, :], ALU.add),
                    reads=[pq, h0sel_p], writes=[hh])
                hb = sb("b1_hb", [P, 16], BF16)
                op("scalar", lambda e: e.copy(hb[:, 0:8], hh[:, 0:8]),
                    reads=[hh], writes=[hb])
                op("scalar", lambda e: e.activation(hb[:, 8:16], hh[:, 0:8],
                                                    AF.Square),
                    reads=[hh], writes=[hb])

                def go_hsum(e):
                    last = None
                    for k in range(G):
                        last = e.matmul(pq[0:1, 8:9], ones_c[:, 0:1], hb[:, k:k + 1],
                                        start=(k == 0), stop=(k == G - 1))
                    for k in range(G):
                        last = e.matmul(pq[0:1, 9:10], ones_c[:, 0:1],
                                        hb[:, 8 + k:9 + k],
                                        start=(k == 0), stop=(k == G - 1))
                    return last
                op("tensor", go_hsum, reads=[hb, ones_c], writes=[pq])
                hr = sb("b1_hr", [1, 4], F32)
                op("vector", lambda e: e.tensor_scalar(
                    hr[0:1, 0:2], pq[0:1, 8:10], 1.0 / D, None, ALU.mult),
                    reads=[pq], writes=[hr])
                op("vector", lambda e: e.tensor_tensor(
                    hr[0:1, 2:3], hr[0:1, 0:1], hr[0:1, 0:1], ALU.mult),
                    reads=[hr], writes=[hr])
                op("vector", lambda e: e.tensor_tensor(
                    hr[0:1, 1:2], hr[0:1, 1:2], hr[0:1, 2:3], ALU.subtract),
                    reads=[hr], writes=[hr])
                op("scalar", lambda e: e.activation(hr[0:1, 1:2], hr[0:1, 1:2],
                                                    AF.Ln, bias=1e-5),
                    reads=[hr], writes=[hr])
                op("scalar", lambda e: e.activation(hr[0:1, 1:2], hr[0:1, 1:2],
                                                    AF.Exp, scale=-0.5),
                    reads=[hr], writes=[hr])
                op("tensor", lambda e: e.matmul(pq[:, 10:11], ones_r[:, :],
                                                hr[0:1, 0:1], start=True, stop=True),
                    reads=[hr, ones_r], writes=[pq])
                op("tensor", lambda e: e.matmul(pq[:, 11:12], ones_r[:, :],
                                                hr[0:1, 1:2], start=True, stop=True),
                    reads=[hr, ones_r], writes=[pq])
                op("vector", lambda e: e.tensor_scalar(
                    hh[:, 8:16], hh[:, 0:8], pq[:, 10:11], None, ALU.subtract),
                    reads=[hh, pq], writes=[hh])
                op("vector", lambda e: e.tensor_scalar(
                    hh[:, 8:16], hh[:, 8:16], pq[:, 11:12], None, ALU.mult),
                    reads=[hh, pq], writes=[hh])
                op("vector", lambda e: e.tensor_tensor(
                    hh[:, 8:16], hh[:, 8:16], pvrow(2), ALU.mult),
                    reads=[hh, pv], writes=[hh])
                op("vector", lambda e: e.tensor_tensor(
                    hh[:, 8:16], hh[:, 8:16], pvrow(3), ALU.add),
                    reads=[hh, pv], writes=[hh])
                op("vector", lambda e: e.tensor_scalar(
                    xn2_last[:, :], hh[:, 8:16], mhalo[:, 0:1], None, ALU.mult),
                    reads=[hh, mhalo], writes=[xn2_last])
                op("vector", lambda e: e.tensor_copy(st2A[:, :], carA[:, :]),
                    reads=[carA], writes=[st2A])
                op("vector", lambda e: e.tensor_copy(st2B[:, :], carB[:, :]),
                    reads=[carB], writes=[st2B])

                def g1(s):
                    for (dst, spt, key) in ((eks, ek_sp, "ek_sp"),
                                            (ekvs, ekv_sp, "ekv_sp"),
                                            (srs, sr_sp, "sr_sp")):
                        op("sync", lambda e, dst=dst, spt=spt, s=s: e.dma_start(
                            out=dst[:, :],
                            in_=bass.AP(spt, s * G * TS1,
                                        [[NS1 * G * TS1, P], [1, G * TS1]])),
                            reads=[key], writes=[dst], dma=True)

                def g1b(s):
                    op("sync", lambda e, s=s: e.dma_start(
                        out=h0[:, :],
                        in_=bass.AP(h0_sp, s * G * TS1,
                                    [[NS1 * G * TS1, P], [1, G * TS1]])),
                        reads=["h0_sp"], writes=[h0], dma=True)

                def g2(s):
                    for (sc_, sct_, st_, arr) in ((scA, scAt, st2A, ekvs),
                                                  (scB, scBt, st2B, eks)):
                        for g in range(G):
                            op("vector", lambda e, g=g, sc_=sc_, st_=st_:
                                e.tensor_copy(
                                    sc_[:, g * PITCH:g * PITCH + 1], st_[:, g:g + 1]),
                                reads=[st_], writes=[sc_])
                            op("vector", lambda e, g=g, sc_=sc_, st_=st_, arr=arr:
                                e.tensor_tensor_scan(
                                    sc_[:, g * PITCH + 1:g * PITCH + 1 + TS1],
                                    dvc(g, 0).broadcast_to([P, TS1]),
                                    arr[:, g * TS1:(g + 1) * TS1], st_[:, g:g + 1],
                                    ALU.mult, ALU.add),
                                reads=[dv, arr, st_], writes=[sc_])
                            op("vector", lambda e, g=g, sc_=sc_, st_=st_:
                                e.tensor_copy(
                                    st_[:, g:g + 1],
                                    sc_[:, g * PITCH + TS1:g * PITCH + TS1 + 1]),
                                reads=[sc_], writes=[st_])
                            yield
                    for g in range(G):
                        op("vector", lambda e, g=g: e.scalar_tensor_tensor(
                            numb[:, g * TS1:(g + 1) * TS1],
                            bass.AP(scAt, g * PITCH, [[G * PITCH, P], [1, TS1]]),
                            dvc(g, 1), ekvs[:, g * TS1:(g + 1) * TS1],
                            ALU.mult, ALU.add),
                            reads=[scA, dv, ekvs], writes=[numb])
                        op("vector", lambda e, g=g: e.scalar_tensor_tensor(
                            denb[:, g * TS1:(g + 1) * TS1],
                            bass.AP(scBt, g * PITCH, [[G * PITCH, P], [1, TS1]]),
                            dvc(g, 1), eks[:, g * TS1:(g + 1) * TS1],
                            ALU.mult, ALU.add),
                            reads=[scB, dv, eks], writes=[denb])
                        yield
                    op("scalar", lambda e: e.activation(denb[:, :], denb[:, :], AF.Ln),
                        reads=[denb], writes=[denb])
                    yield
                    op("scalar", lambda e: e.activation(denb[:, :], denb[:, :],
                                                        AF.Exp, scale=-1.0),
                        reads=[denb], writes=[denb])
                    yield
                    op("gpsimd", lambda e: e.tensor_tensor(
                        numb[:, :], numb[:, :], denb[:, :], ALU.mult),
                        reads=[numb, denb], writes=[numb])
                    yield
                    op("gpsimd", lambda e: e.tensor_tensor(
                        srout[:, :], numb[:, :], srs[:, :], ALU.mult),
                        reads=[numb, srs], writes=[srout])
                    yield

                def g3(s):
                    h1 = h0
                    t1 = denb
                    for m in range(G):
                        bank = pm0 if m % 2 == 0 else pm1
                        mm_group(bank, TS1,
                                 lambda k, m=m: wob[:, k * D + m * P:k * D + m * P + P],
                                 lambda k: srout[:, k * TS1:(k + 1) * TS1], G,
                                 [wob, srout])
                        op("vector", lambda e, bank=bank, m=m: e.tensor_tensor(
                            h1[:, m * TS1:(m + 1) * TS1], bank[:, 0:TS1],
                            h0[:, m * TS1:(m + 1) * TS1], ALU.add),
                            reads=[bank, h0], writes=[h0])
                        yield
                    op("gpsimd", lambda e, s=s: e.dma_start(
                        out=bass.AP(h1_sp, s * G * TS1, [[NS1 * G * TS1, P],
                                                         [1, G * TS1]]),
                        in_=h1[:, :]), reads=[h0], writes=["h1_sp"], dma=True)
                    yield
                    rowm, rowr, _ = yield from ln_stats2(h1, TS1, xb, sq, rows)
                    yield
                    bcast(rowm, TS1, pb, rows.name + "_m", dst=bb0)
                    bcast(rowr, TS1, pa1, rows.name + "_r", dst=bb1)
                    yield
                    for g in range(G):
                        en = "gpsimd" if g < 3 else "vector"
                        op(en, lambda e, g=g: e.tensor_tensor(
                            xn2[:, g * PITCH + 1:g * PITCH + 1 + TS1],
                            h1[:, g * TS1:(g + 1) * TS1], bb0[:, 0:TS1],
                            ALU.subtract), reads=[h0, bb0], writes=[xn2])
                        op(en, lambda e, g=g: e.tensor_tensor(
                            xn2[:, g * PITCH + 1:g * PITCH + 1 + TS1],
                            xn2[:, g * PITCH + 1:g * PITCH + 1 + TS1],
                            bb1[:, 0:TS1], ALU.mult),
                            reads=[xn2, bb1], writes=[xn2])
                        op("scalar", lambda e, g=g: e.activation(
                            xn2[:, g * PITCH + 1:g * PITCH + 1 + TS1],
                            xn2[:, g * PITCH + 1:g * PITCH + 1 + TS1],
                            AF.Identity, scale=pvc(g, 2), bias=pvc(g, 3)),
                            reads=[xn2, pv], writes=[xn2])
                        yield
                    op("vector", lambda e: e.tensor_copy(
                        bass.AP(xn2t, 0, [[G * PITCH, P], [PITCH, G]]),
                        xn2_last[:, :]), reads=[xn2_last], writes=[xn2])
                    op("vector", lambda e: e.tensor_copy(
                        xn2_last[:, :],
                        bass.AP(xn2t, TS1, [[G * PITCH, P], [PITCH, G]])),
                        reads=[xn2], writes=[xn2_last])
                    yield
                    dbuf2 = h0
                    op("vector", lambda e: e.tensor_tensor(
                        dbuf2[:, 0:G * TS1].rearrange("p (g t) -> p g t", g=G),
                        bass.AP(xn2t, 1, [[G * PITCH, P], [PITCH, G], [1, TS1]]),
                        bass.AP(xn2t, 0, [[G * PITCH, P], [PITCH, G], [1, TS1]]),
                        ALU.subtract), reads=[xn2], writes=[h0])
                    yield
                    for g in range(G):
                        op("vector", lambda e, g=g: e.scalar_tensor_tensor(
                            mixb[:, g * TS1:(g + 1) * TS1],
                            dbuf2[:, g * TS1:(g + 1) * TS1], pvc(g, 9),
                            bass.AP(xn2t, g * PITCH, [[G * PITCH, P], [1, TS1]]),
                            ALU.mult, ALU.add),
                            reads=[h0, pv, xn2], writes=[mixb])
                        yield
                    op("gpsimd", lambda e, s=s: e.dma_start(
                        out=bass.AP(mfk_sp, s * G * TS1, [[NS1 * G * TS1, P],
                                                          [1, G * TS1]]),
                        in_=mixb[:, :]), reads=[mixb], writes=["mfk_sp"], dma=True)
                    yield
                    for g in range(G):
                        op("vector", lambda e, g=g: e.scalar_tensor_tensor(
                            mixb[:, g * TS1:(g + 1) * TS1],
                            dbuf2[:, g * TS1:(g + 1) * TS1], pvc(g, 10),
                            bass.AP(xn2t, g * PITCH, [[G * PITCH, P], [1, TS1]]),
                            ALU.mult, ALU.add),
                            reads=[h0, pv, xn2], writes=[mixb])
                        yield
                    for m in range(G):
                        bank = pm0 if m % 2 == 0 else pm1
                        mm_group(bank, TS1,
                                 lambda k, m=m: frb_w[:, k * D + m * P:
                                                      k * D + m * P + P],
                                 lambda k: mixb[:, k * TS1:(k + 1) * TS1], G,
                                 [frb_w, mixb])
                        op("scalar", lambda e, bank=bank: e.activation(
                            pa1[:, 0:TS1], bank[:, 0:TS1], AF.Tanh, scale=0.5),
                            reads=[bank], writes=[pa1])
                        op("vector", lambda e, m=m: e.tensor_scalar(
                            numb[:, m * TS1:(m + 1) * TS1], pa1[:, 0:TS1],
                            0.5, 0.5, ALU.mult, ALU.add),
                            reads=[pa1], writes=[numb])
                        yield
                    op("gpsimd", lambda e, s=s: e.dma_start(
                        out=bass.AP(sgf_sp, s * G * TS1, [[NS1 * G * TS1, P],
                                                          [1, G * TS1]]),
                        in_=numb[:, :]), reads=[numb], writes=["sgf_sp"], dma=True)
                    yield

                g1(0)
                g1b(0)
                drain(g2(0))
                g1(1)
                ilv(g3(0), g2(1))
                g1b(1)
                g1(2)
                ilv(g3(1), g2(2))
                g1b(2)
                g1(3)
                ilv(g3(2), g2(3))
                g1b(3)
                drain(g3(3))
                trk.full_barrier()

        def b2():
            with ExitStack() as ph:
                sb = lambda *a: ph.enter_context(nc.sbuf_tensor(*a))
                fkb = sb("b2_fwk", [P, G * FD], BF16)
                fvb = sb("b2_fwv", [P, 32 * D], BF16)
                for h in range(4):
                    q = "gpsimd"
                    srcap = bass.AP(fwk_d.tensor, h * D,
                                    [[FD, P], [P * FD, G], [1, D]])
                    op(q, lambda e, srcap=srcap, h=h: e.dma_start(
                        out=fkb[:, :].rearrange("p (g t) -> p g t", g=G)
                        [:, :, h * D:(h + 1) * D], in_=srcap),
                        reads=[fwk_d], writes=[fkb], dma=True)
                for h in range(4):
                    q = "gpsimd"
                    srcap = bass.AP(fwv_d.tensor, h * 8 * P * D,
                                    [[D, P], [P * D, G], [1, D]])
                    op(q, lambda e, srcap=srcap, h=h: e.dma_start(
                        out=fvb[:, h * 8 * D:(h + 1) * 8 * D]
                        .rearrange("p (g t) -> p g t", g=G), in_=srcap),
                        reads=[fwv_d], writes=[fvb], dma=True)
                mfk = sb("b2_mfk", [P, G * TS2], BF16)
                h1q = sb("b2_h1", [P, G * TS2], F32)
                sgq = sb("b2_sg", [P, G * TS2], F32)
                rfk = sb("b2_rfk", [P, 32 * TS2], BF16)
                tv = sb("b2_tv", [P, 2 * TS2], F32)
                x2p = h1q
                for s2 in range(NS2):
                    s = s2
                    off = 0
                    op("sync", lambda e, s=s, off=off: e.dma_start(
                        out=mfk[:, :].rearrange("p (g t) -> p g t", g=G),
                        in_=bass.AP(mfk_sp, s * G * TS1 + off,
                                    [[NS1 * G * TS1, P], [TS1, G], [1, TS2]])),
                        reads=["mfk_sp"], writes=[mfk], dma=True)
                    op("sync", lambda e, s=s, off=off: e.dma_start(
                        out=h1q[:, :].rearrange("p (g t) -> p g t", g=G),
                        in_=bass.AP(h1_sp, s * G * TS1 + off,
                                    [[NS1 * G * TS1, P], [TS1, G], [1, TS2]])),
                        reads=["h1_sp"], writes=[h1q], dma=True)
                    op("sync", lambda e, s=s, off=off: e.dma_start(
                        out=sgq[:, :].rearrange("p (g t) -> p g t", g=G),
                        in_=bass.AP(sgf_sp, s * G * TS1 + off,
                                    [[NS1 * G * TS1, P], [TS1, G], [1, TS2]])),
                        reads=["sgf_sp"], writes=[sgq], dma=True)
                    for fo in range(32):
                        bank = pm0 if fo % 2 == 0 else pm1
                        mm_group(bank, TS2,
                                 lambda k, fo=fo: fkb[:, k * FD + fo * P:
                                                      k * FD + fo * P + P],
                                 lambda k: mfk[:, k * TS2:(k + 1) * TS2], G,
                                 [fkb, mfk])
                        op("scalar", lambda e, bank=bank, fo=fo: e.activation(
                            rfk[:, fo * TS2:(fo + 1) * TS2], bank[:, 0:TS2],
                            AF.Relu), reads=[bank], writes=[rfk])
                        op("scalar", lambda e, fo=fo: e.activation(
                            rfk[:, fo * TS2:(fo + 1) * TS2],
                            rfk[:, fo * TS2:(fo + 1) * TS2],
                            AF.Square), reads=[rfk], writes=[rfk])
                    for m in range(G):
                        bank = pa0 if m % 2 == 0 else pa1
                        mm_group(bank, TS2,
                                 lambda k, m=m: fvb[:, k * D + m * P:
                                                    k * D + m * P + P],
                                 lambda k: rfk[:, k * TS2:(k + 1) * TS2], 32,
                                 [fvb, rfk])
                        tvs = tv[:, (m % 2) * TS2:(m % 2) * TS2 + TS2]
                        tvk = f"tv{m % 2}"
                        op("vector", lambda e, bank=bank, m=m, tvs=tvs: e.tensor_tensor(
                            tvs, bank[:, 0:TS2],
                            sgq[:, m * TS2:(m + 1) * TS2], ALU.mult),
                            reads=[bank, sgq], writes=[tvk])
                        op("gpsimd", lambda e, m=m, tvs=tvs: e.tensor_tensor(
                            x2p[:, m * TS2:(m + 1) * TS2], tvs,
                            h1q[:, m * TS2:(m + 1) * TS2], ALU.add),
                            reads=[tvk, h1q], writes=[x2p])
                    op("gpsimd", lambda e, s=s, off=off: e.dma_start(
                        out=bass.AP(x2p_sp, s * G * TS1 + off,
                                    [[NS1 * G * TS1, P], [TS1, G], [1, TS2]]),
                        in_=x2p[:, :].rearrange("p (g t) -> p g t", g=G)),
                        reads=[x2p], writes=["x2p_sp"], dma=True)
                trk.full_barrier()

        def b3():
            with ExitStack() as ph:
                sb = lambda *a: ph.enter_context(nc.sbuf_tensor(*a))
                alloc_ffn_weights(sb, "c")
                x2b = [sb("c_x2A", [P, G * TS1], F32),
                       sb("c_x2B", [P, G * TS1], F32)]
                x1b = sb("c_x1", [P, G * TS1], F32)
                xb = sb("c_xb", [P, G * TS1], BF16)
                sq = sb("c_sq", [P, G * TS1], BF16)
                rows = sb("c_rows", [1, 3 * TS1], F32)
                t1 = sb("c_t1", [P, G * TS1], F32)
                ybb = [sb("c_ybA", [P, G * TS1], BF16),
                       sb("c_ybB", [P, G * TS1], BF16)]
                x3 = sb("c_x3", [P, G * TS1], F32)
                lg = sb("c_lg", [P, 192], F32)
                wgb = sb("c_wgb", [P, 4 * G], BF16)
                axf = sb("c_axf", [P, TS1], F32)
                wgT = sb("c_wgT", [G, TS1], BF16)
                we = sb("c_we", [P, TS1], BF16)
                axw = sb("c_axw", [P, TS1], BF16)
                tep = sb("c_tep", [P, 2 * TS1], F32)
                bbm = sb("c_bbm", [P, TS1], F32)
                bbr = sb("c_bbr", [P, TS1], F32)

                def c1(s):
                    x2, yb = x2b[s % 2], ybb[s % 2]
                    op("sync", lambda e, s=s, x2=x2: e.dma_start(
                        out=x2[:, :],
                        in_=bass.AP(x2p_sp, s * G * TS1,
                                    [[NS1 * G * TS1, P], [1, G * TS1]])),
                        reads=["x2p_sp"], writes=[x2], dma=True)
                    op("sync", lambda e, s=s: e.dma_start(
                        out=x1b[:, :],
                        in_=bass.AP(x1_sp, s * G * TS1,
                                    [[NS1 * G * TS1, P], [1, G * TS1]])),
                        reads=["x1_sp"], writes=[x1b], dma=True)
                    op("vector", lambda e, x2=x2: e.tensor_tensor(
                        x2[:, :], x2[:, :], x1b[:, :], ALU.add),
                        reads=[x2, x1b], writes=[x2])
                    yield
                    rowm, rowr, _ = yield from ln_stats2(x2, TS1, xb, sq, rows)
                    bcast(rowm, TS1, pb, rows.name + "_m", dst=bbm)
                    bcast(rowr, TS1, pa1, rows.name + "_r", dst=bbr)
                    yield
                    for g in range(G):
                        en = "gpsimd" if g < 3 else "vector"
                        op(en, lambda e, g=g, x2=x2: e.tensor_tensor(
                            t1[:, g * TS1:(g + 1) * TS1],
                            x2[:, g * TS1:(g + 1) * TS1], bbm[:, 0:TS1],
                            ALU.subtract), reads=[x2, bbm], writes=[t1])
                        op(en, lambda e, g=g, yb=yb: e.tensor_tensor(
                            yb[:, g * TS1:(g + 1) * TS1],
                            t1[:, g * TS1:(g + 1) * TS1], bbr[:, 0:TS1], ALU.mult),
                            reads=[t1, bbr], writes=[yb])
                        yield

                def c2(s):
                    x2, yb = x2b[s % 2], ybb[s % 2]
                    fb = {"lg": lg, "wgb": wgb, "wgT": wgT, "we": we, "axw": axw,
                          "axf": axf,
                          "tep": tep, "ptq": pa1, "resid": x2}
                    yield from emit_ffn(
                        yb, TS1, lambda g, x2=x2: x2[:, g * TS1:(g + 1) * TS1],
                        lambda g: (x3[:, g * TS1:(g + 1) * TS1], x3), fb)
                    op("sync", lambda e, s=s: e.dma_start(
                        out=bass.AP(y_d.tensor, s * TS1,
                                    [[SP, P], [P * SP, G], [1, TS1]]),
                        in_=x3[:, :].rearrange("p (g t) -> p g t", g=G)),
                        reads=[x3], writes=["y"], dma=True)
                    yield

                ilv(c1(0), load_ffn_weights(sb, "c"))
                ilv(c2(0), c1(1))
                ilv(c2(1), c1(2))
                ilv(c2(2), c1(3))
                drain(c2(3))
                trk.full_barrier()

        block = outer.enter_context(nc.Block())

        @block.sync
        def _(sync):
            emit_all()

    return nc


def _host_inputs(inputs):
    f32 = np.float32
    x = np.ascontiguousarray(np.swapaxes(np.asarray(inputs["L0_attn_x"], f32), 1, 2))
    pvec = np.zeros((D, NPV), f32)
    pvec[:, 0] = inputs["L0_attn_ln1_w"]
    pvec[:, 1] = inputs["L0_attn_ln1_b"]
    pvec[:, 2] = inputs["L0_attn_ln2_w"]
    pvec[:, 3] = inputs["L0_attn_ln2_b"]
    pvec[:, 4] = inputs["L0_attn_time_decay"]
    pvec[:, 5] = inputs["L0_attn_time_first"]
    pvec[:, 6] = np.asarray(inputs["L0_attn_time_mix_k"], f32).ravel()
    pvec[:, 7] = np.asarray(inputs["L0_attn_time_mix_v"], f32).ravel()
    pvec[:, 8] = np.asarray(inputs["L0_attn_time_mix_r"], f32).ravel()
    pvec[:, 9] = np.asarray(inputs["L0_attn_f_time_mix_k"], f32).ravel()
    pvec[:, 10] = np.asarray(inputs["L0_attn_f_time_mix_r"], f32).ravel()
    pvec[:, 11] = inputs["L0_ffn_base_b"]
    import ml_dtypes
    cns = np.zeros((P, 8), f32)
    for j, v in enumerate([0.5, -0.5, -1.0, 1e-5, 2.0, 1.0, 0.0, 0.0]):
        cns[:, j] = v
    ident = np.eye(P, dtype=ml_dtypes.bfloat16)
    expand = np.zeros((G, P), ml_dtypes.bfloat16)
    for e in range(G):
        expand[e, e * 16:(e + 1) * 16] = 1.0
    onesrow = np.ones((1, P), f32)
    shared = dict(
        wk=np.ascontiguousarray(np.asarray(inputs["L0_attn_w_key"], f32)),
        wv=np.ascontiguousarray(np.asarray(inputs["L0_attn_w_value"], f32)),
        wr=np.ascontiguousarray(np.asarray(inputs["L0_attn_w_recept"], f32)),
        wo=np.ascontiguousarray(np.asarray(inputs["L0_attn_w_output"], f32)),
        fwk=np.ascontiguousarray(np.asarray(inputs["L0_attn_f_w_key"], f32)),
        fwr=np.ascontiguousarray(np.asarray(inputs["L0_attn_f_w_recept"], f32)),
        fwv=np.ascontiguousarray(np.asarray(inputs["L0_attn_f_w_value"], f32)),
        bw=np.ascontiguousarray(np.asarray(inputs["L0_ffn_base_w"], f32).T),
        la=np.ascontiguousarray(
            np.asarray(inputs["L0_ffn_lora_A"], f32).transpose(1, 0, 2)
            .reshape(D, 8 * 16)),
        lb=np.ascontiguousarray(
            np.asarray(inputs["L0_ffn_lora_B"], f32).reshape(8 * 16, D)),
        rw=np.ascontiguousarray(np.asarray(inputs["L0_ffn_router_w"], f32).T),
        rb=np.ascontiguousarray(np.asarray(inputs["L0_ffn_router_b"], f32)
                                .reshape(1, G)),
        pvec=pvec, cns=cns, ident=ident, expand=expand, onesrow=onesrow,
    )
    in_maps = []
    for cid in range(NCORES):
        b, q = cid // 4, cid % 4
        m = dict(shared)
        m["x"] = np.ascontiguousarray(x[b, :, q * SP:(q + 1) * SP])
        msel0 = np.zeros((P, G), f32)
        if q > 0:
            msel0[:, cid - 1] = 1.0
        m["msel0"] = msel0
        selA = np.zeros((P, 3 * G), f32)
        selB = np.zeros((P, 3 * G), f32)
        mbits = np.zeros((P, 8), f32)
        for j in range(3):
            if j < q:
                selA[:, j * G + (b * 4 + j)] = 1.0
        mbits[:, 0] = 1.0 if 1 < q or 1 == q else 0.0
        mbits[:, 0] = 1.0 if 1 < q else 0.0
        mbits[:, 1] = 1.0 - mbits[:, 0]
        mbits[:, 2] = 1.0 if 2 < q else 0.0
        mbits[:, 3] = 1.0 - mbits[:, 2]
        for j in range(3):
            if j < q - 1:
                selB[:, j * G + (b * 4 + j)] = 1.0
        mbits[:, 4] = 1.0 if 1 < q - 1 else 0.0
        mbits[:, 5] = 1.0 - mbits[:, 4]
        mbits[:, 6] = 1.0 if 2 < q - 1 else 0.0
        mbits[:, 7] = 1.0 - mbits[:, 6]
        m["selA"] = selA
        m["selB"] = selB
        m["mbits"] = mbits
        mh = np.zeros((P, 2), f32)
        mh[:, 0] = 0.0 if q == 0 else 1.0
        mh[:, 1] = 1.0 - mh[:, 0]
        m["mhalo"] = mh
        in_maps.append(m)
    return in_maps


_NC_CACHE = {}


def kernel(**inputs):
    in_maps = _host_inputs(inputs)
    if "nc" not in _NC_CACHE:
        _NC_CACHE["nc"] = build_program()
    nc = _NC_CACHE["nc"]
    res = run_bass_kernel_spmd(nc, in_maps, core_ids=list(range(NCORES)))
    out = np.empty((2, D, 8192), np.float32)
    for cid in range(NCORES):
        b, q = cid // 4, cid % 4
        out[b, :, q * SP:(q + 1) * SP] = res.results[cid]["y"]
    return np.ascontiguousarray(np.swapaxes(out, 1, 2))


# version 26
# speedup vs baseline: 1.0337x; 1.0337x over previous
import numpy as np
from contextlib import ExitStack

import concourse.bass as bass
import concourse.mybir as mybir
from concourse.bass_utils import run_bass_kernel_spmd

F32 = mybir.dt.float32
BF16 = mybir.dt.bfloat16
AF = mybir.ActivationFunctionType
ALU = mybir.AluOpType
AX = mybir.AxisListType

P = 128
G = 8
D = 1024
SP = 2048
TS1 = 512
NS1 = SP // TS1
TS2 = 512
NS2 = SP // TS2
NCORES = 8
FD = 4096
NPV = 16

ENGINES = ("sync", "vector", "scalar", "tensor", "gpsimd")


def _key(x):
    if isinstance(x, str):
        return x
    t = getattr(x, "tensor", x)
    return t.name


class Trk:
    NDMA = 6

    def __init__(self, nc, stack):
        self.nc = nc
        self.stack = stack
        self.sems = {}
        self.count = {}
        for e in ENGINES:
            self._mksem(e)
        self.dma_uses = {}
        self.dma_next = {}
        self.waited = {}
        for e in ENGINES:
            self.waited[e] = {}
        self.writer = {}
        self.readers = {}

    def _mksem(self, name):
        self.sems[name] = self.stack.enter_context(self.nc.semaphore(f"S_{name}"))
        self.count[name] = 0
        return self.sems[name]

    def _wait(self, engine, e, sem_name, val):
        if self.waited[engine].get(sem_name, 0) >= val:
            return
        e.wait_ge(self.sems[sem_name], val)
        self.waited[engine][sem_name] = val

    def op(self, engine, fn, reads=(), writes=(), dma=False, inc=None,
           async_sem=None):
        nc = self.nc
        e = getattr(nc, engine)
        needs = {}

        def need(sem_name, c):
            if needs.get(sem_name, 0) < c:
                needs[sem_name] = c
        for r in reads:
            w = self.writer.get(_key(r))
            if w:
                need(*w)
        for wr in writes:
            k = _key(wr)
            w = self.writer.get(k)
            if w:
                need(*w)
            for sem_name, c in self.readers.get(k, {}).items():
                need(sem_name, c)
        for sem_name, c in sorted(needs.items()):
            self._wait(engine, e, sem_name, c)
        if async_sem is not None:
            if async_sem not in self.sems:
                self._mksem(async_sem)
            sem_name = async_sem
            step = inc if inc is not None else 1
            inst = fn(e)
            inst.then_inc(self.sems[sem_name], step)
            self.count[sem_name] += step
        elif dma:
            j = self.dma_next.get(engine, 0)
            self.dma_next[engine] = (j + 1) % self.NDMA
            sem_name = f"{engine}_d{j}"
            if sem_name not in self.sems:
                self._mksem(sem_name)
            uses = self.dma_uses.get(sem_name, 0)
            if uses > 0:
                self._wait(engine, e, sem_name, 16 * uses)
            inst = fn(e)
            inst.then_inc(self.sems[sem_name], 16)
            self.dma_uses[sem_name] = uses + 1
            self.count[sem_name] = 16 * (uses + 1)
        else:
            sem_name = engine
            inst = fn(e)
            step = inc if inc is not None else 1
            inst.then_inc(self.sems[sem_name], step)
            self.count[sem_name] += step
        cnt = self.count[sem_name]
        for wr in writes:
            k = _key(wr)
            self.writer[k] = (sem_name, cnt)
            self.readers[k] = {}
        for r in reads:
            k = _key(r)
            self.readers.setdefault(k, {})
            if self.readers[k].get(sem_name, 0) < cnt:
                self.readers[k][sem_name] = cnt
        return inst

    def full_barrier(self):
        for cons in ENGINES:
            e = getattr(self.nc, cons)
            for sem_name, c in list(self.count.items()):
                if sem_name == cons or not c:
                    continue
                self._wait(cons, e, sem_name, c)


def ilv(*gens):
    its = [iter(g) for g in gens]
    while its:
        for it in list(its):
            try:
                next(it)
            except StopIteration:
                its.remove(it)


def drain(gen):
    for _ in gen:
        pass


def fm_in(dram_ap, rows, row_stride, col0, ncols):
    t = dram_ap.tensor
    return bass.AP(t, col0, [[row_stride, P], [P * row_stride, G], [1, ncols]])


def build_program(debug=0):
    nc = bass.Bass("TRN2", target_bir_lowering=False)
    dt_in = {}

    def din(name, shape, dtype=F32):
        dt_in[name] = nc.dram_tensor(name, shape, dtype, kind="ExternalInput").ap()
        return dt_in[name]

    x_d = din("x", [D, SP])
    wk_d = din("wk", [D, D]); wv_d = din("wv", [D, D])
    wr_d = din("wr", [D, D]); wo_d = din("wo", [D, D])
    fwk_d = din("fwk", [D, FD]); fwr_d = din("fwr", [D, D])
    fwv_d = din("fwv", [FD, D])
    bw_d = din("bw", [D, D]); la_d = din("la", [D, P]); lb_d = din("lb", [P, D])
    rw_d = din("rw", [D, G]); rb_d = din("rb", [1, G])
    pv_d = din("pvec", [D, NPV])
    cns_d = din("cns", [P, 8])
    ident_d = din("ident", [P, P], BF16)
    expand_d = din("expand", [G, P], BF16)
    ones_d = din("onesrow", [1, P])
    msel0_d = din("msel0", [P, G])
    selA_d = din("selA", [P, 3 * G])
    selB_d = din("selB", [P, 3 * G])
    mbit_d = din("mbits", [P, 8])
    mhalo_d = din("mhalo", [P, 2])

    y_d = nc.dram_tensor("y", [D, SP], F32, kind="ExternalOutput").ap()

    def scr(name, cols, dtype=F32):
        return nc.dram_tensor(name, [P, cols], dtype, kind="Internal")

    x1_sp = scr("x1_sp", NS1 * G * TS1)
    h0_sp = scr("h0_sp", NS1 * G * TS1)
    ek_sp = scr("ek_sp", NS1 * G * TS1)
    ekv_sp = scr("ekv_sp", NS1 * G * TS1)
    sr_sp = scr("sr_sp", NS1 * G * TS1, BF16)
    h1_sp = scr("h1_sp", NS1 * G * TS1)
    mfk_sp = scr("mfk_sp", NS1 * G * TS1, BF16)
    sgf_sp = scr("sgf_sp", NS1 * G * TS1)
    x2p_sp = scr("x2p_sp", NS1 * G * TS1)
    bc_sc = nc.dram_tensor("bc_sc", [1, 4096], F32, kind="Internal")
    ag0_in = nc.dram_tensor("ag0_in", [P, G], F32, kind="Internal")
    ag0_out = nc.dram_tensor("ag0_out", [NCORES * P, G], F32, kind="Internal")
    ag1_in = nc.dram_tensor("ag1_in", [P, 64], F32, kind="Internal")
    ag1_out = nc.dram_tensor("ag1_out", [NCORES * P, 64], F32, kind="Internal")

    dbg = {}
    for i in range(debug):
        dbg[i] = nc.dram_tensor(f"dbg{i}", [P, G * TS1], F32, kind="ExternalOutput").ap()

    with ExitStack() as outer:
        sbuf = lambda *a: outer.enter_context(nc.sbuf_tensor(*a))
        psum = lambda *a: outer.enter_context(nc.psum_tensor(*a))
        trk = Trk(nc, outer)
        op = trk.op

        FW = {}
        rbt = sbuf("rbt", [P, G], F32)
        exp_b = sbuf("exp_b", [G, P], BF16)
        id_b = sbuf("id_b", [P, P], BF16)
        ones_r = sbuf("ones_r", [1, P], F32)
        ones_c = sbuf("ones_c", [P, 2], BF16)
        cns = sbuf("cns_s", [P, 8], F32)
        pv = sbuf("pv_s", [P, G * NPV], F32)
        dv = sbuf("dv", [P, G * 8], F32)
        SS = {}
        msel0 = sbuf("msel0_s", [P, G], F32)
        selA = sbuf("selA_s", [P, 3 * G], F32)
        selB = sbuf("selB_s", [P, 3 * G], F32)
        mbit = sbuf("mbit_s", [P, 8], F32)
        mhalo = sbuf("mhalo_s", [P, 2], F32)
        xn_last = sbuf("xn_last", [P, G], F32)
        xn2_last = sbuf("xn2_last", [P, G], F32)
        h0_end = sbuf("h0_end", [P, G], F32)
        stA = sbuf("stA", [P, G], F32)
        stB = sbuf("stB", [P, G], F32)
        penA = sbuf("penA", [P, G], F32)
        penB = sbuf("penB", [P, G], F32)
        eke = sbuf("eke", [P, G], F32)
        ekve = sbuf("ekve", [P, G], F32)
        sre = sbuf("sre", [P, G], F32)
        carA = sbuf("carA", [P, G], F32)
        carB = sbuf("carB", [P, G], F32)
        st2A = sbuf("st2A", [P, G], F32)
        st2B = sbuf("st2B", [P, G], F32)
        srh_p = sbuf("srh_p", [P, 8], BF16)
        h0sel_p = sbuf("h0sel_p", [P, 8], F32)

        pm0 = psum("pm0", [P, TS1], F32)
        pm1 = psum("pm1", [P, TS1], F32)
        pa0 = psum("pa0", [P, TS1], F32)
        pa1 = psum("pa1", [P, TS1], F32)
        prs = psum("prs", [64, TS1], F32)
        pb = psum("pb", [P, TS1], F32)
        pq = psum("pq", [P, TS1], F32)
        ptb = psum("ptb", [P, P], BF16)

        def pvc(g, j):
            return pv[:, g * NPV + j:g * NPV + j + 1]

        def dvc(g, j):
            return dv[:, g * 8 + j:g * 8 + j + 1]

        def dvrow(j):
            t = dv.tensor if hasattr(dv, "tensor") else dv
            return bass.AP(t, j, [[G * 8, P], [8, G]])

        def pvrow(j):
            t = pv.tensor if hasattr(pv, "tensor") else pv
            return bass.AP(t, j, [[G * NPV, P], [NPV, G]])

        CN = {"half": cns[:, 0:1], "neghalf": cns[:, 1:2], "negone": cns[:, 2:3],
              "eps": cns[:, 3:4], "two": cns[:, 4:5], "one": cns[:, 5:6]}
        nc.const_aps.register(F32, 0.5, cns[:, 0:1])
        nc.const_aps.register(F32, -0.5, cns[:, 1:2])
        nc.const_aps.register(F32, -1.0, cns[:, 2:3])
        nc.const_aps.register(F32, 1e-5, cns[:, 3:4])
        nc.const_aps.register(F32, 2.0, cns[:, 4:5])
        nc.const_aps.register(F32, 1.0, cns[:, 5:6])

        AH = {}

        def ag0_p1(sb):
            xe = sb("a0_xe", [P, G], F32)
            r2b = sb("a0_r2b", [P, 2], F32)
            op("tensor", lambda e: e.matmul(
                pq[:, 8:9], ones_r[:, :], SS["t"][0:1, SP - 1:SP],
                start=True, stop=True), reads=["srstd2", ones_r], writes=[pq])
            op("vector", lambda e: e.tensor_copy(r2b[:, 0:1], pq[:, 8:9]),
                reads=[pq], writes=[r2b])
            op("vector", lambda e: e.tensor_scalar(
                xe[:, :], h0_end[:, :], r2b[:, 0:1], None, ALU.mult),
                reads=[h0_end, r2b], writes=[xe])
            op("vector", lambda e: e.tensor_tensor(
                xe[:, :], xe[:, :], pvrow(0), ALU.mult),
                reads=[xe, pv], writes=[xe])
            op("vector", lambda e: e.tensor_tensor(
                xe[:, :], xe[:, :], pvrow(1), ALU.add),
                reads=[xe, pv], writes=[xe])
            op("gpsimd", lambda e: e.dma_start(out=ag0_in.ap(), in_=xe[:, :]),
                reads=[xe], writes=["ag0_in"], dma=True)
            op("gpsimd", lambda e: e.collective_compute(
                "AllGather", ALU.bypass, replica_groups=[list(range(NCORES))],
                ins=[ag0_in.ap().opt()], outs=[ag0_out.ap().opt()]),
                reads=["ag0_in"], writes=["ag0_out"], async_sem="cc0", inc=1)

        def ag0_p2(sb):
            sag = sb("a0_sag", [P, NCORES * G], F32)
            tse = sb("a0_tse", [P, NCORES * G], F32)
            op("sync", lambda e: e.dma_start(
                out=sag[:, :].rearrange("p (r c) -> p r c", r=NCORES),
                in_=bass.AP(ag0_out, 0, [[G, P], [P * G, NCORES], [1, G]])),
                reads=["ag0_out"], writes=[sag], dma=True)
            op("vector", lambda e: e.tensor_tensor(
                tse[:, :].rearrange("p (r c) -> p r c", r=NCORES),
                sag[:, :].rearrange("p (r c) -> p r c", r=NCORES),
                msel0[:, :].rearrange("p (r o) -> p r o", o=1)
                    .broadcast_to([P, NCORES, G]), ALU.mult),
                reads=[sag, msel0], writes=[tse])
            tset = tse.tensor if hasattr(tse, "tensor") else tse
            op("vector", lambda e: e.reduce_sum(
                xn_last[:, :],
                bass.AP(tset, 0, [[NCORES * G, P], [1, G], [G, NCORES]]),
                axis=AX.X), reads=[tse], writes=[xn_last])


        def emit_all():
            with nc.named_scope("p0"):
                p0_load()
            with nc.sbuf_tensor("srstd2", [1, SP], F32) as _ss:
                SS["t"] = _ss
                for name, fn in (("p1f", p1f), ("p1a", p1a)):
                    with nc.named_scope(name):
                        fn()
                trk.full_barrier()
            for name, fn in (("b1", b1), ("b2", b2), ("b3", b3)):
                with nc.named_scope(name):
                    fn()
            trk.full_barrier()

        def load_fm(dst, dram, rows, cols, dtype_tmp=None, scale=None, tmp=None,
                    tmpkey=None, tmp_cols=None):
            ng = rows // P
            srcap = bass.AP(dram.tensor, 0, [[cols, P], [P * cols, ng], [1, cols]])
            op("gpsimd", lambda e: e.dma_start(
                out=dst[:, 0:ng * cols].rearrange("p (g t) -> p g t", g=ng),
                in_=srcap),
                reads=[dram], writes=[dst], dma=True)
            if scale is not None:
                op("scalar", lambda e: e.activation(
                    dst[:, 0:ng * cols], dst[:, 0:ng * cols], AF.Copy, scale=scale),
                    reads=[dst], writes=[dst])

        def p0_load():
            for dst, src in ((cns, cns_d), (exp_b, expand_d), (id_b, ident_d),
                             (ones_r, ones_d), (msel0, msel0_d), (selA, selA_d),
                             (selB, selB_d), (mbit, mbit_d), (mhalo, mhalo_d)):
                op("sync", lambda e, d=dst, s=src: e.dma_start(out=d[:, :], in_=s),
                    reads=[src], writes=[dst], dma=True)
            op("sync", lambda e: e.dma_start(
                out=pv[:, :].rearrange("p (g t) -> p g t", g=G),
                in_=bass.AP(pv_d.tensor, 0, [[NPV, P], [P * NPV, G], [1, NPV]])),
                reads=[pv_d], writes=[pv], dma=True)
            op("vector", lambda e: e.memset(ones_c[:, :], 1.0), writes=[ones_c])
            op("vector", lambda e: e.memset(xn_last[:, :], 0.0), writes=[xn_last])
            op("vector", lambda e: e.memset(stA[:, :], 0.0), writes=[stA])
            op("vector", lambda e: e.memset(stB[:, :], 0.0), writes=[stB])
            wrow = dvrow(5)
            op("scalar", lambda e: e.activation(wrow, pvrow(4), AF.Exp),
                reads=[pv], writes=[dv])
            op("scalar", lambda e: e.activation(dvrow(0), dvrow(5), AF.Exp, scale=-1.0),
                reads=[dv], writes=[dv])
            op("scalar", lambda e: e.activation(dvrow(1), pvrow(5), AF.Exp, scale=-1.0),
                reads=[pv], writes=[dv])
            op("vector", lambda e: e.tensor_scalar(
                dvrow(6), dvrow(5), -2048.0, None, ALU.mult), reads=[dv], writes=[dv])
            op("scalar", lambda e: e.activation(dvrow(3), dvrow(6), AF.Exp),
                reads=[dv], writes=[dv])
            op("vector", lambda e: e.tensor_scalar(
                dvrow(6), dvrow(5), -2047.0, None, ALU.mult), reads=[dv], writes=[dv])
            op("scalar", lambda e: e.activation(dvrow(4), dvrow(6), AF.Exp),
                reads=[dv], writes=[dv])
            op("vector", lambda e: e.tensor_scalar(
                dvrow(2), pvrow(11), 0.5, None, ALU.mult), reads=[pv], writes=[dv])
            with nc.sbuf_tensor("wtmp0", [1, G], F32) as wtmp:
                op("sync", lambda e: e.dma_start(out=wtmp[0:1, 0:G], in_=rb_d),
                    reads=[rb_d], writes=[wtmp], dma=True)
                op("tensor", lambda e: e.matmul(pq[:, 0:G], ones_r[:, :],
                                                wtmp[0:1, 0:G], start=True, stop=True),
                    reads=[ones_r, wtmp], writes=[pq])
                op("vector", lambda e: e.tensor_copy(rbt[:, :], pq[:, 0:G]),
                    reads=[pq], writes=[rbt])
                trk.full_barrier()

        def ln_stats2(xbuf, nt, xb_tmp, sq_tmp, rows, keep_var=False):
            rowm = rows[0:1, 0:nt]
            rowr = rows[0:1, nt:2 * nt]
            rowt = rows[0:1, 2 * nt:3 * nt]
            KM, KR, KT = (rows.name + "_m", rows.name + "_r", rows.name + "_t")
            op("scalar", lambda e: e.copy(xb_tmp[:, 0:G * nt], xbuf[:, 0:G * nt]),
                reads=[xbuf], writes=[xb_tmp])
            op("scalar", lambda e: e.activation(
                sq_tmp[:, 0:G * nt], xbuf[:, 0:G * nt], AF.Square),
                reads=[xbuf], writes=[sq_tmp])

            def mm_sum(e, src, bank):
                last = None
                for kg in range(G):
                    last = e.matmul(bank, ones_c[:, 0:1],
                                    src[:, kg * nt:(kg + 1) * nt],
                                    start=(kg == 0), stop=(kg == G - 1))
                return last
            yield
            op("tensor", lambda e: mm_sum(e, xb_tmp, prs[0:1, 0:TS1]),
                reads=[xb_tmp, ones_c], writes=[prs])
            op("tensor", lambda e: mm_sum(e, sq_tmp, prs[32:33, 0:TS1]),
                reads=[sq_tmp, ones_c], writes=[prs])
            op("vector", lambda e: e.tensor_scalar(
                rowm, prs[0:1, 0:nt], 1.0 / D, None, ALU.mult),
                reads=[prs], writes=[KM])
            op("vector", lambda e: e.tensor_tensor(rowt, rowm, rowm, ALU.mult),
                reads=[KM], writes=[KT])
            op("vector", lambda e: e.scalar_tensor_tensor(
                rowr, prs[32:33, 0:nt], 1.0 / D, rowt, ALU.mult, ALU.subtract),
                reads=[prs, KT], writes=[KR])
            if keep_var:
                op("vector", lambda e: e.tensor_copy(rowt, rowr),
                    reads=[KR], writes=[KT])
            op("scalar", lambda e: e.activation(rowr, rowr, AF.Ln, bias=1e-5),
                reads=[KR], writes=[KR])
            op("scalar", lambda e: e.activation(rowr, rowr, AF.Exp, scale=-0.5),
                reads=[KR], writes=[KR])
            return rowm, rowr, rowt

        def bcast(row_ap, nt, bank, rows_key, dst=None):
            op("tensor", lambda e: e.matmul(bank[:, 0:nt], ones_r[:, :], row_ap,
                                            start=True, stop=True),
                reads=[rows_key, ones_r], writes=[bank])
            op("scalar", lambda e: e.copy(dst[:, 0:nt], bank[:, 0:nt]),
                reads=[bank], writes=[dst])

        def mm_group(bank, nt, lhs_fn, rhs_fn, nk, reads, extra=None, col0=0,
                     open_accum=False, close_accum=False):
            def go(e):
                last = None
                for k in range(nk):
                    last = e.matmul(bank[:, col0:col0 + nt], lhs_fn(k), rhs_fn(k),
                                    start=(k == 0 and not close_accum),
                                    stop=(k == nk - 1 and extra is None
                                          and not open_accum))
                if extra is not None:
                    last = e.matmul(bank[:, col0:col0 + nt], extra[0], extra[1],
                                    start=(nk == 0 and not close_accum),
                                    stop=True)
                return last
            op("tensor", go, reads=list(reads), writes=[bank])

        def alloc_ffn_weights(sb, tag):
            FW["bw"] = sb(f"ffw_bw{tag}", [P, G * D], BF16)
            FW["la"] = sb(f"ffw_la{tag}", [P, G * P], BF16)
            FW["lb"] = sb(f"ffw_lb{tag}", [P, D], BF16)
            FW["rw"] = sb(f"ffw_rw{tag}", [P, G * G], BF16)

        def load_ffn_weights(sb, tag):
            load_fm(FW["bw"], bw_d, D, D)
            yield
            load_fm(FW["la"], la_d, D, P)
            load_fm(FW["lb"], lb_d, P, D, scale=2.0)
            load_fm(FW["rw"], rw_d, D, G)
            yield

        def emit_ffn(y_bf, nt, resid_g, out_g, fb):
            ntile = (nt + P - 1) // P
            lg = fb["lg"]
            for mt in range(ntile):
                mw = min(P, nt - mt * P)
                B = mt * 48
                mm_group(pq, G, lambda k, mt=mt, mw=mw: y_bf[:, k * nt + mt * P:
                                                             k * nt + mt * P + mw],
                         lambda k: FW["rw"][:, k * G:(k + 1) * G], G,
                         [y_bf, FW["rw"]], col0=0)
                op("vector", lambda e, B=B, mw=mw: e.tensor_tensor(
                    lg[0:mw, B + 40:B + 48], pq[0:mw, 0:G],
                    rbt[0:mw, :], ALU.add),
                    reads=[pq, rbt], writes=[f"lg{mt}"])
                yield
            mm_group(pa0, nt, lambda k: FW["la"][:, k * P:(k + 1) * P],
                     lambda k: y_bf[:, k * nt:(k + 1) * nt], G, [FW["la"], y_bf])
            op("scalar", lambda e: e.copy(fb["axf"][:, 0:nt], pa0[:, 0:nt]),
                reads=[pa0], writes=["axf"])
            yield
            OB = (pm0, pm1, pq, pa0)
            for m in range(4):
                mm_group(OB[m], nt,
                         lambda k, m=m: FW["bw"][:, k * D + m * P:k * D + m * P + P],
                         lambda k: y_bf[:, k * nt:(k + 1) * nt], G,
                         [FW["bw"], y_bf] + (["axf"] if m == 3 else []),
                         open_accum=True)
                if m == 1:
                    yield
            yield
            wgb = fb["wgb"]

            def gate_op(opi, mt):
                mw = min(P, nt - mt * P)
                B = mt * 48
                lk = f"lg{mt}"
                if opi == 0:
                    op("vector", lambda e: e.max(
                        lg[0:mw, B + 8:B + 16], lg[0:mw, B + 40:B + 48]),
                        reads=[lk], writes=[lk])
                elif opi == 1:
                    op("vector", lambda e: e.tensor_scalar(
                        lg[0:mw, B + 16:B + 17], lg[0:mw, B + 8:B + 9], -1.0,
                        None, ALU.mult), reads=[lk], writes=[lk])
                elif opi == 2:
                    op("vector", lambda e: e.tensor_scalar(
                        lg[0:mw, B + 24:B + 32], lg[0:mw, B + 40:B + 48],
                        lg[0:mw, B + 9:B + 10], None, ALU.is_ge),
                        reads=[lk], writes=[lk])
                elif opi == 3:
                    op("scalar", lambda e: e.activation(
                        lg[0:mw, B + 32:B + 40], lg[0:mw, B + 40:B + 48], AF.Exp,
                        bias=lg[0:mw, B + 16:B + 17]),
                        reads=[lk], writes=[lk])
                elif opi == 4:
                    op("vector", lambda e: e.tensor_tensor(
                        lg[0:mw, B + 32:B + 40], lg[0:mw, B + 32:B + 40],
                        lg[0:mw, B + 24:B + 32], ALU.mult),
                        reads=[lk], writes=[lk])
                elif opi == 5:
                    op("vector", lambda e: e.reduce_sum(
                        lg[0:mw, B + 17:B + 18], lg[0:mw, B + 32:B + 40],
                        axis=AX.X), reads=[lk], writes=[lk])
                elif opi == 6:
                    op("vector", lambda e: e.reciprocal(
                        lg[0:mw, B + 18:B + 19], lg[0:mw, B + 17:B + 18]),
                        reads=[lk], writes=[lk])
                elif opi == 7:
                    op("vector", lambda e: e.tensor_scalar(
                        wgb[0:mw, mt * G:(mt + 1) * G], lg[0:mw, B + 32:B + 40],
                        lg[0:mw, B + 18:B + 19], None, ALU.mult),
                        reads=[lk], writes=[f"wgb{mt}"])
                elif opi == 8:
                    op("tensor", lambda e: e.transpose(
                        ptb[0:G, 0:P], wgb[:, mt * G:(mt + 1) * G], id_b[:, :]),
                        reads=[f"wgb{mt}", id_b], writes=[ptb])
                    op("scalar", lambda e: e.copy(
                        fb["wgT"][0:G, mt * P:mt * P + mw], ptb[0:G, 0:mw]),
                        reads=[ptb], writes=[f"wgT{mt}"])

            for opi in range(9):
                for mt in range(ntile):
                    gate_op(opi, mt)
                if opi in (3, 6):
                    yield
            yield
            wgTk = [f"wgT{mt}" for mt in range(ntile)]
            mm_group(pa1, nt, lambda k: exp_b[:, :],
                     lambda k: fb["wgT"][0:G, 0:nt], 1, [exp_b] + wgTk)
            op("scalar", lambda e: e.copy(fb["we"][:, 0:nt], pa1[:, 0:nt]),
                reads=[pa1], writes=[fb["we"]])
            op("vector", lambda e: e.tensor_tensor(
                fb["axw"][:, 0:nt], fb["axf"][:, 0:nt], fb["we"][:, 0:nt],
                ALU.mult), reads=["axf", fb["we"]], writes=[fb["axw"]])
            yield
            for m in range(G):
                bank = OB[m % 4]
                if m < 4:
                    mm_group(bank, nt, None, None, 0,
                             [FW["lb"], fb["axw"]],
                             extra=(FW["lb"][:, m * P:m * P + P],
                                    fb["axw"][:, 0:nt]), close_accum=True)
                else:
                    mm_group(bank, nt,
                             lambda k, m=m: FW["bw"][:, k * D + m * P:
                                                     k * D + m * P + P],
                             lambda k: y_bf[:, k * nt:(k + 1) * nt], G,
                             [FW["bw"], y_bf, fb["axw"], FW["lb"]],
                             extra=(FW["lb"][:, m * P:m * P + P],
                                    fb["axw"][:, 0:nt]))
                tb = fb["tep"][:, (m % 2) * nt:(m % 2) * nt + nt]
                tbk = f"tep{m % 2}"
                op("vector", lambda e, bank=bank, m=m, tb=tb: e.tensor_scalar(
                    tb, bank[:, 0:nt], 0.5, dvc(m, 2),
                    ALU.mult, ALU.add),
                    reads=[bank, dv], writes=[tbk])
                dst, dkey = out_g(m)
                op("gpsimd", lambda e, m=m, tb=tb: e.tensor_tensor(
                    dst, tb, resid_g(m), ALU.add),
                    reads=[tbk, fb["resid"]], writes=[dkey])
                yield

        def p1f():
            with ExitStack() as ph:
                sb = lambda *a: ph.enter_context(nc.sbuf_tensor(*a))
                alloc_ffn_weights(sb, "f")
                xsb = [sb("f_xsA", [P, G * TS1], F32),
                       sb("f_xsB", [P, G * TS1], F32)]
                xb = sb("f_xb", [P, G * TS1], BF16)
                sq = sb("f_sq", [P, G * TS1], BF16)
                xb2 = sb("f_xb2", [P, G * TS1], BF16)
                sq2 = sb("f_sq2", [P, G * TS1], BF16)
                rows = sb("f_rows", [1, 3 * TS1], F32)
                rows2 = sb("f_rows2", [1, 3 * TS1], F32)
                t1 = sb("f_t1", [P, G * TS1], F32)
                ybb = [sb("f_ybA", [P, G * TS1], BF16),
                       sb("f_ybB", [P, G * TS1], BF16)]
                h0 = sb("f_h0", [P, G * TS1], F32)
                x1b2 = [sb("f_x1A", [P, G * TS1], F32),
                        sb("f_x1B", [P, G * TS1], F32)]
                lg = sb("f_lg", [P, 192], F32)
                wgb = sb("f_wgb", [P, 4 * G], BF16)
                axf = sb("f_axf", [P, TS1], F32)
                wgT = sb("f_wgT", [G, TS1], BF16)
                we = sb("f_we", [P, TS1], BF16)
                axw = sb("f_axw", [P, TS1], BF16)
                tep = sb("f_tep", [P, 2 * TS1], F32)

                bbm = sb("f_bbm", [P, TS1], F32)
                bbr = sb("f_bbr", [P, TS1], F32)
                bbm2 = sb("f_bbm2", [P, TS1], F32)
                bbr2 = sb("f_bbr2", [P, TS1], F32)

                def f1(s):
                    xs, yb = xsb[s % 2], ybb[s % 2]
                    op("sync", lambda e, s=s: e.dma_start(
                        out=xs[:, :].rearrange("p (g t) -> p g t", g=G),
                        in_=fm_in(x_d, D, SP, s * TS1, TS1)),
                        reads=[x_d], writes=[xs], dma=True)
                    rowm, rowr, _ = yield from ln_stats2(xs, TS1, xb, sq, rows)
                    bcast(rowm, TS1, pb, rows.name + "_m", dst=bbm)
                    bcast(rowr, TS1, pa1, rows.name + "_r", dst=bbr)
                    yield
                    for g in range(G):
                        en = "gpsimd" if g < 3 else "vector"
                        op(en, lambda e, g=g, xs=xs: e.tensor_tensor(
                            t1[:, g * TS1:(g + 1) * TS1],
                            xs[:, g * TS1:(g + 1) * TS1], bbm[:, 0:TS1],
                            ALU.subtract), reads=[xs, bbm], writes=[t1])
                        op(en, lambda e, g=g, yb=yb: e.tensor_tensor(
                            yb[:, g * TS1:(g + 1) * TS1],
                            t1[:, g * TS1:(g + 1) * TS1], bbr[:, 0:TS1], ALU.mult),
                            reads=[t1, bbr], writes=[yb])
                        yield

                def f2(s):
                    xs, yb, x1 = xsb[s % 2], ybb[s % 2], x1b2[s % 2]
                    fb = {"lg": lg, "wgb": wgb, "wgT": wgT, "we": we, "axw": axw,
                          "axf": axf,
                          "tep": tep, "ptq": pa1, "resid": xs}
                    yield from emit_ffn(
                        yb, TS1, lambda g, xs=xs: xs[:, g * TS1:(g + 1) * TS1],
                        lambda g, x1=x1: (x1[:, g * TS1:(g + 1) * TS1], x1), fb)

                def f3(s):
                    x1 = x1b2[s % 2]
                    rowm, rowr, rowt = yield from ln_stats2(x1, TS1, xb2, sq2,
                                                            rows2, keep_var=True)
                    op("vector", lambda e: e.tensor_tensor(
                        rows2[0:1, 0:TS1], rowm, rowr, ALU.mult),
                        reads=[rows2.name + "_m", rows2.name + "_r"],
                        writes=[rows2.name + "_m"])
                    bcast(rows2[0:1, 0:TS1], TS1, pb, rows2.name + "_m",
                          dst=bbm2)
                    bcast(rowr, TS1, pa1, rows2.name + "_r", dst=bbr2)
                    yield
                    for g in range(G):
                        en = "gpsimd" if g < 3 else "vector"
                        op(en, lambda e, g=g, x1=x1: e.tensor_tensor(
                            h0[:, g * TS1:(g + 1) * TS1],
                            x1[:, g * TS1:(g + 1) * TS1], bbr2[:, 0:TS1],
                            ALU.mult), reads=[x1, bbr2], writes=[h0])
                        op(en, lambda e, g=g: e.tensor_tensor(
                            h0[:, g * TS1:(g + 1) * TS1],
                            h0[:, g * TS1:(g + 1) * TS1], bbm2[:, 0:TS1],
                            ALU.subtract),
                            reads=[h0, bbm2], writes=[h0])
                        yield
                    op("vector", lambda e: e.tensor_tensor(
                        rows2[0:1, 0:TS1], rowr, rowr, ALU.mult),
                        reads=[rows2.name + "_r"], writes=[rows2.name + "_m"])
                    op("vector", lambda e: e.tensor_tensor(
                        rows2[0:1, 0:TS1], rows2[0:1, 0:TS1], rowt, ALU.mult),
                        reads=[rows2.name + "_m", rows2.name + "_t"],
                        writes=[rows2.name + "_m"])
                    op("scalar", lambda e: e.activation(
                        rows2[0:1, 0:TS1], rows2[0:1, 0:TS1], AF.Ln, bias=1e-5),
                        reads=[rows2.name + "_m"], writes=[rows2.name + "_m"])
                    op("scalar", lambda e, s=s: e.activation(
                        SS["t"][0:1, s * TS1:(s + 1) * TS1], rows2[0:1, 0:TS1],
                        AF.Exp, scale=-0.5), reads=[rows2.name + "_m"],
                        writes=["srstd2"])
                    yield
                    op("scalar", lambda e, s=s, x1=x1: e.dma_start(
                        out=bass.AP(x1_sp, s * G * TS1, [[NS1 * G * TS1, P],
                                                         [1, G * TS1]]),
                        in_=x1[:, :]), reads=[x1], writes=["x1_sp"], dma=True)
                    op("scalar", lambda e, s=s: e.dma_start(
                        out=bass.AP(h0_sp, s * G * TS1, [[NS1 * G * TS1, P],
                                                         [1, G * TS1]]),
                        in_=h0[:, :]), reads=[h0], writes=["h0_sp"], dma=True)
                    yield
                    if s == NS1 - 1:
                        h0t = h0.tensor if hasattr(h0, "tensor") else h0
                        op("vector", lambda e, h0t=h0t: e.tensor_copy(
                            h0_end[:, :],
                            bass.AP(h0t, TS1 - 1, [[G * TS1, P], [TS1, G]])),
                            reads=[h0], writes=[h0_end])
                        yield

                ilv(f1(3), load_ffn_weights(sb, "f"))
                ilv(f2(3), f1(0))
                ilv(f3(3), f2(0), f1(1))
                ag0_p1(sb)
                ilv(f3(0), f2(1), f1(2))
                ilv(f3(1), f2(2))
                drain(f3(2))
                trk.full_barrier()

        def p1a():
            with ExitStack() as ph:
                sb = lambda *a: ph.enter_context(nc.sbuf_tensor(*a))
                wkb = sb("a_wk", [P, G * D], BF16)
                wvb = sb("a_wv", [P, G * D], BF16)
                wrb = sb("a_wr", [P, G * D], BF16)
                h0b = [sb("a_h0A", [P, G * TS1], F32),
                       sb("a_h0B", [P, G * TS1], F32)]
                xnb = [sb("a_xnA", [P, G * (TS1 + 1)], F32),
                       sb("a_xnB", [P, G * (TS1 + 1)], F32)]
                mkb = [sb("a_mkA", [P, G * TS1], BF16),
                       sb("a_mkB", [P, G * TS1], BF16)]
                mixb = sb("a_mix", [P, G * TS1], BF16)
                ekb = [sb("a_ekA", [P, G * TS1], F32),
                       sb("a_ekB", [P, G * TS1], F32)]
                ekvs = sb("a_ekv", [P, G * TS1], F32)
                srb = sb("a_sr", [P, G * TS1], BF16)
                scn = sb("a_scn", [P, TS1], F32)
                bb0 = sb("a_bb0", [P, TS1], F32)
                scn2 = pa1
                PITCH = TS1 + 1

                def e1(s):
                    p = s % 2
                    h0, xn, mixk = h0b[p], xnb[p], mkb[p]
                    xnt = xn.tensor if hasattr(xn, "tensor") else xn
                    op("sync", lambda e, s=s: e.dma_start(
                        out=h0[:, :],
                        in_=bass.AP(h0_sp, s * G * TS1,
                                    [[NS1 * G * TS1, P], [1, G * TS1]])),
                        reads=["h0_sp"], writes=[h0], dma=True)
                    bcast(SS["t"][0:1, s * TS1:(s + 1) * TS1], TS1, pb,
                          "srstd2", dst=bb0)
                    for g in range(G):
                        op("vector", lambda e, g=g: e.tensor_tensor(
                            xn[:, g * PITCH + 1:g * PITCH + 1 + TS1],
                            h0[:, g * TS1:(g + 1) * TS1], bb0[:, 0:TS1], ALU.mult),
                            reads=[h0, bb0], writes=[xn])
                    for g in range(G):
                        op("scalar", lambda e, g=g: e.activation(
                            xn[:, g * PITCH + 1:g * PITCH + 1 + TS1],
                            xn[:, g * PITCH + 1:g * PITCH + 1 + TS1],
                            AF.Identity, scale=pvc(g, 0), bias=pvc(g, 1)),
                            reads=[xn, pv], writes=[xn])
                    op("vector", lambda e: e.tensor_copy(
                        bass.AP(xnt, 0, [[G * PITCH, P], [PITCH, G]]),
                        xn_last[:, :]), reads=[xn_last], writes=[xn])
                    op("vector", lambda e: e.tensor_copy(
                        xn_last[:, :],
                        bass.AP(xnt, TS1, [[G * PITCH, P], [PITCH, G]])),
                        reads=[xn], writes=[xn_last])
                    op("vector", lambda e: e.tensor_tensor(
                        h0[:, :].rearrange("p (g t) -> p g t", g=G),
                        bass.AP(xnt, 1, [[G * PITCH, P], [PITCH, G], [1, TS1]]),
                        bass.AP(xnt, 0, [[G * PITCH, P], [PITCH, G], [1, TS1]]),
                        ALU.subtract), reads=[xn], writes=[h0])
                    for g in range(G):
                        op("vector", lambda e, g=g: e.scalar_tensor_tensor(
                            mixk[:, g * TS1:(g + 1) * TS1],
                            h0[:, g * TS1:(g + 1) * TS1], pvc(g, 6),
                            bass.AP(xnt, g * PITCH, [[G * PITCH, P], [1, TS1]]),
                            ALU.mult, ALU.add),
                            reads=[h0, pv, xn], writes=[mixk])

                def e2(s):
                    p = s % 2
                    h0, xn, mixk = h0b[p], xnb[p], mkb[p]
                    eks = ekb[p]
                    xnt = xn.tensor if hasattr(xn, "tensor") else xn
                    for m in range(G):
                        bank = pm0 if m % 2 == 0 else pm1
                        mm_group(bank, TS1,
                                 lambda k, m=m: wkb[:, k * D + m * P:
                                                    k * D + m * P + P],
                                 lambda k: mixk[:, k * TS1:(k + 1) * TS1], G,
                                 [wkb, mixk])
                        op("scalar", lambda e, bank=bank, m=m, eks=eks: e.activation(
                            eks[:, m * TS1:(m + 1) * TS1], bank[:, 0:TS1],
                            AF.Exp), reads=[bank], writes=[eks])
                        yield
                    for wb_, pj, kind in ((wvb, 7, "v"), (wrb, 8, "r")):
                        for g in range(G):
                            op("vector", lambda e, g=g, pj=pj, h0=h0, xnt=xnt:
                                e.scalar_tensor_tensor(
                                    mixb[:, g * TS1:(g + 1) * TS1],
                                    h0[:, g * TS1:(g + 1) * TS1], pvc(g, pj),
                                    bass.AP(xnt, g * PITCH,
                                            [[G * PITCH, P], [1, TS1]]),
                                    ALU.mult, ALU.add),
                                reads=[h0, pv, xn], writes=[mixb])
                        yield
                        for m in range(G):
                            bank = pm0 if m % 2 == 0 else pm1
                            mm_group(bank, TS1,
                                     lambda k, m=m, wb_=wb_: wb_[:, k * D + m * P:
                                                                 k * D + m * P + P],
                                     lambda k: mixb[:, k * TS1:(k + 1) * TS1], G,
                                     [wb_, mixb])
                            if kind == "v":
                                op("vector", lambda e, bank=bank, m=m, eks=eks:
                                    e.tensor_tensor(
                                        ekvs[:, m * TS1:(m + 1) * TS1],
                                        bank[:, 0:TS1],
                                        eks[:, m * TS1:(m + 1) * TS1], ALU.mult),
                                    reads=[bank, eks], writes=[ekvs])
                            else:
                                op("scalar", lambda e, bank=bank, m=m: e.activation(
                                    scn[:, 0:TS1], bank[:, 0:TS1], AF.Tanh,
                                    scale=0.5), reads=[bank], writes=[scn])
                                op("vector", lambda e, m=m: e.tensor_scalar(
                                    srb[:, m * TS1:(m + 1) * TS1], scn[:, 0:TS1],
                                    0.5, 0.5, ALU.mult, ALU.add),
                                    reads=[scn], writes=[srb])
                            yield
                    op("scalar", lambda e, s=s, eks=eks: e.dma_start(
                        out=bass.AP(ek_sp, s * G * TS1, [[NS1 * G * TS1, P],
                                                         [1, G * TS1]]),
                        in_=eks[:, :]), reads=[eks], writes=["ek_sp"], dma=True)
                    op("scalar", lambda e, s=s: e.dma_start(
                        out=bass.AP(ekv_sp, s * G * TS1, [[NS1 * G * TS1, P],
                                                          [1, G * TS1]]),
                        in_=ekvs[:, :]), reads=[ekvs], writes=["ekv_sp"], dma=True)
                    op("scalar", lambda e, s=s: e.dma_start(
                        out=bass.AP(sr_sp, s * G * TS1, [[NS1 * G * TS1, P],
                                                         [1, G * TS1]]),
                        in_=srb[:, :]), reads=[srb], writes=["sr_sp"], dma=True)
                    yield

                def e3(s):
                    eks = ekb[s % 2]
                    for arr, st_, pen in ((ekvs, stA, penA), (eks, stB, penB)):
                        for g in range(G):
                            op("vector", lambda e, g=g, arr=arr, st_=st_:
                                e.tensor_tensor_scan(
                                    scn2[:, 0:TS1], dvc(g, 0).broadcast_to([P, TS1]),
                                    arr[:, g * TS1:(g + 1) * TS1], st_[:, g:g + 1],
                                    ALU.mult, ALU.add),
                                reads=[dv, arr, st_], writes=[scn2])
                            if s == NS1 - 1:
                                op("vector", lambda e, g=g, pen=pen: e.tensor_copy(
                                    pen[:, g:g + 1], scn2[:, TS1 - 2:TS1 - 1]),
                                    reads=[scn2], writes=[pen])
                            op("vector", lambda e, g=g, st_=st_: e.tensor_copy(
                                st_[:, g:g + 1], scn2[:, TS1 - 1:TS1]),
                                reads=[scn2], writes=[st_])
                            yield
                    if s == NS1 - 1:
                        for arr, dst in ((eks, eke), (ekvs, ekve), (srb, sre)):
                            at = arr.tensor if hasattr(arr, "tensor") else arr
                            op("vector", lambda e, at=at, dst=dst, arr=arr:
                                e.tensor_copy(
                                    dst[:, :],
                                    bass.AP(at, TS1 - 1, [[G * TS1, P], [TS1, G]])),
                                reads=[arr], writes=[dst])
                        yield

                def lw_a():
                    load_fm(wkb, wk_d, D, D)
                    yield
                    load_fm(wvb, wv_d, D, D)
                    yield
                    load_fm(wrb, wr_d, D, D)
                    yield
                drain(lw_a())
                ag0_p2(sb)
                e1(0)
                e1(1)
                drain(e2(0))
                e1(2)
                ilv(e3(0), e2(1))
                e1(3)
                ilv(e3(1), e2(2))
                ilv(e3(2), e2(3))
                drain(e3(3))
                trk.full_barrier()

        def sel_from(sag, msk, col0, width, dst, dkey, tse):
            op("vector", lambda e: e.tensor_tensor(
                tse[:, 0:NCORES * width].rearrange("p (r c) -> p r c", r=NCORES),
                sag[:, :].rearrange("p (r c) -> p r c", r=NCORES)[:, :, col0:col0 + width],
                msk.rearrange("p (r o) -> p r o", o=1).broadcast_to([P, NCORES, width]),
                ALU.mult), reads=[sag, msk], writes=[tse])
            tset = tse.tensor if hasattr(tse, "tensor") else tse
            op("vector", lambda e: e.reduce_sum(
                dst, bass.AP(tset, 0, [[NCORES * 64, P], [1, width], [width, NCORES]]),
                axis=AX.X), reads=[tse], writes=[dkey])

        def ag1_p1(sb):
            pk = sb("g_pk", [P, 64], F32)
            for i, src_ in enumerate((stA, stB, penA, penB, eke, ekve, sre,
                                      h0_end)):
                op("vector", lambda e, i=i, src_=src_: e.tensor_copy(
                    pk[:, i * 8:(i + 1) * 8], src_[:, :]),
                    reads=[src_], writes=[pk])
            op("gpsimd", lambda e: e.dma_start(out=ag1_in.ap(), in_=pk[:, :]),
                reads=[pk], writes=["ag1_in"], dma=True)
            op("gpsimd", lambda e: e.collective_compute(
                "AllGather", ALU.bypass, replica_groups=[list(range(NCORES))],
                ins=[ag1_in.ap().opt()], outs=[ag1_out.ap().opt()]),
                reads=["ag1_in"], writes=["ag1_out"], async_sem="cc1", inc=1)

        def ag1_p2(sb):
            if True:
                pkunused = None
                sag = sb("g_sag", [P, NCORES * 64], F32)

                tse = sb("g_tse", [P, NCORES * 64], F32)
                dsp2 = sb("g_dsp2", [P, 16], F32)
                SAB = sb("g_sab", [P, 16], F32)
                SAB2 = sb("g_sab2", [P, 16], F32)
                selbuf = sb("g_selb", [P, 16], F32)
                dm = sb("g_dm", [P, 16], F32)
                hl = sb("g_hl", [P, 64], F32)
                op("gpsimd", lambda e: e.dma_start(
                    out=sag[:, :].rearrange("p (r c) -> p r c", r=NCORES),
                    in_=bass.AP(ag1_out, 0, [[64, P], [P * 64, NCORES], [1, 64]])),
                    reads=["ag1_out"], writes=[sag], dma=True)
                op("vector", lambda e: e.tensor_copy(dsp2[:, 0:8], dvrow(3)),
                    reads=[dv], writes=[dsp2])
                op("vector", lambda e: e.tensor_copy(dsp2[:, 8:16], dvrow(3)),
                    reads=[dv], writes=[dsp2])

                def fold(dst, selm, mb_off):
                    sel_from(sag, selm[:, 0:G], 0, 16, dst[:, :], dst, tse)
                    for j in (1, 2):
                        op("vector", lambda e, j=j: e.tensor_scalar(
                            dm[:, :], dsp2[:, :],
                            mbit[:, mb_off + 2 * (j - 1):mb_off + 2 * (j - 1) + 1],
                            None, ALU.mult), reads=[dsp2, mbit], writes=[dm])
                        op("vector", lambda e, j=j: e.tensor_scalar(
                            dm[:, :], dm[:, :],
                            mbit[:, mb_off + 2 * (j - 1) + 1:mb_off + 2 * (j - 1) + 2],
                            None, ALU.add), reads=[dm, mbit], writes=[dm])
                        op("vector", lambda e: e.tensor_tensor(
                            dst[:, :], dst[:, :], dm[:, :], ALU.mult),
                            reads=[dst, dm], writes=[dst])
                        sel_from(sag, selm[:, j * G:(j + 1) * G], 0, 16,
                                 selbuf[:, :], selbuf, tse)
                        op("vector", lambda e: e.tensor_tensor(
                            dst[:, :], dst[:, :], selbuf[:, :], ALU.add),
                            reads=[dst, selbuf], writes=[dst])
                fold(SAB, selA, 0)
                fold(SAB2, selB, 4)
                op("vector", lambda e: e.tensor_copy(carA[:, :], SAB[:, 0:8]),
                    reads=[SAB], writes=[carA])
                op("vector", lambda e: e.tensor_copy(carB[:, :], SAB[:, 8:16]),
                    reads=[SAB], writes=[carB])
                sel_from(sag, msel0[:, :], 16, 16, hl[:, 0:16], hl, tse)
                sel_from(sag, msel0[:, :], 32, 32, hl[:, 16:48], hl, tse)
                op("vector", lambda e: e.tensor_copy(dm[:, 0:8], dvrow(4)),
                    reads=[dv], writes=[dm])
                op("vector", lambda e: e.tensor_copy(dm[:, 8:16], dvrow(4)),
                    reads=[dv], writes=[dm])
                op("vector", lambda e: e.tensor_tensor(
                    SAB2[:, :], SAB2[:, :], dm[:, :], ALU.mult),
                    reads=[SAB2, dm], writes=[SAB2])
                op("vector", lambda e: e.tensor_tensor(
                    hl[:, 0:16], hl[:, 0:16], SAB2[:, :], ALU.add),
                    reads=[hl, SAB2], writes=[hl])
                op("vector", lambda e: e.tensor_tensor(
                    hl[:, 48:56], hl[:, 0:8], dvrow(1), ALU.mult),
                    reads=[hl, dv], writes=[hl])
                op("vector", lambda e: e.tensor_tensor(
                    hl[:, 48:56], hl[:, 48:56], hl[:, 24:32], ALU.add),
                    reads=[hl], writes=[hl])
                op("vector", lambda e: e.tensor_tensor(
                    hl[:, 56:64], hl[:, 8:16], dvrow(1), ALU.mult),
                    reads=[hl, dv], writes=[hl])
                op("vector", lambda e: e.tensor_tensor(
                    hl[:, 56:64], hl[:, 56:64], hl[:, 16:24], ALU.add),
                    reads=[hl], writes=[hl])
                op("vector", lambda e: e.tensor_scalar(
                    hl[:, 56:64], hl[:, 56:64], mhalo[:, 1:2], None, ALU.add),
                    reads=[hl, mhalo], writes=[hl])
                op("vector", lambda e: e.reciprocal(hl[:, 56:64], hl[:, 56:64]),
                    reads=[hl], writes=[hl])
                op("vector", lambda e: e.tensor_tensor(
                    hl[:, 48:56], hl[:, 48:56], hl[:, 56:64], ALU.mult),
                    reads=[hl], writes=[hl])
                srh = sb("g_srh", [P, 8], BF16)
                op("vector", lambda e: e.tensor_tensor(
                    srh[:, :], hl[:, 48:56], hl[:, 32:40], ALU.mult),
                    reads=[hl], writes=[srh])
                op("vector", lambda e: e.tensor_copy(srh_p[:, :], srh[:, :]),
                    reads=[srh], writes=[srh_p])
                op("vector", lambda e: e.tensor_copy(h0sel_p[:, :], hl[:, 40:48]),
                    reads=[hl], writes=[h0sel_p])

        def b1():
            with ExitStack() as ph:
                sb = lambda *a: ph.enter_context(nc.sbuf_tensor(*a))
                ag1_p1(sb)
                wob = sb("b1_wo", [P, G * D], BF16)
                frb_w = sb("b1_fwr", [P, G * D], BF16)
                eks = sb("b1_ek", [P, G * TS1], F32)
                ekvs = sb("b1_ekv", [P, G * TS1], F32)
                srs = sb("b1_sr", [P, G * TS1], BF16)
                h0 = sb("b1_h0", [P, G * TS1], F32)
                scA = sb("b1_scA", [P, G * (TS1 + 1)], F32)
                scB = sb("b1_scB", [P, G * (TS1 + 1)], F32)
                numb = sb("b1_num", [P, G * TS1], F32)
                denb = sb("b1_den", [P, G * TS1], F32)
                srout = sb("b1_srout", [P, G * TS1], BF16)
                xn2 = sb("b1_xn2", [P, G * (TS1 + 1)], F32)
                rows = sb("b1_rows", [1, 3 * TS1], F32)
                mixb = sb("b1_mix", [P, G * TS1], BF16)
                bb0 = sb("b1_bb0", [P, TS1], F32)
                bb1 = sb("b1_bb1", [P, TS1], F32)
                xb = srout
                sq = mixb
                load_fm(wob, wo_d, D, D)
                load_fm(frb_w, fwr_d, D, D)
                ag1_p2(sb)
                PITCH = TS1 + 1
                scAt = scA.tensor if hasattr(scA, "tensor") else scA
                scBt = scB.tensor if hasattr(scB, "tensor") else scB
                xn2t = xn2.tensor if hasattr(xn2, "tensor") else xn2

                def go_projh(e):
                    last = None
                    for m in range(G):
                        for k in range(G):
                            last = e.matmul(pq[:, m:m + 1],
                                            wob[:, k * D + m * P:k * D + m * P + P],
                                            srh_p[:, k:k + 1], start=(k == 0),
                                            stop=(k == G - 1))
                    return last
                op("tensor", go_projh, reads=[wob, srh_p], writes=[pq])
                hh = sb("b1_hh", [P, 16], F32)
                op("vector", lambda e: e.tensor_tensor(
                    hh[:, 0:8], pq[:, 0:8], h0sel_p[:, :], ALU.add),
                    reads=[pq, h0sel_p], writes=[hh])
                hb = sb("b1_hb", [P, 16], BF16)
                op("scalar", lambda e: e.copy(hb[:, 0:8], hh[:, 0:8]),
                    reads=[hh], writes=[hb])
                op("scalar", lambda e: e.activation(hb[:, 8:16], hh[:, 0:8],
                                                    AF.Square),
                    reads=[hh], writes=[hb])

                def go_hsum(e):
                    last = None
                    for k in range(G):
                        last = e.matmul(pq[0:1, 8:9], ones_c[:, 0:1], hb[:, k:k + 1],
                                        start=(k == 0), stop=(k == G - 1))
                    for k in range(G):
                        last = e.matmul(pq[0:1, 9:10], ones_c[:, 0:1],
                                        hb[:, 8 + k:9 + k],
                                        start=(k == 0), stop=(k == G - 1))
                    return last
                op("tensor", go_hsum, reads=[hb, ones_c], writes=[pq])
                hr = sb("b1_hr", [1, 4], F32)
                op("vector", lambda e: e.tensor_scalar(
                    hr[0:1, 0:2], pq[0:1, 8:10], 1.0 / D, None, ALU.mult),
                    reads=[pq], writes=[hr])
                op("vector", lambda e: e.tensor_tensor(
                    hr[0:1, 2:3], hr[0:1, 0:1], hr[0:1, 0:1], ALU.mult),
                    reads=[hr], writes=[hr])
                op("vector", lambda e: e.tensor_tensor(
                    hr[0:1, 1:2], hr[0:1, 1:2], hr[0:1, 2:3], ALU.subtract),
                    reads=[hr], writes=[hr])
                op("scalar", lambda e: e.activation(hr[0:1, 1:2], hr[0:1, 1:2],
                                                    AF.Ln, bias=1e-5),
                    reads=[hr], writes=[hr])
                op("scalar", lambda e: e.activation(hr[0:1, 1:2], hr[0:1, 1:2],
                                                    AF.Exp, scale=-0.5),
                    reads=[hr], writes=[hr])
                op("tensor", lambda e: e.matmul(pq[:, 10:11], ones_r[:, :],
                                                hr[0:1, 0:1], start=True, stop=True),
                    reads=[hr, ones_r], writes=[pq])
                op("tensor", lambda e: e.matmul(pq[:, 11:12], ones_r[:, :],
                                                hr[0:1, 1:2], start=True, stop=True),
                    reads=[hr, ones_r], writes=[pq])
                op("vector", lambda e: e.tensor_scalar(
                    hh[:, 8:16], hh[:, 0:8], pq[:, 10:11], None, ALU.subtract),
                    reads=[hh, pq], writes=[hh])
                op("vector", lambda e: e.tensor_scalar(
                    hh[:, 8:16], hh[:, 8:16], pq[:, 11:12], None, ALU.mult),
                    reads=[hh, pq], writes=[hh])
                op("vector", lambda e: e.tensor_tensor(
                    hh[:, 8:16], hh[:, 8:16], pvrow(2), ALU.mult),
                    reads=[hh, pv], writes=[hh])
                op("vector", lambda e: e.tensor_tensor(
                    hh[:, 8:16], hh[:, 8:16], pvrow(3), ALU.add),
                    reads=[hh, pv], writes=[hh])
                op("vector", lambda e: e.tensor_scalar(
                    xn2_last[:, :], hh[:, 8:16], mhalo[:, 0:1], None, ALU.mult),
                    reads=[hh, mhalo], writes=[xn2_last])
                op("vector", lambda e: e.tensor_copy(st2A[:, :], carA[:, :]),
                    reads=[carA], writes=[st2A])
                op("vector", lambda e: e.tensor_copy(st2B[:, :], carB[:, :]),
                    reads=[carB], writes=[st2B])

                def g1(s):
                    for (dst, spt, key) in ((eks, ek_sp, "ek_sp"),
                                            (ekvs, ekv_sp, "ekv_sp"),
                                            (srs, sr_sp, "sr_sp")):
                        op("sync", lambda e, dst=dst, spt=spt, s=s: e.dma_start(
                            out=dst[:, :],
                            in_=bass.AP(spt, s * G * TS1,
                                        [[NS1 * G * TS1, P], [1, G * TS1]])),
                            reads=[key], writes=[dst], dma=True)

                def g1b(s):
                    op("sync", lambda e, s=s: e.dma_start(
                        out=h0[:, :],
                        in_=bass.AP(h0_sp, s * G * TS1,
                                    [[NS1 * G * TS1, P], [1, G * TS1]])),
                        reads=["h0_sp"], writes=[h0], dma=True)

                def g2(s):
                    for (sc_, sct_, st_, arr) in ((scA, scAt, st2A, ekvs),
                                                  (scB, scBt, st2B, eks)):
                        for g in range(G):
                            op("vector", lambda e, g=g, sc_=sc_, st_=st_:
                                e.tensor_copy(
                                    sc_[:, g * PITCH:g * PITCH + 1], st_[:, g:g + 1]),
                                reads=[st_], writes=[sc_])
                            op("vector", lambda e, g=g, sc_=sc_, st_=st_, arr=arr:
                                e.tensor_tensor_scan(
                                    sc_[:, g * PITCH + 1:g * PITCH + 1 + TS1],
                                    dvc(g, 0).broadcast_to([P, TS1]),
                                    arr[:, g * TS1:(g + 1) * TS1], st_[:, g:g + 1],
                                    ALU.mult, ALU.add),
                                reads=[dv, arr, st_], writes=[sc_])
                            op("vector", lambda e, g=g, sc_=sc_, st_=st_:
                                e.tensor_copy(
                                    st_[:, g:g + 1],
                                    sc_[:, g * PITCH + TS1:g * PITCH + TS1 + 1]),
                                reads=[sc_], writes=[st_])
                            yield
                    for g in range(G):
                        op("vector", lambda e, g=g: e.scalar_tensor_tensor(
                            numb[:, g * TS1:(g + 1) * TS1],
                            bass.AP(scAt, g * PITCH, [[G * PITCH, P], [1, TS1]]),
                            dvc(g, 1), ekvs[:, g * TS1:(g + 1) * TS1],
                            ALU.mult, ALU.add),
                            reads=[scA, dv, ekvs], writes=[numb])
                        op("vector", lambda e, g=g: e.scalar_tensor_tensor(
                            denb[:, g * TS1:(g + 1) * TS1],
                            bass.AP(scBt, g * PITCH, [[G * PITCH, P], [1, TS1]]),
                            dvc(g, 1), eks[:, g * TS1:(g + 1) * TS1],
                            ALU.mult, ALU.add),
                            reads=[scB, dv, eks], writes=[denb])
                        yield
                    op("scalar", lambda e: e.activation(denb[:, :], denb[:, :], AF.Ln),
                        reads=[denb], writes=[denb])
                    yield
                    op("scalar", lambda e: e.activation(denb[:, :], denb[:, :],
                                                        AF.Exp, scale=-1.0),
                        reads=[denb], writes=[denb])
                    yield
                    op("gpsimd", lambda e: e.tensor_tensor(
                        numb[:, :], numb[:, :], denb[:, :], ALU.mult),
                        reads=[numb, denb], writes=[numb])
                    yield
                    op("gpsimd", lambda e: e.tensor_tensor(
                        srout[:, :], numb[:, :], srs[:, :], ALU.mult),
                        reads=[numb, srs], writes=[srout])
                    yield

                def g3(s):
                    h1 = h0
                    t1 = denb
                    for m in range(G):
                        bank = pm0 if m % 2 == 0 else pm1
                        mm_group(bank, TS1,
                                 lambda k, m=m: wob[:, k * D + m * P:k * D + m * P + P],
                                 lambda k: srout[:, k * TS1:(k + 1) * TS1], G,
                                 [wob, srout])
                        op("vector", lambda e, bank=bank, m=m: e.tensor_tensor(
                            h1[:, m * TS1:(m + 1) * TS1], bank[:, 0:TS1],
                            h0[:, m * TS1:(m + 1) * TS1], ALU.add),
                            reads=[bank, h0], writes=[h0])
                        yield
                    op("scalar", lambda e, s=s: e.dma_start(
                        out=bass.AP(h1_sp, s * G * TS1, [[NS1 * G * TS1, P],
                                                         [1, G * TS1]]),
                        in_=h1[:, :]), reads=[h0], writes=["h1_sp"], dma=True)
                    yield
                    rowm, rowr, _ = yield from ln_stats2(h1, TS1, xb, sq, rows)
                    yield
                    bcast(rowm, TS1, pb, rows.name + "_m", dst=bb0)
                    bcast(rowr, TS1, pa1, rows.name + "_r", dst=bb1)
                    yield
                    for g in range(G):
                        en = "gpsimd" if g < 3 else "vector"
                        op(en, lambda e, g=g: e.tensor_tensor(
                            xn2[:, g * PITCH + 1:g * PITCH + 1 + TS1],
                            h1[:, g * TS1:(g + 1) * TS1], bb0[:, 0:TS1],
                            ALU.subtract), reads=[h0, bb0], writes=[xn2])
                        op(en, lambda e, g=g: e.tensor_tensor(
                            xn2[:, g * PITCH + 1:g * PITCH + 1 + TS1],
                            xn2[:, g * PITCH + 1:g * PITCH + 1 + TS1],
                            bb1[:, 0:TS1], ALU.mult),
                            reads=[xn2, bb1], writes=[xn2])
                        op("scalar", lambda e, g=g: e.activation(
                            xn2[:, g * PITCH + 1:g * PITCH + 1 + TS1],
                            xn2[:, g * PITCH + 1:g * PITCH + 1 + TS1],
                            AF.Identity, scale=pvc(g, 2), bias=pvc(g, 3)),
                            reads=[xn2, pv], writes=[xn2])
                        yield
                    op("vector", lambda e: e.tensor_copy(
                        bass.AP(xn2t, 0, [[G * PITCH, P], [PITCH, G]]),
                        xn2_last[:, :]), reads=[xn2_last], writes=[xn2])
                    op("vector", lambda e: e.tensor_copy(
                        xn2_last[:, :],
                        bass.AP(xn2t, TS1, [[G * PITCH, P], [PITCH, G]])),
                        reads=[xn2], writes=[xn2_last])
                    yield
                    dbuf2 = h0
                    op("vector", lambda e: e.tensor_tensor(
                        dbuf2[:, 0:G * TS1].rearrange("p (g t) -> p g t", g=G),
                        bass.AP(xn2t, 1, [[G * PITCH, P], [PITCH, G], [1, TS1]]),
                        bass.AP(xn2t, 0, [[G * PITCH, P], [PITCH, G], [1, TS1]]),
                        ALU.subtract), reads=[xn2], writes=[h0])
                    yield
                    for g in range(G):
                        op("vector", lambda e, g=g: e.scalar_tensor_tensor(
                            mixb[:, g * TS1:(g + 1) * TS1],
                            dbuf2[:, g * TS1:(g + 1) * TS1], pvc(g, 9),
                            bass.AP(xn2t, g * PITCH, [[G * PITCH, P], [1, TS1]]),
                            ALU.mult, ALU.add),
                            reads=[h0, pv, xn2], writes=[mixb])
                        yield
                    op("scalar", lambda e, s=s: e.dma_start(
                        out=bass.AP(mfk_sp, s * G * TS1, [[NS1 * G * TS1, P],
                                                          [1, G * TS1]]),
                        in_=mixb[:, :]), reads=[mixb], writes=["mfk_sp"], dma=True)
                    yield
                    for g in range(G):
                        op("vector", lambda e, g=g: e.scalar_tensor_tensor(
                            mixb[:, g * TS1:(g + 1) * TS1],
                            dbuf2[:, g * TS1:(g + 1) * TS1], pvc(g, 10),
                            bass.AP(xn2t, g * PITCH, [[G * PITCH, P], [1, TS1]]),
                            ALU.mult, ALU.add),
                            reads=[h0, pv, xn2], writes=[mixb])
                        yield
                    for m in range(G):
                        bank = pm0 if m % 2 == 0 else pm1
                        mm_group(bank, TS1,
                                 lambda k, m=m: frb_w[:, k * D + m * P:
                                                      k * D + m * P + P],
                                 lambda k: mixb[:, k * TS1:(k + 1) * TS1], G,
                                 [frb_w, mixb])
                        op("scalar", lambda e, bank=bank: e.activation(
                            pa1[:, 0:TS1], bank[:, 0:TS1], AF.Tanh, scale=0.5),
                            reads=[bank], writes=[pa1])
                        op("vector", lambda e, m=m: e.tensor_scalar(
                            numb[:, m * TS1:(m + 1) * TS1], pa1[:, 0:TS1],
                            0.5, 0.5, ALU.mult, ALU.add),
                            reads=[pa1], writes=[numb])
                        yield
                    op("scalar", lambda e, s=s: e.dma_start(
                        out=bass.AP(sgf_sp, s * G * TS1, [[NS1 * G * TS1, P],
                                                          [1, G * TS1]]),
                        in_=numb[:, :]), reads=[numb], writes=["sgf_sp"], dma=True)
                    yield

                g1(0)
                g1b(0)
                drain(g2(0))
                g1(1)
                ilv(g3(0), g2(1))
                g1b(1)
                g1(2)
                ilv(g3(1), g2(2))
                g1b(2)
                g1(3)
                ilv(g3(2), g2(3))
                g1b(3)
                drain(g3(3))
                trk.full_barrier()

        def b2():
            with ExitStack() as ph:
                sb = lambda *a: ph.enter_context(nc.sbuf_tensor(*a))
                fkb = sb("b2_fwk", [P, G * FD], BF16)
                fvb = sb("b2_fwv", [P, 32 * D], BF16)
                for h in range(4):
                    q = "gpsimd"
                    srcap = bass.AP(fwk_d.tensor, h * D,
                                    [[FD, P], [P * FD, G], [1, D]])
                    op(q, lambda e, srcap=srcap, h=h: e.dma_start(
                        out=fkb[:, :].rearrange("p (g t) -> p g t", g=G)
                        [:, :, h * D:(h + 1) * D], in_=srcap),
                        reads=[fwk_d], writes=[fkb], dma=True)
                for h in range(4):
                    q = "gpsimd"
                    srcap = bass.AP(fwv_d.tensor, h * 8 * P * D,
                                    [[D, P], [P * D, G], [1, D]])
                    op(q, lambda e, srcap=srcap, h=h: e.dma_start(
                        out=fvb[:, h * 8 * D:(h + 1) * 8 * D]
                        .rearrange("p (g t) -> p g t", g=G), in_=srcap),
                        reads=[fwv_d], writes=[fvb], dma=True)
                mfk = sb("b2_mfk", [P, G * TS2], BF16)
                h1q = sb("b2_h1", [P, G * TS2], F32)
                sgq = sb("b2_sg", [P, G * TS2], F32)
                rfk = sb("b2_rfk", [P, 32 * TS2], BF16)
                tv = sb("b2_tv", [P, 2 * TS2], F32)
                x2p = h1q
                for s2 in range(NS2):
                    s = s2
                    off = 0
                    op("sync", lambda e, s=s, off=off: e.dma_start(
                        out=mfk[:, :].rearrange("p (g t) -> p g t", g=G),
                        in_=bass.AP(mfk_sp, s * G * TS1 + off,
                                    [[NS1 * G * TS1, P], [TS1, G], [1, TS2]])),
                        reads=["mfk_sp"], writes=[mfk], dma=True)
                    op("sync", lambda e, s=s, off=off: e.dma_start(
                        out=h1q[:, :].rearrange("p (g t) -> p g t", g=G),
                        in_=bass.AP(h1_sp, s * G * TS1 + off,
                                    [[NS1 * G * TS1, P], [TS1, G], [1, TS2]])),
                        reads=["h1_sp"], writes=[h1q], dma=True)
                    op("sync", lambda e, s=s, off=off: e.dma_start(
                        out=sgq[:, :].rearrange("p (g t) -> p g t", g=G),
                        in_=bass.AP(sgf_sp, s * G * TS1 + off,
                                    [[NS1 * G * TS1, P], [TS1, G], [1, TS2]])),
                        reads=["sgf_sp"], writes=[sgq], dma=True)
                    for fo in range(32):
                        bank = pm0 if fo % 2 == 0 else pm1
                        mm_group(bank, TS2,
                                 lambda k, fo=fo: fkb[:, k * FD + fo * P:
                                                      k * FD + fo * P + P],
                                 lambda k: mfk[:, k * TS2:(k + 1) * TS2], G,
                                 [fkb, mfk])
                        op("scalar", lambda e, bank=bank, fo=fo: e.activation(
                            rfk[:, fo * TS2:(fo + 1) * TS2], bank[:, 0:TS2],
                            AF.Relu), reads=[bank], writes=[rfk])
                        op("scalar", lambda e, fo=fo: e.activation(
                            rfk[:, fo * TS2:(fo + 1) * TS2],
                            rfk[:, fo * TS2:(fo + 1) * TS2],
                            AF.Square), reads=[rfk], writes=[rfk])
                    for m in range(G):
                        bank = pa0 if m % 2 == 0 else pa1
                        mm_group(bank, TS2,
                                 lambda k, m=m: fvb[:, k * D + m * P:
                                                    k * D + m * P + P],
                                 lambda k: rfk[:, k * TS2:(k + 1) * TS2], 32,
                                 [fvb, rfk])
                        tvs = tv[:, (m % 2) * TS2:(m % 2) * TS2 + TS2]
                        tvk = f"tv{m % 2}"
                        op("vector", lambda e, bank=bank, m=m, tvs=tvs: e.tensor_tensor(
                            tvs, bank[:, 0:TS2],
                            sgq[:, m * TS2:(m + 1) * TS2], ALU.mult),
                            reads=[bank, sgq], writes=[tvk])
                        op("gpsimd", lambda e, m=m, tvs=tvs: e.tensor_tensor(
                            x2p[:, m * TS2:(m + 1) * TS2], tvs,
                            h1q[:, m * TS2:(m + 1) * TS2], ALU.add),
                            reads=[tvk, h1q], writes=[x2p])
                    op("scalar", lambda e, s=s, off=off: e.dma_start(
                        out=bass.AP(x2p_sp, s * G * TS1 + off,
                                    [[NS1 * G * TS1, P], [TS1, G], [1, TS2]]),
                        in_=x2p[:, :].rearrange("p (g t) -> p g t", g=G)),
                        reads=[x2p], writes=["x2p_sp"], dma=True)
                trk.full_barrier()

        def b3():
            with ExitStack() as ph:
                sb = lambda *a: ph.enter_context(nc.sbuf_tensor(*a))
                alloc_ffn_weights(sb, "c")
                x2b = [sb("c_x2A", [P, G * TS1], F32),
                       sb("c_x2B", [P, G * TS1], F32)]
                x1b = sb("c_x1", [P, G * TS1], F32)
                xb = sb("c_xb", [P, G * TS1], BF16)
                sq = sb("c_sq", [P, G * TS1], BF16)
                rows = sb("c_rows", [1, 3 * TS1], F32)
                t1 = sb("c_t1", [P, G * TS1], F32)
                ybb = [sb("c_ybA", [P, G * TS1], BF16),
                       sb("c_ybB", [P, G * TS1], BF16)]
                x3 = sb("c_x3", [P, G * TS1], F32)
                lg = sb("c_lg", [P, 192], F32)
                wgb = sb("c_wgb", [P, 4 * G], BF16)
                axf = sb("c_axf", [P, TS1], F32)
                wgT = sb("c_wgT", [G, TS1], BF16)
                we = sb("c_we", [P, TS1], BF16)
                axw = sb("c_axw", [P, TS1], BF16)
                tep = sb("c_tep", [P, 2 * TS1], F32)
                bbm = sb("c_bbm", [P, TS1], F32)
                bbr = sb("c_bbr", [P, TS1], F32)

                def c1(s):
                    x2, yb = x2b[s % 2], ybb[s % 2]
                    op("sync", lambda e, s=s, x2=x2: e.dma_start(
                        out=x2[:, :],
                        in_=bass.AP(x2p_sp, s * G * TS1,
                                    [[NS1 * G * TS1, P], [1, G * TS1]])),
                        reads=["x2p_sp"], writes=[x2], dma=True)
                    op("sync", lambda e, s=s: e.dma_start(
                        out=x1b[:, :],
                        in_=bass.AP(x1_sp, s * G * TS1,
                                    [[NS1 * G * TS1, P], [1, G * TS1]])),
                        reads=["x1_sp"], writes=[x1b], dma=True)
                    op("vector", lambda e, x2=x2: e.tensor_tensor(
                        x2[:, :], x2[:, :], x1b[:, :], ALU.add),
                        reads=[x2, x1b], writes=[x2])
                    yield
                    rowm, rowr, _ = yield from ln_stats2(x2, TS1, xb, sq, rows)
                    bcast(rowm, TS1, pb, rows.name + "_m", dst=bbm)
                    bcast(rowr, TS1, pa1, rows.name + "_r", dst=bbr)
                    yield
                    for g in range(G):
                        en = "gpsimd" if g < 3 else "vector"
                        op(en, lambda e, g=g, x2=x2: e.tensor_tensor(
                            t1[:, g * TS1:(g + 1) * TS1],
                            x2[:, g * TS1:(g + 1) * TS1], bbm[:, 0:TS1],
                            ALU.subtract), reads=[x2, bbm], writes=[t1])
                        op(en, lambda e, g=g, yb=yb: e.tensor_tensor(
                            yb[:, g * TS1:(g + 1) * TS1],
                            t1[:, g * TS1:(g + 1) * TS1], bbr[:, 0:TS1], ALU.mult),
                            reads=[t1, bbr], writes=[yb])
                        yield

                def c2(s):
                    x2, yb = x2b[s % 2], ybb[s % 2]
                    fb = {"lg": lg, "wgb": wgb, "wgT": wgT, "we": we, "axw": axw,
                          "axf": axf,
                          "tep": tep, "ptq": pa1, "resid": x2}
                    yield from emit_ffn(
                        yb, TS1, lambda g, x2=x2: x2[:, g * TS1:(g + 1) * TS1],
                        lambda g: (x3[:, g * TS1:(g + 1) * TS1], x3), fb)
                    op("sync", lambda e, s=s: e.dma_start(
                        out=bass.AP(y_d.tensor, s * TS1,
                                    [[SP, P], [P * SP, G], [1, TS1]]),
                        in_=x3[:, :].rearrange("p (g t) -> p g t", g=G)),
                        reads=[x3], writes=["y"], dma=True)
                    yield

                ilv(c1(0), load_ffn_weights(sb, "c"))
                ilv(c2(0), c1(1))
                ilv(c2(1), c1(2))
                ilv(c2(2), c1(3))
                drain(c2(3))
                trk.full_barrier()

        block = outer.enter_context(nc.Block())

        @block.sync
        def _(sync):
            emit_all()

    return nc


def _host_inputs(inputs):
    f32 = np.float32
    x = np.ascontiguousarray(np.swapaxes(np.asarray(inputs["L0_attn_x"], f32), 1, 2))
    pvec = np.zeros((D, NPV), f32)
    pvec[:, 0] = inputs["L0_attn_ln1_w"]
    pvec[:, 1] = inputs["L0_attn_ln1_b"]
    pvec[:, 2] = inputs["L0_attn_ln2_w"]
    pvec[:, 3] = inputs["L0_attn_ln2_b"]
    pvec[:, 4] = inputs["L0_attn_time_decay"]
    pvec[:, 5] = inputs["L0_attn_time_first"]
    pvec[:, 6] = np.asarray(inputs["L0_attn_time_mix_k"], f32).ravel()
    pvec[:, 7] = np.asarray(inputs["L0_attn_time_mix_v"], f32).ravel()
    pvec[:, 8] = np.asarray(inputs["L0_attn_time_mix_r"], f32).ravel()
    pvec[:, 9] = np.asarray(inputs["L0_attn_f_time_mix_k"], f32).ravel()
    pvec[:, 10] = np.asarray(inputs["L0_attn_f_time_mix_r"], f32).ravel()
    pvec[:, 11] = inputs["L0_ffn_base_b"]
    import ml_dtypes
    cns = np.zeros((P, 8), f32)
    for j, v in enumerate([0.5, -0.5, -1.0, 1e-5, 2.0, 1.0, 0.0, 0.0]):
        cns[:, j] = v
    ident = np.eye(P, dtype=ml_dtypes.bfloat16)
    expand = np.zeros((G, P), ml_dtypes.bfloat16)
    for e in range(G):
        expand[e, e * 16:(e + 1) * 16] = 1.0
    onesrow = np.ones((1, P), f32)
    shared = dict(
        wk=np.ascontiguousarray(np.asarray(inputs["L0_attn_w_key"], f32)),
        wv=np.ascontiguousarray(np.asarray(inputs["L0_attn_w_value"], f32)),
        wr=np.ascontiguousarray(np.asarray(inputs["L0_attn_w_recept"], f32)),
        wo=np.ascontiguousarray(np.asarray(inputs["L0_attn_w_output"], f32)),
        fwk=np.ascontiguousarray(np.asarray(inputs["L0_attn_f_w_key"], f32)),
        fwr=np.ascontiguousarray(np.asarray(inputs["L0_attn_f_w_recept"], f32)),
        fwv=np.ascontiguousarray(np.asarray(inputs["L0_attn_f_w_value"], f32)),
        bw=np.ascontiguousarray(np.asarray(inputs["L0_ffn_base_w"], f32).T),
        la=np.ascontiguousarray(
            np.asarray(inputs["L0_ffn_lora_A"], f32).transpose(1, 0, 2)
            .reshape(D, 8 * 16)),
        lb=np.ascontiguousarray(
            np.asarray(inputs["L0_ffn_lora_B"], f32).reshape(8 * 16, D)),
        rw=np.ascontiguousarray(np.asarray(inputs["L0_ffn_router_w"], f32).T),
        rb=np.ascontiguousarray(np.asarray(inputs["L0_ffn_router_b"], f32)
                                .reshape(1, G)),
        pvec=pvec, cns=cns, ident=ident, expand=expand, onesrow=onesrow,
    )
    in_maps = []
    for cid in range(NCORES):
        b, q = cid // 4, cid % 4
        m = dict(shared)
        m["x"] = np.ascontiguousarray(x[b, :, q * SP:(q + 1) * SP])
        msel0 = np.zeros((P, G), f32)
        if q > 0:
            msel0[:, cid - 1] = 1.0
        m["msel0"] = msel0
        selA = np.zeros((P, 3 * G), f32)
        selB = np.zeros((P, 3 * G), f32)
        mbits = np.zeros((P, 8), f32)
        for j in range(3):
            if j < q:
                selA[:, j * G + (b * 4 + j)] = 1.0
        mbits[:, 0] = 1.0 if 1 < q or 1 == q else 0.0
        mbits[:, 0] = 1.0 if 1 < q else 0.0
        mbits[:, 1] = 1.0 - mbits[:, 0]
        mbits[:, 2] = 1.0 if 2 < q else 0.0
        mbits[:, 3] = 1.0 - mbits[:, 2]
        for j in range(3):
            if j < q - 1:
                selB[:, j * G + (b * 4 + j)] = 1.0
        mbits[:, 4] = 1.0 if 1 < q - 1 else 0.0
        mbits[:, 5] = 1.0 - mbits[:, 4]
        mbits[:, 6] = 1.0 if 2 < q - 1 else 0.0
        mbits[:, 7] = 1.0 - mbits[:, 6]
        m["selA"] = selA
        m["selB"] = selB
        m["mbits"] = mbits
        mh = np.zeros((P, 2), f32)
        mh[:, 0] = 0.0 if q == 0 else 1.0
        mh[:, 1] = 1.0 - mh[:, 0]
        m["mhalo"] = mh
        in_maps.append(m)
    return in_maps


_NC_CACHE = {}


def kernel(**inputs):
    in_maps = _host_inputs(inputs)
    if "nc" not in _NC_CACHE:
        _NC_CACHE["nc"] = build_program()
    nc = _NC_CACHE["nc"]
    res = run_bass_kernel_spmd(nc, in_maps, core_ids=list(range(NCORES)))
    out = np.empty((2, D, 8192), np.float32)
    for cid in range(NCORES):
        b, q = cid // 4, cid % 4
        out[b, :, q * SP:(q + 1) * SP] = res.results[cid]["y"]
    return np.ascontiguousarray(np.swapaxes(out, 1, 2))


# version 27
# speedup vs baseline: 1.0360x; 1.0022x over previous
import numpy as np
from contextlib import ExitStack

import concourse.bass as bass
import concourse.mybir as mybir
from concourse.bass_utils import run_bass_kernel_spmd

F32 = mybir.dt.float32
BF16 = mybir.dt.bfloat16
AF = mybir.ActivationFunctionType
ALU = mybir.AluOpType
AX = mybir.AxisListType

P = 128
G = 8
D = 1024
SP = 2048
TS1 = 512
NS1 = SP // TS1
TS2 = 512
NS2 = SP // TS2
NCORES = 8
FD = 4096
NPV = 16

ENGINES = ("sync", "vector", "scalar", "tensor", "gpsimd")


def _key(x):
    if isinstance(x, str):
        return x
    t = getattr(x, "tensor", x)
    return t.name


class Trk:
    NDMA = 6

    def __init__(self, nc, stack):
        self.nc = nc
        self.stack = stack
        self.sems = {}
        self.count = {}
        for e in ENGINES:
            self._mksem(e)
        self.dma_uses = {}
        self.dma_next = {}
        self.waited = {}
        for e in ENGINES:
            self.waited[e] = {}
        self.writer = {}
        self.readers = {}

    def _mksem(self, name):
        self.sems[name] = self.stack.enter_context(self.nc.semaphore(f"S_{name}"))
        self.count[name] = 0
        return self.sems[name]

    def _wait(self, engine, e, sem_name, val):
        if self.waited[engine].get(sem_name, 0) >= val:
            return
        e.wait_ge(self.sems[sem_name], val)
        self.waited[engine][sem_name] = val

    def op(self, engine, fn, reads=(), writes=(), dma=False, inc=None,
           async_sem=None):
        nc = self.nc
        e = getattr(nc, engine)
        needs = {}

        def need(sem_name, c):
            if needs.get(sem_name, 0) < c:
                needs[sem_name] = c
        for r in reads:
            w = self.writer.get(_key(r))
            if w:
                need(*w)
        for wr in writes:
            k = _key(wr)
            w = self.writer.get(k)
            if w:
                need(*w)
            for sem_name, c in self.readers.get(k, {}).items():
                need(sem_name, c)
        for sem_name, c in sorted(needs.items()):
            self._wait(engine, e, sem_name, c)
        if async_sem is not None:
            if async_sem not in self.sems:
                self._mksem(async_sem)
            sem_name = async_sem
            step = inc if inc is not None else 1
            inst = fn(e)
            inst.then_inc(self.sems[sem_name], step)
            self.count[sem_name] += step
        elif dma:
            j = self.dma_next.get(engine, 0)
            self.dma_next[engine] = (j + 1) % self.NDMA
            sem_name = f"{engine}_d{j}"
            if sem_name not in self.sems:
                self._mksem(sem_name)
            uses = self.dma_uses.get(sem_name, 0)
            if uses > 0:
                self._wait(engine, e, sem_name, 16 * uses)
            inst = fn(e)
            inst.then_inc(self.sems[sem_name], 16)
            self.dma_uses[sem_name] = uses + 1
            self.count[sem_name] = 16 * (uses + 1)
        else:
            sem_name = engine
            inst = fn(e)
            step = inc if inc is not None else 1
            inst.then_inc(self.sems[sem_name], step)
            self.count[sem_name] += step
        cnt = self.count[sem_name]
        for wr in writes:
            k = _key(wr)
            self.writer[k] = (sem_name, cnt)
            self.readers[k] = {}
        for r in reads:
            k = _key(r)
            self.readers.setdefault(k, {})
            if self.readers[k].get(sem_name, 0) < cnt:
                self.readers[k][sem_name] = cnt
        return inst

    def full_barrier(self):
        for cons in ENGINES:
            e = getattr(self.nc, cons)
            for sem_name, c in list(self.count.items()):
                if sem_name == cons or not c:
                    continue
                self._wait(cons, e, sem_name, c)


def ilv(*gens):
    its = [iter(g) for g in gens]
    while its:
        for it in list(its):
            try:
                next(it)
            except StopIteration:
                its.remove(it)


def drain(gen):
    for _ in gen:
        pass


def fm_in(dram_ap, rows, row_stride, col0, ncols):
    t = dram_ap.tensor
    return bass.AP(t, col0, [[row_stride, P], [P * row_stride, G], [1, ncols]])


def build_program(debug=0):
    nc = bass.Bass("TRN2", target_bir_lowering=False)
    dt_in = {}

    def din(name, shape, dtype=F32):
        dt_in[name] = nc.dram_tensor(name, shape, dtype, kind="ExternalInput").ap()
        return dt_in[name]

    x_d = din("x", [D, SP])
    wk_d = din("wk", [D, D]); wv_d = din("wv", [D, D])
    wr_d = din("wr", [D, D]); wo_d = din("wo", [D, D])
    fwk_d = din("fwk", [D, FD]); fwr_d = din("fwr", [D, D])
    fwv_d = din("fwv", [FD, D])
    bw_d = din("bw", [D, D]); la_d = din("la", [D, P]); lb_d = din("lb", [P, D])
    rw_d = din("rw", [D, G]); rb_d = din("rb", [1, G])
    pv_d = din("pvec", [D, NPV])
    cns_d = din("cns", [P, 8])
    ident_d = din("ident", [P, P], BF16)
    expand_d = din("expand", [G, P], BF16)
    ones_d = din("onesrow", [1, P])
    msel0_d = din("msel0", [P, G])
    selA_d = din("selA", [P, 3 * G])
    selB_d = din("selB", [P, 3 * G])
    mbit_d = din("mbits", [P, 8])
    mhalo_d = din("mhalo", [P, 2])

    y_d = nc.dram_tensor("y", [D, SP], F32, kind="ExternalOutput").ap()

    def scr(name, cols, dtype=F32):
        return nc.dram_tensor(name, [P, cols], dtype, kind="Internal")

    x1_sp = scr("x1_sp", NS1 * G * TS1)
    h0_sp = scr("h0_sp", NS1 * G * TS1)
    ek_sp = scr("ek_sp", NS1 * G * TS1)
    ekv_sp = scr("ekv_sp", NS1 * G * TS1)
    sr_sp = scr("sr_sp", NS1 * G * TS1, BF16)
    h1_sp = scr("h1_sp", NS1 * G * TS1)
    mfk_sp = scr("mfk_sp", NS1 * G * TS1, BF16)
    sgf_sp = scr("sgf_sp", NS1 * G * TS1)
    x2p_sp = scr("x2p_sp", NS1 * G * TS1)
    bc_sc = nc.dram_tensor("bc_sc", [1, 4096], F32, kind="Internal")
    ag0_in = nc.dram_tensor("ag0_in", [P, G], F32, kind="Internal")
    ag0_out = nc.dram_tensor("ag0_out", [NCORES * P, G], F32, kind="Internal")
    ag1_in = nc.dram_tensor("ag1_in", [P, 64], F32, kind="Internal")
    ag1_out = nc.dram_tensor("ag1_out", [NCORES * P, 64], F32, kind="Internal")

    dbg = {}
    for i in range(debug):
        dbg[i] = nc.dram_tensor(f"dbg{i}", [P, G * TS1], F32, kind="ExternalOutput").ap()

    with ExitStack() as outer:
        sbuf = lambda *a: outer.enter_context(nc.sbuf_tensor(*a))
        psum = lambda *a: outer.enter_context(nc.psum_tensor(*a))
        trk = Trk(nc, outer)
        op = trk.op

        FW = {}
        rbt = sbuf("rbt", [P, G], F32)
        exp_b = sbuf("exp_b", [G, P], BF16)
        id_b = sbuf("id_b", [P, P], BF16)
        ones_r = sbuf("ones_r", [1, P], F32)
        ones_c = sbuf("ones_c", [P, 2], BF16)
        cns = sbuf("cns_s", [P, 8], F32)
        pv = sbuf("pv_s", [P, G * NPV], F32)
        dv = sbuf("dv", [P, G * 8], F32)
        SS = {}
        msel0 = sbuf("msel0_s", [P, G], F32)
        selA = sbuf("selA_s", [P, 3 * G], F32)
        selB = sbuf("selB_s", [P, 3 * G], F32)
        mbit = sbuf("mbit_s", [P, 8], F32)
        mhalo = sbuf("mhalo_s", [P, 2], F32)
        xn_last = sbuf("xn_last", [P, G], F32)
        xn2_last = sbuf("xn2_last", [P, G], F32)
        h0_end = sbuf("h0_end", [P, G], F32)
        stA = sbuf("stA", [P, G], F32)
        stB = sbuf("stB", [P, G], F32)
        penA = sbuf("penA", [P, G], F32)
        penB = sbuf("penB", [P, G], F32)
        eke = sbuf("eke", [P, G], F32)
        ekve = sbuf("ekve", [P, G], F32)
        sre = sbuf("sre", [P, G], F32)
        carA = sbuf("carA", [P, G], F32)
        carB = sbuf("carB", [P, G], F32)
        st2A = sbuf("st2A", [P, G], F32)
        st2B = sbuf("st2B", [P, G], F32)
        srh_p = sbuf("srh_p", [P, 8], BF16)
        h0sel_p = sbuf("h0sel_p", [P, 8], F32)

        pm0 = psum("pm0", [P, TS1], F32)
        pm1 = psum("pm1", [P, TS1], F32)
        pa0 = psum("pa0", [P, TS1], F32)
        pa1 = psum("pa1", [P, TS1], F32)
        prs = psum("prs", [64, TS1], F32)
        pb = psum("pb", [P, TS1], F32)
        pq = psum("pq", [P, TS1], F32)
        ptb = psum("ptb", [P, P], BF16)

        def pvc(g, j):
            return pv[:, g * NPV + j:g * NPV + j + 1]

        def dvc(g, j):
            return dv[:, g * 8 + j:g * 8 + j + 1]

        def dvrow(j):
            t = dv.tensor if hasattr(dv, "tensor") else dv
            return bass.AP(t, j, [[G * 8, P], [8, G]])

        def pvrow(j):
            t = pv.tensor if hasattr(pv, "tensor") else pv
            return bass.AP(t, j, [[G * NPV, P], [NPV, G]])

        CN = {"half": cns[:, 0:1], "neghalf": cns[:, 1:2], "negone": cns[:, 2:3],
              "eps": cns[:, 3:4], "two": cns[:, 4:5], "one": cns[:, 5:6]}
        nc.const_aps.register(F32, 0.5, cns[:, 0:1])
        nc.const_aps.register(F32, -0.5, cns[:, 1:2])
        nc.const_aps.register(F32, -1.0, cns[:, 2:3])
        nc.const_aps.register(F32, 1e-5, cns[:, 3:4])
        nc.const_aps.register(F32, 2.0, cns[:, 4:5])
        nc.const_aps.register(F32, 1.0, cns[:, 5:6])

        AH = {}

        def ag0_p1(sb):
            xe = sb("a0_xe", [P, G], F32)
            r2b = sb("a0_r2b", [P, 2], F32)
            op("tensor", lambda e: e.matmul(
                pq[:, 8:9], ones_r[:, :], SS["t"][0:1, SP - 1:SP],
                start=True, stop=True), reads=["srstd2", ones_r], writes=[pq])
            op("vector", lambda e: e.tensor_copy(r2b[:, 0:1], pq[:, 8:9]),
                reads=[pq], writes=[r2b])
            op("vector", lambda e: e.tensor_scalar(
                xe[:, :], h0_end[:, :], r2b[:, 0:1], None, ALU.mult),
                reads=[h0_end, r2b], writes=[xe])
            op("vector", lambda e: e.tensor_tensor(
                xe[:, :], xe[:, :], pvrow(0), ALU.mult),
                reads=[xe, pv], writes=[xe])
            op("vector", lambda e: e.tensor_tensor(
                xe[:, :], xe[:, :], pvrow(1), ALU.add),
                reads=[xe, pv], writes=[xe])
            op("gpsimd", lambda e: e.dma_start(out=ag0_in.ap(), in_=xe[:, :]),
                reads=[xe], writes=["ag0_in"], dma=True)
            op("gpsimd", lambda e: e.collective_compute(
                "AllGather", ALU.bypass, replica_groups=[list(range(NCORES))],
                ins=[ag0_in.ap().opt()], outs=[ag0_out.ap().opt()]),
                reads=["ag0_in"], writes=["ag0_out"], async_sem="cc0", inc=1)

        def ag0_p2(sb):
            sag = sb("a0_sag", [P, NCORES * G], F32)
            tse = sb("a0_tse", [P, NCORES * G], F32)
            op("sync", lambda e: e.dma_start(
                out=sag[:, :].rearrange("p (r c) -> p r c", r=NCORES),
                in_=bass.AP(ag0_out, 0, [[G, P], [P * G, NCORES], [1, G]])),
                reads=["ag0_out"], writes=[sag], dma=True)
            op("vector", lambda e: e.tensor_tensor(
                tse[:, :].rearrange("p (r c) -> p r c", r=NCORES),
                sag[:, :].rearrange("p (r c) -> p r c", r=NCORES),
                msel0[:, :].rearrange("p (r o) -> p r o", o=1)
                    .broadcast_to([P, NCORES, G]), ALU.mult),
                reads=[sag, msel0], writes=[tse])
            tset = tse.tensor if hasattr(tse, "tensor") else tse
            op("vector", lambda e: e.reduce_sum(
                xn_last[:, :],
                bass.AP(tset, 0, [[NCORES * G, P], [1, G], [G, NCORES]]),
                axis=AX.X), reads=[tse], writes=[xn_last])


        def emit_all():
            with nc.named_scope("p0"):
                p0_load()
            with nc.sbuf_tensor("srstd2", [1, SP], F32) as _ss:
                SS["t"] = _ss
                for name, fn in (("p1f", p1f), ("p1a", p1a)):
                    with nc.named_scope(name):
                        fn()
                trk.full_barrier()
            for name, fn in (("b1", b1), ("b2", b2), ("b3", b3)):
                with nc.named_scope(name):
                    fn()
            trk.full_barrier()

        def load_fm(dst, dram, rows, cols, dtype_tmp=None, scale=None, tmp=None,
                    tmpkey=None, tmp_cols=None):
            ng = rows // P
            srcap = bass.AP(dram.tensor, 0, [[cols, P], [P * cols, ng], [1, cols]])
            op("gpsimd", lambda e: e.dma_start(
                out=dst[:, 0:ng * cols].rearrange("p (g t) -> p g t", g=ng),
                in_=srcap),
                reads=[dram], writes=[dst], dma=True)
            if scale is not None:
                op("scalar", lambda e: e.activation(
                    dst[:, 0:ng * cols], dst[:, 0:ng * cols], AF.Copy, scale=scale),
                    reads=[dst], writes=[dst])

        def p0_load():
            for dst, src in ((cns, cns_d), (exp_b, expand_d), (id_b, ident_d),
                             (ones_r, ones_d), (msel0, msel0_d), (selA, selA_d),
                             (selB, selB_d), (mbit, mbit_d), (mhalo, mhalo_d)):
                op("sync", lambda e, d=dst, s=src: e.dma_start(out=d[:, :], in_=s),
                    reads=[src], writes=[dst], dma=True)
            op("sync", lambda e: e.dma_start(
                out=pv[:, :].rearrange("p (g t) -> p g t", g=G),
                in_=bass.AP(pv_d.tensor, 0, [[NPV, P], [P * NPV, G], [1, NPV]])),
                reads=[pv_d], writes=[pv], dma=True)
            op("vector", lambda e: e.memset(ones_c[:, :], 1.0), writes=[ones_c])
            op("vector", lambda e: e.memset(xn_last[:, :], 0.0), writes=[xn_last])
            op("vector", lambda e: e.memset(stA[:, :], 0.0), writes=[stA])
            op("vector", lambda e: e.memset(stB[:, :], 0.0), writes=[stB])
            wrow = dvrow(5)
            op("scalar", lambda e: e.activation(wrow, pvrow(4), AF.Exp),
                reads=[pv], writes=[dv])
            op("scalar", lambda e: e.activation(dvrow(0), dvrow(5), AF.Exp, scale=-1.0),
                reads=[dv], writes=[dv])
            op("scalar", lambda e: e.activation(dvrow(1), pvrow(5), AF.Exp, scale=-1.0),
                reads=[pv], writes=[dv])
            op("vector", lambda e: e.tensor_scalar(
                dvrow(6), dvrow(5), -2048.0, None, ALU.mult), reads=[dv], writes=[dv])
            op("scalar", lambda e: e.activation(dvrow(3), dvrow(6), AF.Exp),
                reads=[dv], writes=[dv])
            op("vector", lambda e: e.tensor_scalar(
                dvrow(6), dvrow(5), -2047.0, None, ALU.mult), reads=[dv], writes=[dv])
            op("scalar", lambda e: e.activation(dvrow(4), dvrow(6), AF.Exp),
                reads=[dv], writes=[dv])
            op("vector", lambda e: e.tensor_scalar(
                dvrow(2), pvrow(11), 0.5, None, ALU.mult), reads=[pv], writes=[dv])
            with nc.sbuf_tensor("wtmp0", [1, G], F32) as wtmp:
                op("sync", lambda e: e.dma_start(out=wtmp[0:1, 0:G], in_=rb_d),
                    reads=[rb_d], writes=[wtmp], dma=True)
                op("tensor", lambda e: e.matmul(pq[:, 0:G], ones_r[:, :],
                                                wtmp[0:1, 0:G], start=True, stop=True),
                    reads=[ones_r, wtmp], writes=[pq])
                op("vector", lambda e: e.tensor_copy(rbt[:, :], pq[:, 0:G]),
                    reads=[pq], writes=[rbt])
                trk.full_barrier()

        def ln_stats2(xbuf, nt, xb_tmp, sq_tmp, rows, keep_var=False):
            rowm = rows[0:1, 0:nt]
            rowr = rows[0:1, nt:2 * nt]
            rowt = rows[0:1, 2 * nt:3 * nt]
            KM, KR, KT = (rows.name + "_m", rows.name + "_r", rows.name + "_t")
            op("scalar", lambda e: e.copy(xb_tmp[:, 0:G * nt], xbuf[:, 0:G * nt]),
                reads=[xbuf], writes=[xb_tmp])
            op("scalar", lambda e: e.activation(
                sq_tmp[:, 0:G * nt], xbuf[:, 0:G * nt], AF.Square),
                reads=[xbuf], writes=[sq_tmp])

            def mm_sum(e, src, bank):
                last = None
                for kg in range(G):
                    last = e.matmul(bank, ones_c[:, 0:1],
                                    src[:, kg * nt:(kg + 1) * nt],
                                    start=(kg == 0), stop=(kg == G - 1))
                return last
            yield
            op("tensor", lambda e: mm_sum(e, xb_tmp, prs[0:1, 0:TS1]),
                reads=[xb_tmp, ones_c], writes=[prs])
            op("tensor", lambda e: mm_sum(e, sq_tmp, prs[32:33, 0:TS1]),
                reads=[sq_tmp, ones_c], writes=[prs])
            op("vector", lambda e: e.tensor_scalar(
                rowm, prs[0:1, 0:nt], 1.0 / D, None, ALU.mult),
                reads=[prs], writes=[KM])
            op("vector", lambda e: e.tensor_tensor(rowt, rowm, rowm, ALU.mult),
                reads=[KM], writes=[KT])
            op("vector", lambda e: e.scalar_tensor_tensor(
                rowr, prs[32:33, 0:nt], 1.0 / D, rowt, ALU.mult, ALU.subtract),
                reads=[prs, KT], writes=[KR])
            if keep_var:
                op("vector", lambda e: e.tensor_copy(rowt, rowr),
                    reads=[KR], writes=[KT])
            op("scalar", lambda e: e.activation(rowr, rowr, AF.Ln, bias=1e-5),
                reads=[KR], writes=[KR])
            op("scalar", lambda e: e.activation(rowr, rowr, AF.Exp, scale=-0.5),
                reads=[KR], writes=[KR])
            return rowm, rowr, rowt

        def bcast(row_ap, nt, bank, rows_key, dst=None):
            op("tensor", lambda e: e.matmul(bank[:, 0:nt], ones_r[:, :], row_ap,
                                            start=True, stop=True),
                reads=[rows_key, ones_r], writes=[bank])
            op("scalar", lambda e: e.copy(dst[:, 0:nt], bank[:, 0:nt]),
                reads=[bank], writes=[dst])

        def mm_group(bank, nt, lhs_fn, rhs_fn, nk, reads, extra=None, col0=0,
                     open_accum=False, close_accum=False):
            def go(e):
                last = None
                for k in range(nk):
                    last = e.matmul(bank[:, col0:col0 + nt], lhs_fn(k), rhs_fn(k),
                                    start=(k == 0 and not close_accum),
                                    stop=(k == nk - 1 and extra is None
                                          and not open_accum))
                if extra is not None:
                    last = e.matmul(bank[:, col0:col0 + nt], extra[0], extra[1],
                                    start=(nk == 0 and not close_accum),
                                    stop=True)
                return last
            op("tensor", go, reads=list(reads), writes=[bank])

        def alloc_ffn_weights(sb, tag):
            FW["bw"] = sb(f"ffw_bw{tag}", [P, G * D], BF16)
            FW["la"] = sb(f"ffw_la{tag}", [P, G * P], BF16)
            FW["lb"] = sb(f"ffw_lb{tag}", [P, D], BF16)
            FW["rw"] = sb(f"ffw_rw{tag}", [P, G * G], BF16)

        def load_ffn_weights(sb, tag):
            load_fm(FW["bw"], bw_d, D, D)
            yield
            load_fm(FW["la"], la_d, D, P)
            load_fm(FW["lb"], lb_d, P, D, scale=2.0)
            load_fm(FW["rw"], rw_d, D, G)
            yield

        def emit_ffn(y_bf, nt, resid_g, out_g, fb):
            ntile = (nt + P - 1) // P
            lg = fb["lg"]
            for mt in range(ntile):
                mw = min(P, nt - mt * P)
                B = mt * 48
                mm_group(pq, G, lambda k, mt=mt, mw=mw: y_bf[:, k * nt + mt * P:
                                                             k * nt + mt * P + mw],
                         lambda k: FW["rw"][:, k * G:(k + 1) * G], G,
                         [y_bf, FW["rw"]], col0=0)
                op("vector", lambda e, B=B, mw=mw: e.tensor_tensor(
                    lg[0:mw, B + 40:B + 48], pq[0:mw, 0:G],
                    rbt[0:mw, :], ALU.add),
                    reads=[pq, rbt], writes=[f"lg{mt}"])
                yield
            mm_group(pa0, nt, lambda k: FW["la"][:, k * P:(k + 1) * P],
                     lambda k: y_bf[:, k * nt:(k + 1) * nt], G, [FW["la"], y_bf])
            op("scalar", lambda e: e.copy(fb["axf"][:, 0:nt], pa0[:, 0:nt]),
                reads=[pa0], writes=["axf"])
            yield
            OB = (pm0, pm1, pq, pa0)
            for m in range(4):
                mm_group(OB[m], nt,
                         lambda k, m=m: FW["bw"][:, k * D + m * P:k * D + m * P + P],
                         lambda k: y_bf[:, k * nt:(k + 1) * nt], G,
                         [FW["bw"], y_bf] + (["axf"] if m == 3 else []),
                         open_accum=True)
                if m == 1:
                    yield
            yield
            wgb = fb["wgb"]

            def gate_op(opi, mt):
                mw = min(P, nt - mt * P)
                B = mt * 48
                lk = f"lg{mt}"
                if opi == 0:
                    op("vector", lambda e: e.max(
                        lg[0:mw, B + 8:B + 16], lg[0:mw, B + 40:B + 48]),
                        reads=[lk], writes=[lk])
                elif opi == 1:
                    op("vector", lambda e: e.tensor_scalar(
                        lg[0:mw, B + 16:B + 17], lg[0:mw, B + 8:B + 9], -1.0,
                        None, ALU.mult), reads=[lk], writes=[lk])
                elif opi == 2:
                    op("vector", lambda e: e.tensor_scalar(
                        lg[0:mw, B + 24:B + 32], lg[0:mw, B + 40:B + 48],
                        lg[0:mw, B + 9:B + 10], None, ALU.is_ge),
                        reads=[lk], writes=[lk])
                elif opi == 3:
                    op("scalar", lambda e: e.activation(
                        lg[0:mw, B + 32:B + 40], lg[0:mw, B + 40:B + 48], AF.Exp,
                        bias=lg[0:mw, B + 16:B + 17]),
                        reads=[lk], writes=[lk])
                elif opi == 4:
                    op("vector", lambda e: e.tensor_tensor(
                        lg[0:mw, B + 32:B + 40], lg[0:mw, B + 32:B + 40],
                        lg[0:mw, B + 24:B + 32], ALU.mult),
                        reads=[lk], writes=[lk])
                elif opi == 5:
                    op("vector", lambda e: e.reduce_sum(
                        lg[0:mw, B + 17:B + 18], lg[0:mw, B + 32:B + 40],
                        axis=AX.X), reads=[lk], writes=[lk])
                elif opi == 6:
                    op("vector", lambda e: e.reciprocal(
                        lg[0:mw, B + 18:B + 19], lg[0:mw, B + 17:B + 18]),
                        reads=[lk], writes=[lk])
                elif opi == 7:
                    op("vector", lambda e: e.tensor_scalar(
                        wgb[0:mw, mt * G:(mt + 1) * G], lg[0:mw, B + 32:B + 40],
                        lg[0:mw, B + 18:B + 19], None, ALU.mult),
                        reads=[lk], writes=[f"wgb{mt}"])
                elif opi == 8:
                    op("tensor", lambda e: e.transpose(
                        ptb[0:G, 0:P], wgb[:, mt * G:(mt + 1) * G], id_b[:, :]),
                        reads=[f"wgb{mt}", id_b], writes=[ptb])
                    op("scalar", lambda e: e.copy(
                        fb["wgT"][0:G, mt * P:mt * P + mw], ptb[0:G, 0:mw]),
                        reads=[ptb], writes=[f"wgT{mt}"])

            for opi in range(9):
                for mt in range(ntile):
                    gate_op(opi, mt)
                if opi in (3, 6):
                    yield
            yield
            wgTk = [f"wgT{mt}" for mt in range(ntile)]
            mm_group(pa1, nt, lambda k: exp_b[:, :],
                     lambda k: fb["wgT"][0:G, 0:nt], 1, [exp_b] + wgTk)
            op("scalar", lambda e: e.copy(fb["we"][:, 0:nt], pa1[:, 0:nt]),
                reads=[pa1], writes=[fb["we"]])
            op("vector", lambda e: e.tensor_tensor(
                fb["axw"][:, 0:nt], fb["axf"][:, 0:nt], fb["we"][:, 0:nt],
                ALU.mult), reads=["axf", fb["we"]], writes=[fb["axw"]])
            yield
            for m in range(G):
                bank = OB[m % 4]
                if m < 4:
                    mm_group(bank, nt, None, None, 0,
                             [FW["lb"], fb["axw"]],
                             extra=(FW["lb"][:, m * P:m * P + P],
                                    fb["axw"][:, 0:nt]), close_accum=True)
                else:
                    mm_group(bank, nt,
                             lambda k, m=m: FW["bw"][:, k * D + m * P:
                                                     k * D + m * P + P],
                             lambda k: y_bf[:, k * nt:(k + 1) * nt], G,
                             [FW["bw"], y_bf, fb["axw"], FW["lb"]],
                             extra=(FW["lb"][:, m * P:m * P + P],
                                    fb["axw"][:, 0:nt]))
                tb = fb["tep"][:, (m % 2) * nt:(m % 2) * nt + nt]
                tbk = f"tep{m % 2}"
                op("vector", lambda e, bank=bank, m=m, tb=tb: e.tensor_scalar(
                    tb, bank[:, 0:nt], 0.5, dvc(m, 2),
                    ALU.mult, ALU.add),
                    reads=[bank, dv], writes=[tbk])
                dst, dkey = out_g(m)
                op("gpsimd", lambda e, m=m, tb=tb: e.tensor_tensor(
                    dst, tb, resid_g(m), ALU.add),
                    reads=[tbk, fb["resid"]], writes=[dkey])
                yield

        def p1f():
            with ExitStack() as ph:
                sb = lambda *a: ph.enter_context(nc.sbuf_tensor(*a))
                alloc_ffn_weights(sb, "f")
                xsb = [sb("f_xsA", [P, G * TS1], F32),
                       sb("f_xsB", [P, G * TS1], F32)]
                xb = sb("f_xb", [P, G * TS1], BF16)
                sq = sb("f_sq", [P, G * TS1], BF16)
                xb2 = sb("f_xb2", [P, G * TS1], BF16)
                sq2 = sb("f_sq2", [P, G * TS1], BF16)
                rows = sb("f_rows", [1, 3 * TS1], F32)
                rows2 = sb("f_rows2", [1, 3 * TS1], F32)
                t1 = sb("f_t1", [P, G * TS1], F32)
                ybb = [sb("f_ybA", [P, G * TS1], BF16),
                       sb("f_ybB", [P, G * TS1], BF16)]
                h0 = sb("f_h0", [P, G * TS1], F32)
                x1b2 = [sb("f_x1A", [P, G * TS1], F32),
                        sb("f_x1B", [P, G * TS1], F32)]
                lg = sb("f_lg", [P, 192], F32)
                wgb = sb("f_wgb", [P, 4 * G], BF16)
                axf = sb("f_axf", [P, TS1], F32)
                wgT = sb("f_wgT", [G, TS1], BF16)
                we = sb("f_we", [P, TS1], BF16)
                axw = sb("f_axw", [P, TS1], BF16)
                tep = sb("f_tep", [P, 2 * TS1], F32)

                bbm = sb("f_bbm", [P, TS1], F32)
                bbr = sb("f_bbr", [P, TS1], F32)
                bbm2 = sb("f_bbm2", [P, TS1], F32)
                bbr2 = sb("f_bbr2", [P, TS1], F32)

                def f1(s):
                    xs, yb = xsb[s % 2], ybb[s % 2]
                    op("sync", lambda e, s=s: e.dma_start(
                        out=xs[:, :].rearrange("p (g t) -> p g t", g=G),
                        in_=fm_in(x_d, D, SP, s * TS1, TS1)),
                        reads=[x_d], writes=[xs], dma=True)
                    rowm, rowr, _ = yield from ln_stats2(xs, TS1, xb, sq, rows)
                    bcast(rowm, TS1, pb, rows.name + "_m", dst=bbm)
                    bcast(rowr, TS1, pa1, rows.name + "_r", dst=bbr)
                    yield
                    for g in range(G):
                        en = "gpsimd" if g < 3 else "vector"
                        op(en, lambda e, g=g, xs=xs: e.tensor_tensor(
                            t1[:, g * TS1:(g + 1) * TS1],
                            xs[:, g * TS1:(g + 1) * TS1], bbm[:, 0:TS1],
                            ALU.subtract), reads=[xs, bbm], writes=[t1])
                        op(en, lambda e, g=g, yb=yb: e.tensor_tensor(
                            yb[:, g * TS1:(g + 1) * TS1],
                            t1[:, g * TS1:(g + 1) * TS1], bbr[:, 0:TS1], ALU.mult),
                            reads=[t1, bbr], writes=[yb])
                        yield

                def f2(s):
                    xs, yb, x1 = xsb[s % 2], ybb[s % 2], x1b2[s % 2]
                    fb = {"lg": lg, "wgb": wgb, "wgT": wgT, "we": we, "axw": axw,
                          "axf": axf,
                          "tep": tep, "ptq": pa1, "resid": xs}
                    yield from emit_ffn(
                        yb, TS1, lambda g, xs=xs: xs[:, g * TS1:(g + 1) * TS1],
                        lambda g, x1=x1: (x1[:, g * TS1:(g + 1) * TS1], x1), fb)

                def f3(s):
                    x1 = x1b2[s % 2]
                    rowm, rowr, rowt = yield from ln_stats2(x1, TS1, xb2, sq2,
                                                            rows2, keep_var=True)
                    op("vector", lambda e: e.tensor_tensor(
                        rows2[0:1, 0:TS1], rowm, rowr, ALU.mult),
                        reads=[rows2.name + "_m", rows2.name + "_r"],
                        writes=[rows2.name + "_m"])
                    bcast(rows2[0:1, 0:TS1], TS1, pb, rows2.name + "_m",
                          dst=bbm2)
                    bcast(rowr, TS1, pa1, rows2.name + "_r", dst=bbr2)
                    yield
                    for g in range(G):
                        en = "gpsimd" if g < 3 else "vector"
                        op(en, lambda e, g=g, x1=x1: e.tensor_tensor(
                            h0[:, g * TS1:(g + 1) * TS1],
                            x1[:, g * TS1:(g + 1) * TS1], bbr2[:, 0:TS1],
                            ALU.mult), reads=[x1, bbr2], writes=[h0])
                        op(en, lambda e, g=g: e.tensor_tensor(
                            h0[:, g * TS1:(g + 1) * TS1],
                            h0[:, g * TS1:(g + 1) * TS1], bbm2[:, 0:TS1],
                            ALU.subtract),
                            reads=[h0, bbm2], writes=[h0])
                        yield
                    op("vector", lambda e: e.tensor_tensor(
                        rows2[0:1, 0:TS1], rowr, rowr, ALU.mult),
                        reads=[rows2.name + "_r"], writes=[rows2.name + "_m"])
                    op("vector", lambda e: e.tensor_tensor(
                        rows2[0:1, 0:TS1], rows2[0:1, 0:TS1], rowt, ALU.mult),
                        reads=[rows2.name + "_m", rows2.name + "_t"],
                        writes=[rows2.name + "_m"])
                    op("scalar", lambda e: e.activation(
                        rows2[0:1, 0:TS1], rows2[0:1, 0:TS1], AF.Ln, bias=1e-5),
                        reads=[rows2.name + "_m"], writes=[rows2.name + "_m"])
                    op("scalar", lambda e, s=s: e.activation(
                        SS["t"][0:1, s * TS1:(s + 1) * TS1], rows2[0:1, 0:TS1],
                        AF.Exp, scale=-0.5), reads=[rows2.name + "_m"],
                        writes=["srstd2"])
                    yield
                    op("gpsimd", lambda e, s=s, x1=x1: e.dma_start(
                        out=bass.AP(x1_sp, s * G * TS1, [[NS1 * G * TS1, P],
                                                         [1, G * TS1]]),
                        in_=x1[:, :]), reads=[x1], writes=["x1_sp"], dma=True)
                    op("gpsimd", lambda e, s=s: e.dma_start(
                        out=bass.AP(h0_sp, s * G * TS1, [[NS1 * G * TS1, P],
                                                         [1, G * TS1]]),
                        in_=h0[:, :]), reads=[h0], writes=["h0_sp"], dma=True)
                    yield
                    if s == NS1 - 1:
                        h0t = h0.tensor if hasattr(h0, "tensor") else h0
                        op("vector", lambda e, h0t=h0t: e.tensor_copy(
                            h0_end[:, :],
                            bass.AP(h0t, TS1 - 1, [[G * TS1, P], [TS1, G]])),
                            reads=[h0], writes=[h0_end])
                        yield

                ilv(f1(3), load_ffn_weights(sb, "f"))
                ilv(f2(3), f1(0))
                ilv(f3(3), f2(0), f1(1))
                ag0_p1(sb)
                ilv(f3(0), f2(1), f1(2))
                ilv(f3(1), f2(2))
                drain(f3(2))
                trk.full_barrier()

        def p1a():
            with ExitStack() as ph:
                sb = lambda *a: ph.enter_context(nc.sbuf_tensor(*a))
                wkb = sb("a_wk", [P, G * D], BF16)
                wvb = sb("a_wv", [P, G * D], BF16)
                wrb = sb("a_wr", [P, G * D], BF16)
                h0b = [sb("a_h0A", [P, G * TS1], F32),
                       sb("a_h0B", [P, G * TS1], F32)]
                xnb = [sb("a_xnA", [P, G * (TS1 + 1)], F32),
                       sb("a_xnB", [P, G * (TS1 + 1)], F32)]
                mkb = [sb("a_mkA", [P, G * TS1], BF16),
                       sb("a_mkB", [P, G * TS1], BF16)]
                mixb = sb("a_mix", [P, G * TS1], BF16)
                ekb = [sb("a_ekA", [P, G * TS1], F32),
                       sb("a_ekB", [P, G * TS1], F32)]
                ekvs = sb("a_ekv", [P, G * TS1], F32)
                srb = sb("a_sr", [P, G * TS1], BF16)
                scn = sb("a_scn", [P, TS1], F32)
                bb0 = sb("a_bb0", [P, TS1], F32)
                scn2 = pa1
                PITCH = TS1 + 1

                def e1(s):
                    p = s % 2
                    h0, xn, mixk = h0b[p], xnb[p], mkb[p]
                    xnt = xn.tensor if hasattr(xn, "tensor") else xn
                    op("sync", lambda e, s=s: e.dma_start(
                        out=h0[:, :],
                        in_=bass.AP(h0_sp, s * G * TS1,
                                    [[NS1 * G * TS1, P], [1, G * TS1]])),
                        reads=["h0_sp"], writes=[h0], dma=True)
                    bcast(SS["t"][0:1, s * TS1:(s + 1) * TS1], TS1, pb,
                          "srstd2", dst=bb0)
                    for g in range(G):
                        op("vector", lambda e, g=g: e.tensor_tensor(
                            xn[:, g * PITCH + 1:g * PITCH + 1 + TS1],
                            h0[:, g * TS1:(g + 1) * TS1], bb0[:, 0:TS1], ALU.mult),
                            reads=[h0, bb0], writes=[xn])
                    for g in range(G):
                        op("scalar", lambda e, g=g: e.activation(
                            xn[:, g * PITCH + 1:g * PITCH + 1 + TS1],
                            xn[:, g * PITCH + 1:g * PITCH + 1 + TS1],
                            AF.Identity, scale=pvc(g, 0), bias=pvc(g, 1)),
                            reads=[xn, pv], writes=[xn])
                    op("vector", lambda e: e.tensor_copy(
                        bass.AP(xnt, 0, [[G * PITCH, P], [PITCH, G]]),
                        xn_last[:, :]), reads=[xn_last], writes=[xn])
                    op("vector", lambda e: e.tensor_copy(
                        xn_last[:, :],
                        bass.AP(xnt, TS1, [[G * PITCH, P], [PITCH, G]])),
                        reads=[xn], writes=[xn_last])
                    op("vector", lambda e: e.tensor_tensor(
                        h0[:, :].rearrange("p (g t) -> p g t", g=G),
                        bass.AP(xnt, 1, [[G * PITCH, P], [PITCH, G], [1, TS1]]),
                        bass.AP(xnt, 0, [[G * PITCH, P], [PITCH, G], [1, TS1]]),
                        ALU.subtract), reads=[xn], writes=[h0])
                    for g in range(G):
                        op("vector", lambda e, g=g: e.scalar_tensor_tensor(
                            mixk[:, g * TS1:(g + 1) * TS1],
                            h0[:, g * TS1:(g + 1) * TS1], pvc(g, 6),
                            bass.AP(xnt, g * PITCH, [[G * PITCH, P], [1, TS1]]),
                            ALU.mult, ALU.add),
                            reads=[h0, pv, xn], writes=[mixk])

                def e2(s):
                    p = s % 2
                    h0, xn, mixk = h0b[p], xnb[p], mkb[p]
                    eks = ekb[p]
                    xnt = xn.tensor if hasattr(xn, "tensor") else xn
                    for m in range(G):
                        bank = pm0 if m % 2 == 0 else pm1
                        mm_group(bank, TS1,
                                 lambda k, m=m: wkb[:, k * D + m * P:
                                                    k * D + m * P + P],
                                 lambda k: mixk[:, k * TS1:(k + 1) * TS1], G,
                                 [wkb, mixk])
                        op("scalar", lambda e, bank=bank, m=m, eks=eks: e.activation(
                            eks[:, m * TS1:(m + 1) * TS1], bank[:, 0:TS1],
                            AF.Exp), reads=[bank], writes=[eks])
                        yield
                    for wb_, pj, kind in ((wvb, 7, "v"), (wrb, 8, "r")):
                        for g in range(G):
                            op("vector", lambda e, g=g, pj=pj, h0=h0, xnt=xnt:
                                e.scalar_tensor_tensor(
                                    mixb[:, g * TS1:(g + 1) * TS1],
                                    h0[:, g * TS1:(g + 1) * TS1], pvc(g, pj),
                                    bass.AP(xnt, g * PITCH,
                                            [[G * PITCH, P], [1, TS1]]),
                                    ALU.mult, ALU.add),
                                reads=[h0, pv, xn], writes=[mixb])
                        yield
                        for m in range(G):
                            bank = pm0 if m % 2 == 0 else pm1
                            mm_group(bank, TS1,
                                     lambda k, m=m, wb_=wb_: wb_[:, k * D + m * P:
                                                                 k * D + m * P + P],
                                     lambda k: mixb[:, k * TS1:(k + 1) * TS1], G,
                                     [wb_, mixb])
                            if kind == "v":
                                op("vector", lambda e, bank=bank, m=m, eks=eks:
                                    e.tensor_tensor(
                                        ekvs[:, m * TS1:(m + 1) * TS1],
                                        bank[:, 0:TS1],
                                        eks[:, m * TS1:(m + 1) * TS1], ALU.mult),
                                    reads=[bank, eks], writes=[ekvs])
                            else:
                                op("scalar", lambda e, bank=bank, m=m: e.activation(
                                    scn[:, 0:TS1], bank[:, 0:TS1], AF.Tanh,
                                    scale=0.5), reads=[bank], writes=[scn])
                                op("vector", lambda e, m=m: e.tensor_scalar(
                                    srb[:, m * TS1:(m + 1) * TS1], scn[:, 0:TS1],
                                    0.5, 0.5, ALU.mult, ALU.add),
                                    reads=[scn], writes=[srb])
                            yield
                    op("gpsimd", lambda e, s=s, eks=eks: e.dma_start(
                        out=bass.AP(ek_sp, s * G * TS1, [[NS1 * G * TS1, P],
                                                         [1, G * TS1]]),
                        in_=eks[:, :]), reads=[eks], writes=["ek_sp"], dma=True)
                    op("gpsimd", lambda e, s=s: e.dma_start(
                        out=bass.AP(ekv_sp, s * G * TS1, [[NS1 * G * TS1, P],
                                                          [1, G * TS1]]),
                        in_=ekvs[:, :]), reads=[ekvs], writes=["ekv_sp"], dma=True)
                    op("gpsimd", lambda e, s=s: e.dma_start(
                        out=bass.AP(sr_sp, s * G * TS1, [[NS1 * G * TS1, P],
                                                         [1, G * TS1]]),
                        in_=srb[:, :]), reads=[srb], writes=["sr_sp"], dma=True)
                    yield

                def e3(s):
                    eks = ekb[s % 2]
                    for arr, st_, pen in ((ekvs, stA, penA), (eks, stB, penB)):
                        for g in range(G):
                            op("vector", lambda e, g=g, arr=arr, st_=st_:
                                e.tensor_tensor_scan(
                                    scn2[:, 0:TS1], dvc(g, 0).broadcast_to([P, TS1]),
                                    arr[:, g * TS1:(g + 1) * TS1], st_[:, g:g + 1],
                                    ALU.mult, ALU.add),
                                reads=[dv, arr, st_], writes=[scn2])
                            if s == NS1 - 1:
                                op("vector", lambda e, g=g, pen=pen: e.tensor_copy(
                                    pen[:, g:g + 1], scn2[:, TS1 - 2:TS1 - 1]),
                                    reads=[scn2], writes=[pen])
                            op("vector", lambda e, g=g, st_=st_: e.tensor_copy(
                                st_[:, g:g + 1], scn2[:, TS1 - 1:TS1]),
                                reads=[scn2], writes=[st_])
                            yield
                    if s == NS1 - 1:
                        for arr, dst in ((eks, eke), (ekvs, ekve), (srb, sre)):
                            at = arr.tensor if hasattr(arr, "tensor") else arr
                            op("vector", lambda e, at=at, dst=dst, arr=arr:
                                e.tensor_copy(
                                    dst[:, :],
                                    bass.AP(at, TS1 - 1, [[G * TS1, P], [TS1, G]])),
                                reads=[arr], writes=[dst])
                        yield

                def lw_a():
                    load_fm(wkb, wk_d, D, D)
                    yield
                    load_fm(wvb, wv_d, D, D)
                    yield
                    load_fm(wrb, wr_d, D, D)
                    yield
                drain(lw_a())
                ag0_p2(sb)
                e1(0)
                e1(1)
                drain(e2(0))
                e1(2)
                ilv(e3(0), e2(1))
                e1(3)
                ilv(e3(1), e2(2))
                ilv(e3(2), e2(3))
                drain(e3(3))
                trk.full_barrier()

        def sel_from(sag, msk, col0, width, dst, dkey, tse):
            op("vector", lambda e: e.tensor_tensor(
                tse[:, 0:NCORES * width].rearrange("p (r c) -> p r c", r=NCORES),
                sag[:, :].rearrange("p (r c) -> p r c", r=NCORES)[:, :, col0:col0 + width],
                msk.rearrange("p (r o) -> p r o", o=1).broadcast_to([P, NCORES, width]),
                ALU.mult), reads=[sag, msk], writes=[tse])
            tset = tse.tensor if hasattr(tse, "tensor") else tse
            op("vector", lambda e: e.reduce_sum(
                dst, bass.AP(tset, 0, [[NCORES * 64, P], [1, width], [width, NCORES]]),
                axis=AX.X), reads=[tse], writes=[dkey])

        def ag1_p1(sb):
            pk = sb("g_pk", [P, 64], F32)
            for i, src_ in enumerate((stA, stB, penA, penB, eke, ekve, sre,
                                      h0_end)):
                op("vector", lambda e, i=i, src_=src_: e.tensor_copy(
                    pk[:, i * 8:(i + 1) * 8], src_[:, :]),
                    reads=[src_], writes=[pk])
            op("gpsimd", lambda e: e.dma_start(out=ag1_in.ap(), in_=pk[:, :]),
                reads=[pk], writes=["ag1_in"], dma=True)
            op("gpsimd", lambda e: e.collective_compute(
                "AllGather", ALU.bypass, replica_groups=[list(range(NCORES))],
                ins=[ag1_in.ap().opt()], outs=[ag1_out.ap().opt()]),
                reads=["ag1_in"], writes=["ag1_out"], async_sem="cc1", inc=1)

        def ag1_p2(sb):
            if True:
                pkunused = None
                sag = sb("g_sag", [P, NCORES * 64], F32)

                tse = sb("g_tse", [P, NCORES * 64], F32)
                dsp2 = sb("g_dsp2", [P, 16], F32)
                SAB = sb("g_sab", [P, 16], F32)
                SAB2 = sb("g_sab2", [P, 16], F32)
                selbuf = sb("g_selb", [P, 16], F32)
                dm = sb("g_dm", [P, 16], F32)
                hl = sb("g_hl", [P, 64], F32)
                op("gpsimd", lambda e: e.dma_start(
                    out=sag[:, :].rearrange("p (r c) -> p r c", r=NCORES),
                    in_=bass.AP(ag1_out, 0, [[64, P], [P * 64, NCORES], [1, 64]])),
                    reads=["ag1_out"], writes=[sag], dma=True)
                op("vector", lambda e: e.tensor_copy(dsp2[:, 0:8], dvrow(3)),
                    reads=[dv], writes=[dsp2])
                op("vector", lambda e: e.tensor_copy(dsp2[:, 8:16], dvrow(3)),
                    reads=[dv], writes=[dsp2])

                def fold(dst, selm, mb_off):
                    sel_from(sag, selm[:, 0:G], 0, 16, dst[:, :], dst, tse)
                    for j in (1, 2):
                        op("vector", lambda e, j=j: e.tensor_scalar(
                            dm[:, :], dsp2[:, :],
                            mbit[:, mb_off + 2 * (j - 1):mb_off + 2 * (j - 1) + 1],
                            None, ALU.mult), reads=[dsp2, mbit], writes=[dm])
                        op("vector", lambda e, j=j: e.tensor_scalar(
                            dm[:, :], dm[:, :],
                            mbit[:, mb_off + 2 * (j - 1) + 1:mb_off + 2 * (j - 1) + 2],
                            None, ALU.add), reads=[dm, mbit], writes=[dm])
                        op("vector", lambda e: e.tensor_tensor(
                            dst[:, :], dst[:, :], dm[:, :], ALU.mult),
                            reads=[dst, dm], writes=[dst])
                        sel_from(sag, selm[:, j * G:(j + 1) * G], 0, 16,
                                 selbuf[:, :], selbuf, tse)
                        op("vector", lambda e: e.tensor_tensor(
                            dst[:, :], dst[:, :], selbuf[:, :], ALU.add),
                            reads=[dst, selbuf], writes=[dst])
                fold(SAB, selA, 0)
                fold(SAB2, selB, 4)
                op("vector", lambda e: e.tensor_copy(carA[:, :], SAB[:, 0:8]),
                    reads=[SAB], writes=[carA])
                op("vector", lambda e: e.tensor_copy(carB[:, :], SAB[:, 8:16]),
                    reads=[SAB], writes=[carB])
                sel_from(sag, msel0[:, :], 16, 16, hl[:, 0:16], hl, tse)
                sel_from(sag, msel0[:, :], 32, 32, hl[:, 16:48], hl, tse)
                op("vector", lambda e: e.tensor_copy(dm[:, 0:8], dvrow(4)),
                    reads=[dv], writes=[dm])
                op("vector", lambda e: e.tensor_copy(dm[:, 8:16], dvrow(4)),
                    reads=[dv], writes=[dm])
                op("vector", lambda e: e.tensor_tensor(
                    SAB2[:, :], SAB2[:, :], dm[:, :], ALU.mult),
                    reads=[SAB2, dm], writes=[SAB2])
                op("vector", lambda e: e.tensor_tensor(
                    hl[:, 0:16], hl[:, 0:16], SAB2[:, :], ALU.add),
                    reads=[hl, SAB2], writes=[hl])
                op("vector", lambda e: e.tensor_tensor(
                    hl[:, 48:56], hl[:, 0:8], dvrow(1), ALU.mult),
                    reads=[hl, dv], writes=[hl])
                op("vector", lambda e: e.tensor_tensor(
                    hl[:, 48:56], hl[:, 48:56], hl[:, 24:32], ALU.add),
                    reads=[hl], writes=[hl])
                op("vector", lambda e: e.tensor_tensor(
                    hl[:, 56:64], hl[:, 8:16], dvrow(1), ALU.mult),
                    reads=[hl, dv], writes=[hl])
                op("vector", lambda e: e.tensor_tensor(
                    hl[:, 56:64], hl[:, 56:64], hl[:, 16:24], ALU.add),
                    reads=[hl], writes=[hl])
                op("vector", lambda e: e.tensor_scalar(
                    hl[:, 56:64], hl[:, 56:64], mhalo[:, 1:2], None, ALU.add),
                    reads=[hl, mhalo], writes=[hl])
                op("vector", lambda e: e.reciprocal(hl[:, 56:64], hl[:, 56:64]),
                    reads=[hl], writes=[hl])
                op("vector", lambda e: e.tensor_tensor(
                    hl[:, 48:56], hl[:, 48:56], hl[:, 56:64], ALU.mult),
                    reads=[hl], writes=[hl])
                srh = sb("g_srh", [P, 8], BF16)
                op("vector", lambda e: e.tensor_tensor(
                    srh[:, :], hl[:, 48:56], hl[:, 32:40], ALU.mult),
                    reads=[hl], writes=[srh])
                op("vector", lambda e: e.tensor_copy(srh_p[:, :], srh[:, :]),
                    reads=[srh], writes=[srh_p])
                op("vector", lambda e: e.tensor_copy(h0sel_p[:, :], hl[:, 40:48]),
                    reads=[hl], writes=[h0sel_p])

        def b1():
            with ExitStack() as ph:
                sb = lambda *a: ph.enter_context(nc.sbuf_tensor(*a))
                ag1_p1(sb)
                wob = sb("b1_wo", [P, G * D], BF16)
                frb_w = sb("b1_fwr", [P, G * D], BF16)
                eks = sb("b1_ek", [P, G * TS1], F32)
                ekvs = sb("b1_ekv", [P, G * TS1], F32)
                srs = sb("b1_sr", [P, G * TS1], BF16)
                h0 = sb("b1_h0", [P, G * TS1], F32)
                scA = sb("b1_scA", [P, G * (TS1 + 1)], F32)
                scB = sb("b1_scB", [P, G * (TS1 + 1)], F32)
                numb = sb("b1_num", [P, G * TS1], F32)
                denb = sb("b1_den", [P, G * TS1], F32)
                srout = sb("b1_srout", [P, G * TS1], BF16)
                xn2 = sb("b1_xn2", [P, G * (TS1 + 1)], F32)
                rows = sb("b1_rows", [1, 3 * TS1], F32)
                mixb = sb("b1_mix", [P, G * TS1], BF16)
                bb0 = sb("b1_bb0", [P, TS1], F32)
                bb1 = sb("b1_bb1", [P, TS1], F32)
                xb = srout
                sq = mixb
                load_fm(wob, wo_d, D, D)
                load_fm(frb_w, fwr_d, D, D)
                ag1_p2(sb)
                PITCH = TS1 + 1
                scAt = scA.tensor if hasattr(scA, "tensor") else scA
                scBt = scB.tensor if hasattr(scB, "tensor") else scB
                xn2t = xn2.tensor if hasattr(xn2, "tensor") else xn2

                def go_projh(e):
                    last = None
                    for m in range(G):
                        for k in range(G):
                            last = e.matmul(pq[:, m:m + 1],
                                            wob[:, k * D + m * P:k * D + m * P + P],
                                            srh_p[:, k:k + 1], start=(k == 0),
                                            stop=(k == G - 1))
                    return last
                op("tensor", go_projh, reads=[wob, srh_p], writes=[pq])
                hh = sb("b1_hh", [P, 16], F32)
                op("vector", lambda e: e.tensor_tensor(
                    hh[:, 0:8], pq[:, 0:8], h0sel_p[:, :], ALU.add),
                    reads=[pq, h0sel_p], writes=[hh])
                hb = sb("b1_hb", [P, 16], BF16)
                op("scalar", lambda e: e.copy(hb[:, 0:8], hh[:, 0:8]),
                    reads=[hh], writes=[hb])
                op("scalar", lambda e: e.activation(hb[:, 8:16], hh[:, 0:8],
                                                    AF.Square),
                    reads=[hh], writes=[hb])

                def go_hsum(e):
                    last = None
                    for k in range(G):
                        last = e.matmul(pq[0:1, 8:9], ones_c[:, 0:1], hb[:, k:k + 1],
                                        start=(k == 0), stop=(k == G - 1))
                    for k in range(G):
                        last = e.matmul(pq[0:1, 9:10], ones_c[:, 0:1],
                                        hb[:, 8 + k:9 + k],
                                        start=(k == 0), stop=(k == G - 1))
                    return last
                op("tensor", go_hsum, reads=[hb, ones_c], writes=[pq])
                hr = sb("b1_hr", [1, 4], F32)
                op("vector", lambda e: e.tensor_scalar(
                    hr[0:1, 0:2], pq[0:1, 8:10], 1.0 / D, None, ALU.mult),
                    reads=[pq], writes=[hr])
                op("vector", lambda e: e.tensor_tensor(
                    hr[0:1, 2:3], hr[0:1, 0:1], hr[0:1, 0:1], ALU.mult),
                    reads=[hr], writes=[hr])
                op("vector", lambda e: e.tensor_tensor(
                    hr[0:1, 1:2], hr[0:1, 1:2], hr[0:1, 2:3], ALU.subtract),
                    reads=[hr], writes=[hr])
                op("scalar", lambda e: e.activation(hr[0:1, 1:2], hr[0:1, 1:2],
                                                    AF.Ln, bias=1e-5),
                    reads=[hr], writes=[hr])
                op("scalar", lambda e: e.activation(hr[0:1, 1:2], hr[0:1, 1:2],
                                                    AF.Exp, scale=-0.5),
                    reads=[hr], writes=[hr])
                op("tensor", lambda e: e.matmul(pq[:, 10:11], ones_r[:, :],
                                                hr[0:1, 0:1], start=True, stop=True),
                    reads=[hr, ones_r], writes=[pq])
                op("tensor", lambda e: e.matmul(pq[:, 11:12], ones_r[:, :],
                                                hr[0:1, 1:2], start=True, stop=True),
                    reads=[hr, ones_r], writes=[pq])
                op("vector", lambda e: e.tensor_scalar(
                    hh[:, 8:16], hh[:, 0:8], pq[:, 10:11], None, ALU.subtract),
                    reads=[hh, pq], writes=[hh])
                op("vector", lambda e: e.tensor_scalar(
                    hh[:, 8:16], hh[:, 8:16], pq[:, 11:12], None, ALU.mult),
                    reads=[hh, pq], writes=[hh])
                op("vector", lambda e: e.tensor_tensor(
                    hh[:, 8:16], hh[:, 8:16], pvrow(2), ALU.mult),
                    reads=[hh, pv], writes=[hh])
                op("vector", lambda e: e.tensor_tensor(
                    hh[:, 8:16], hh[:, 8:16], pvrow(3), ALU.add),
                    reads=[hh, pv], writes=[hh])
                op("vector", lambda e: e.tensor_scalar(
                    xn2_last[:, :], hh[:, 8:16], mhalo[:, 0:1], None, ALU.mult),
                    reads=[hh, mhalo], writes=[xn2_last])
                op("vector", lambda e: e.tensor_copy(st2A[:, :], carA[:, :]),
                    reads=[carA], writes=[st2A])
                op("vector", lambda e: e.tensor_copy(st2B[:, :], carB[:, :]),
                    reads=[carB], writes=[st2B])

                def g1(s):
                    for (dst, spt, key) in ((eks, ek_sp, "ek_sp"),
                                            (ekvs, ekv_sp, "ekv_sp"),
                                            (srs, sr_sp, "sr_sp")):
                        op("sync", lambda e, dst=dst, spt=spt, s=s: e.dma_start(
                            out=dst[:, :],
                            in_=bass.AP(spt, s * G * TS1,
                                        [[NS1 * G * TS1, P], [1, G * TS1]])),
                            reads=[key], writes=[dst], dma=True)

                def g1b(s):
                    op("sync", lambda e, s=s: e.dma_start(
                        out=h0[:, :],
                        in_=bass.AP(h0_sp, s * G * TS1,
                                    [[NS1 * G * TS1, P], [1, G * TS1]])),
                        reads=["h0_sp"], writes=[h0], dma=True)

                def g2(s):
                    for (sc_, sct_, st_, arr) in ((scA, scAt, st2A, ekvs),
                                                  (scB, scBt, st2B, eks)):
                        for g in range(G):
                            op("vector", lambda e, g=g, sc_=sc_, st_=st_:
                                e.tensor_copy(
                                    sc_[:, g * PITCH:g * PITCH + 1], st_[:, g:g + 1]),
                                reads=[st_], writes=[sc_])
                            op("vector", lambda e, g=g, sc_=sc_, st_=st_, arr=arr:
                                e.tensor_tensor_scan(
                                    sc_[:, g * PITCH + 1:g * PITCH + 1 + TS1],
                                    dvc(g, 0).broadcast_to([P, TS1]),
                                    arr[:, g * TS1:(g + 1) * TS1], st_[:, g:g + 1],
                                    ALU.mult, ALU.add),
                                reads=[dv, arr, st_], writes=[sc_])
                            op("vector", lambda e, g=g, sc_=sc_, st_=st_:
                                e.tensor_copy(
                                    st_[:, g:g + 1],
                                    sc_[:, g * PITCH + TS1:g * PITCH + TS1 + 1]),
                                reads=[sc_], writes=[st_])
                            yield
                    for g in range(G):
                        op("vector", lambda e, g=g: e.scalar_tensor_tensor(
                            numb[:, g * TS1:(g + 1) * TS1],
                            bass.AP(scAt, g * PITCH, [[G * PITCH, P], [1, TS1]]),
                            dvc(g, 1), ekvs[:, g * TS1:(g + 1) * TS1],
                            ALU.mult, ALU.add),
                            reads=[scA, dv, ekvs], writes=[numb])
                        op("vector", lambda e, g=g: e.scalar_tensor_tensor(
                            denb[:, g * TS1:(g + 1) * TS1],
                            bass.AP(scBt, g * PITCH, [[G * PITCH, P], [1, TS1]]),
                            dvc(g, 1), eks[:, g * TS1:(g + 1) * TS1],
                            ALU.mult, ALU.add),
                            reads=[scB, dv, eks], writes=[denb])
                        yield
                    op("scalar", lambda e: e.activation(denb[:, :], denb[:, :], AF.Ln),
                        reads=[denb], writes=[denb])
                    yield
                    op("scalar", lambda e: e.activation(denb[:, :], denb[:, :],
                                                        AF.Exp, scale=-1.0),
                        reads=[denb], writes=[denb])
                    yield
                    op("gpsimd", lambda e: e.tensor_tensor(
                        numb[:, :], numb[:, :], denb[:, :], ALU.mult),
                        reads=[numb, denb], writes=[numb])
                    yield
                    op("gpsimd", lambda e: e.tensor_tensor(
                        srout[:, :], numb[:, :], srs[:, :], ALU.mult),
                        reads=[numb, srs], writes=[srout])
                    yield

                def g3(s):
                    h1 = h0
                    t1 = denb
                    for m in range(G):
                        bank = pm0 if m % 2 == 0 else pm1
                        mm_group(bank, TS1,
                                 lambda k, m=m: wob[:, k * D + m * P:k * D + m * P + P],
                                 lambda k: srout[:, k * TS1:(k + 1) * TS1], G,
                                 [wob, srout])
                        op("vector", lambda e, bank=bank, m=m: e.tensor_tensor(
                            h1[:, m * TS1:(m + 1) * TS1], bank[:, 0:TS1],
                            h0[:, m * TS1:(m + 1) * TS1], ALU.add),
                            reads=[bank, h0], writes=[h0])
                        yield
                    op("gpsimd", lambda e, s=s: e.dma_start(
                        out=bass.AP(h1_sp, s * G * TS1, [[NS1 * G * TS1, P],
                                                         [1, G * TS1]]),
                        in_=h1[:, :]), reads=[h0], writes=["h1_sp"], dma=True)
                    yield
                    rowm, rowr, _ = yield from ln_stats2(h1, TS1, xb, sq, rows)
                    yield
                    bcast(rowm, TS1, pb, rows.name + "_m", dst=bb0)
                    bcast(rowr, TS1, pa1, rows.name + "_r", dst=bb1)
                    yield
                    for g in range(G):
                        en = "gpsimd" if g < 3 else "vector"
                        op(en, lambda e, g=g: e.tensor_tensor(
                            xn2[:, g * PITCH + 1:g * PITCH + 1 + TS1],
                            h1[:, g * TS1:(g + 1) * TS1], bb0[:, 0:TS1],
                            ALU.subtract), reads=[h0, bb0], writes=[xn2])
                        op(en, lambda e, g=g: e.tensor_tensor(
                            xn2[:, g * PITCH + 1:g * PITCH + 1 + TS1],
                            xn2[:, g * PITCH + 1:g * PITCH + 1 + TS1],
                            bb1[:, 0:TS1], ALU.mult),
                            reads=[xn2, bb1], writes=[xn2])
                        op("scalar", lambda e, g=g: e.activation(
                            xn2[:, g * PITCH + 1:g * PITCH + 1 + TS1],
                            xn2[:, g * PITCH + 1:g * PITCH + 1 + TS1],
                            AF.Identity, scale=pvc(g, 2), bias=pvc(g, 3)),
                            reads=[xn2, pv], writes=[xn2])
                        yield
                    op("vector", lambda e: e.tensor_copy(
                        bass.AP(xn2t, 0, [[G * PITCH, P], [PITCH, G]]),
                        xn2_last[:, :]), reads=[xn2_last], writes=[xn2])
                    op("vector", lambda e: e.tensor_copy(
                        xn2_last[:, :],
                        bass.AP(xn2t, TS1, [[G * PITCH, P], [PITCH, G]])),
                        reads=[xn2], writes=[xn2_last])
                    yield
                    dbuf2 = h0
                    op("vector", lambda e: e.tensor_tensor(
                        dbuf2[:, 0:G * TS1].rearrange("p (g t) -> p g t", g=G),
                        bass.AP(xn2t, 1, [[G * PITCH, P], [PITCH, G], [1, TS1]]),
                        bass.AP(xn2t, 0, [[G * PITCH, P], [PITCH, G], [1, TS1]]),
                        ALU.subtract), reads=[xn2], writes=[h0])
                    yield
                    for g in range(G):
                        op("vector", lambda e, g=g: e.scalar_tensor_tensor(
                            mixb[:, g * TS1:(g + 1) * TS1],
                            dbuf2[:, g * TS1:(g + 1) * TS1], pvc(g, 9),
                            bass.AP(xn2t, g * PITCH, [[G * PITCH, P], [1, TS1]]),
                            ALU.mult, ALU.add),
                            reads=[h0, pv, xn2], writes=[mixb])
                        yield
                    op("gpsimd", lambda e, s=s: e.dma_start(
                        out=bass.AP(mfk_sp, s * G * TS1, [[NS1 * G * TS1, P],
                                                          [1, G * TS1]]),
                        in_=mixb[:, :]), reads=[mixb], writes=["mfk_sp"], dma=True)
                    yield
                    for g in range(G):
                        op("vector", lambda e, g=g: e.scalar_tensor_tensor(
                            mixb[:, g * TS1:(g + 1) * TS1],
                            dbuf2[:, g * TS1:(g + 1) * TS1], pvc(g, 10),
                            bass.AP(xn2t, g * PITCH, [[G * PITCH, P], [1, TS1]]),
                            ALU.mult, ALU.add),
                            reads=[h0, pv, xn2], writes=[mixb])
                        yield
                    for m in range(G):
                        bank = pm0 if m % 2 == 0 else pm1
                        mm_group(bank, TS1,
                                 lambda k, m=m: frb_w[:, k * D + m * P:
                                                      k * D + m * P + P],
                                 lambda k: mixb[:, k * TS1:(k + 1) * TS1], G,
                                 [frb_w, mixb])
                        op("scalar", lambda e, bank=bank: e.activation(
                            pa1[:, 0:TS1], bank[:, 0:TS1], AF.Tanh, scale=0.5),
                            reads=[bank], writes=[pa1])
                        op("vector", lambda e, m=m: e.tensor_scalar(
                            numb[:, m * TS1:(m + 1) * TS1], pa1[:, 0:TS1],
                            0.5, 0.5, ALU.mult, ALU.add),
                            reads=[pa1], writes=[numb])
                        yield
                    op("gpsimd", lambda e, s=s: e.dma_start(
                        out=bass.AP(sgf_sp, s * G * TS1, [[NS1 * G * TS1, P],
                                                          [1, G * TS1]]),
                        in_=numb[:, :]), reads=[numb], writes=["sgf_sp"], dma=True)
                    yield

                g1(0)
                g1b(0)
                drain(g2(0))
                g1(1)
                ilv(g3(0), g2(1))
                g1b(1)
                g1(2)
                ilv(g3(1), g2(2))
                g1b(2)
                g1(3)
                ilv(g3(2), g2(3))
                g1b(3)
                drain(g3(3))
                trk.full_barrier()

        def b2():
            with ExitStack() as ph:
                sb = lambda *a: ph.enter_context(nc.sbuf_tensor(*a))
                fkb = sb("b2_fwk", [P, G * FD], BF16)
                fvb = sb("b2_fwv", [P, 32 * D], BF16)
                for h in range(4):
                    q = "gpsimd"
                    srcap = bass.AP(fwk_d.tensor, h * D,
                                    [[FD, P], [P * FD, G], [1, D]])
                    op(q, lambda e, srcap=srcap, h=h: e.dma_start(
                        out=fkb[:, :].rearrange("p (g t) -> p g t", g=G)
                        [:, :, h * D:(h + 1) * D], in_=srcap),
                        reads=[fwk_d], writes=[f"fkbH{h}"], dma=True)
                for h in range(4):
                    q = "gpsimd"
                    srcap = bass.AP(fwv_d.tensor, h * 8 * P * D,
                                    [[D, P], [P * D, G], [1, D]])
                    op(q, lambda e, srcap=srcap, h=h: e.dma_start(
                        out=fvb[:, h * 8 * D:(h + 1) * 8 * D]
                        .rearrange("p (g t) -> p g t", g=G), in_=srcap),
                        reads=[fwv_d], writes=[fvb], dma=True)
                mfk = sb("b2_mfk", [P, G * TS2], BF16)
                h1q = sb("b2_h1", [P, G * TS2], F32)
                sgq = sb("b2_sg", [P, G * TS2], F32)
                rfk = sb("b2_rfk", [P, 32 * TS2], BF16)
                tv = sb("b2_tv", [P, 2 * TS2], F32)
                x2p = h1q
                for s2 in range(NS2):
                    s = s2
                    off = 0
                    op("sync", lambda e, s=s, off=off: e.dma_start(
                        out=mfk[:, :].rearrange("p (g t) -> p g t", g=G),
                        in_=bass.AP(mfk_sp, s * G * TS1 + off,
                                    [[NS1 * G * TS1, P], [TS1, G], [1, TS2]])),
                        reads=["mfk_sp"], writes=[mfk], dma=True)
                    op("sync", lambda e, s=s, off=off: e.dma_start(
                        out=h1q[:, :].rearrange("p (g t) -> p g t", g=G),
                        in_=bass.AP(h1_sp, s * G * TS1 + off,
                                    [[NS1 * G * TS1, P], [TS1, G], [1, TS2]])),
                        reads=["h1_sp"], writes=[h1q], dma=True)
                    op("sync", lambda e, s=s, off=off: e.dma_start(
                        out=sgq[:, :].rearrange("p (g t) -> p g t", g=G),
                        in_=bass.AP(sgf_sp, s * G * TS1 + off,
                                    [[NS1 * G * TS1, P], [TS1, G], [1, TS2]])),
                        reads=["sgf_sp"], writes=[sgq], dma=True)
                    for fo in range(32):
                        bank = pm0 if fo % 2 == 0 else pm1
                        mm_group(bank, TS2,
                                 lambda k, fo=fo: fkb[:, k * FD + fo * P:
                                                      k * FD + fo * P + P],
                                 lambda k: mfk[:, k * TS2:(k + 1) * TS2], G,
                                 [f"fkbH{fo // 8}", mfk])
                        op("scalar", lambda e, bank=bank, fo=fo: e.activation(
                            rfk[:, fo * TS2:(fo + 1) * TS2], bank[:, 0:TS2],
                            AF.Relu), reads=[bank], writes=[rfk])
                        op("scalar", lambda e, fo=fo: e.activation(
                            rfk[:, fo * TS2:(fo + 1) * TS2],
                            rfk[:, fo * TS2:(fo + 1) * TS2],
                            AF.Square), reads=[rfk], writes=[rfk])
                    for m in range(G):
                        bank = pa0 if m % 2 == 0 else pa1
                        mm_group(bank, TS2,
                                 lambda k, m=m: fvb[:, k * D + m * P:
                                                    k * D + m * P + P],
                                 lambda k: rfk[:, k * TS2:(k + 1) * TS2], 32,
                                 [fvb, rfk])
                        tvs = tv[:, (m % 2) * TS2:(m % 2) * TS2 + TS2]
                        tvk = f"tv{m % 2}"
                        op("vector", lambda e, bank=bank, m=m, tvs=tvs: e.tensor_tensor(
                            tvs, bank[:, 0:TS2],
                            sgq[:, m * TS2:(m + 1) * TS2], ALU.mult),
                            reads=[bank, sgq], writes=[tvk])
                        op("gpsimd", lambda e, m=m, tvs=tvs: e.tensor_tensor(
                            x2p[:, m * TS2:(m + 1) * TS2], tvs,
                            h1q[:, m * TS2:(m + 1) * TS2], ALU.add),
                            reads=[tvk, h1q], writes=[x2p])
                    op("gpsimd", lambda e, s=s, off=off: e.dma_start(
                        out=bass.AP(x2p_sp, s * G * TS1 + off,
                                    [[NS1 * G * TS1, P], [TS1, G], [1, TS2]]),
                        in_=x2p[:, :].rearrange("p (g t) -> p g t", g=G)),
                        reads=[x2p], writes=["x2p_sp"], dma=True)
                trk.full_barrier()

        def b3():
            with ExitStack() as ph:
                sb = lambda *a: ph.enter_context(nc.sbuf_tensor(*a))
                alloc_ffn_weights(sb, "c")
                x2b = [sb("c_x2A", [P, G * TS1], F32),
                       sb("c_x2B", [P, G * TS1], F32)]
                x1b = sb("c_x1", [P, G * TS1], F32)
                xb = sb("c_xb", [P, G * TS1], BF16)
                sq = sb("c_sq", [P, G * TS1], BF16)
                rows = sb("c_rows", [1, 3 * TS1], F32)
                t1 = sb("c_t1", [P, G * TS1], F32)
                ybb = [sb("c_ybA", [P, G * TS1], BF16),
                       sb("c_ybB", [P, G * TS1], BF16)]
                x3 = sb("c_x3", [P, G * TS1], F32)
                lg = sb("c_lg", [P, 192], F32)
                wgb = sb("c_wgb", [P, 4 * G], BF16)
                axf = sb("c_axf", [P, TS1], F32)
                wgT = sb("c_wgT", [G, TS1], BF16)
                we = sb("c_we", [P, TS1], BF16)
                axw = sb("c_axw", [P, TS1], BF16)
                tep = sb("c_tep", [P, 2 * TS1], F32)
                bbm = sb("c_bbm", [P, TS1], F32)
                bbr = sb("c_bbr", [P, TS1], F32)

                def c1(s):
                    x2, yb = x2b[s % 2], ybb[s % 2]
                    op("sync", lambda e, s=s, x2=x2: e.dma_start(
                        out=x2[:, :],
                        in_=bass.AP(x2p_sp, s * G * TS1,
                                    [[NS1 * G * TS1, P], [1, G * TS1]])),
                        reads=["x2p_sp"], writes=[x2], dma=True)
                    op("sync", lambda e, s=s: e.dma_start(
                        out=x1b[:, :],
                        in_=bass.AP(x1_sp, s * G * TS1,
                                    [[NS1 * G * TS1, P], [1, G * TS1]])),
                        reads=["x1_sp"], writes=[x1b], dma=True)
                    op("vector", lambda e, x2=x2: e.tensor_tensor(
                        x2[:, :], x2[:, :], x1b[:, :], ALU.add),
                        reads=[x2, x1b], writes=[x2])
                    yield
                    rowm, rowr, _ = yield from ln_stats2(x2, TS1, xb, sq, rows)
                    bcast(rowm, TS1, pb, rows.name + "_m", dst=bbm)
                    bcast(rowr, TS1, pa1, rows.name + "_r", dst=bbr)
                    yield
                    for g in range(G):
                        en = "gpsimd" if g < 3 else "vector"
                        op(en, lambda e, g=g, x2=x2: e.tensor_tensor(
                            t1[:, g * TS1:(g + 1) * TS1],
                            x2[:, g * TS1:(g + 1) * TS1], bbm[:, 0:TS1],
                            ALU.subtract), reads=[x2, bbm], writes=[t1])
                        op(en, lambda e, g=g, yb=yb: e.tensor_tensor(
                            yb[:, g * TS1:(g + 1) * TS1],
                            t1[:, g * TS1:(g + 1) * TS1], bbr[:, 0:TS1], ALU.mult),
                            reads=[t1, bbr], writes=[yb])
                        yield

                def c2(s):
                    x2, yb = x2b[s % 2], ybb[s % 2]
                    fb = {"lg": lg, "wgb": wgb, "wgT": wgT, "we": we, "axw": axw,
                          "axf": axf,
                          "tep": tep, "ptq": pa1, "resid": x2}
                    yield from emit_ffn(
                        yb, TS1, lambda g, x2=x2: x2[:, g * TS1:(g + 1) * TS1],
                        lambda g: (x3[:, g * TS1:(g + 1) * TS1], x3), fb)
                    op("sync", lambda e, s=s: e.dma_start(
                        out=bass.AP(y_d.tensor, s * TS1,
                                    [[SP, P], [P * SP, G], [1, TS1]]),
                        in_=x3[:, :].rearrange("p (g t) -> p g t", g=G)),
                        reads=[x3], writes=["y"], dma=True)
                    yield

                ilv(c1(0), load_ffn_weights(sb, "c"))
                ilv(c2(0), c1(1))
                ilv(c2(1), c1(2))
                ilv(c2(2), c1(3))
                drain(c2(3))
                trk.full_barrier()

        block = outer.enter_context(nc.Block())

        @block.sync
        def _(sync):
            emit_all()

    return nc


def _host_inputs(inputs):
    f32 = np.float32
    x = np.ascontiguousarray(np.swapaxes(np.asarray(inputs["L0_attn_x"], f32), 1, 2))
    pvec = np.zeros((D, NPV), f32)
    pvec[:, 0] = inputs["L0_attn_ln1_w"]
    pvec[:, 1] = inputs["L0_attn_ln1_b"]
    pvec[:, 2] = inputs["L0_attn_ln2_w"]
    pvec[:, 3] = inputs["L0_attn_ln2_b"]
    pvec[:, 4] = inputs["L0_attn_time_decay"]
    pvec[:, 5] = inputs["L0_attn_time_first"]
    pvec[:, 6] = np.asarray(inputs["L0_attn_time_mix_k"], f32).ravel()
    pvec[:, 7] = np.asarray(inputs["L0_attn_time_mix_v"], f32).ravel()
    pvec[:, 8] = np.asarray(inputs["L0_attn_time_mix_r"], f32).ravel()
    pvec[:, 9] = np.asarray(inputs["L0_attn_f_time_mix_k"], f32).ravel()
    pvec[:, 10] = np.asarray(inputs["L0_attn_f_time_mix_r"], f32).ravel()
    pvec[:, 11] = inputs["L0_ffn_base_b"]
    import ml_dtypes
    cns = np.zeros((P, 8), f32)
    for j, v in enumerate([0.5, -0.5, -1.0, 1e-5, 2.0, 1.0, 0.0, 0.0]):
        cns[:, j] = v
    ident = np.eye(P, dtype=ml_dtypes.bfloat16)
    expand = np.zeros((G, P), ml_dtypes.bfloat16)
    for e in range(G):
        expand[e, e * 16:(e + 1) * 16] = 1.0
    onesrow = np.ones((1, P), f32)
    shared = dict(
        wk=np.ascontiguousarray(np.asarray(inputs["L0_attn_w_key"], f32)),
        wv=np.ascontiguousarray(np.asarray(inputs["L0_attn_w_value"], f32)),
        wr=np.ascontiguousarray(np.asarray(inputs["L0_attn_w_recept"], f32)),
        wo=np.ascontiguousarray(np.asarray(inputs["L0_attn_w_output"], f32)),
        fwk=np.ascontiguousarray(np.asarray(inputs["L0_attn_f_w_key"], f32)),
        fwr=np.ascontiguousarray(np.asarray(inputs["L0_attn_f_w_recept"], f32)),
        fwv=np.ascontiguousarray(np.asarray(inputs["L0_attn_f_w_value"], f32)),
        bw=np.ascontiguousarray(np.asarray(inputs["L0_ffn_base_w"], f32).T),
        la=np.ascontiguousarray(
            np.asarray(inputs["L0_ffn_lora_A"], f32).transpose(1, 0, 2)
            .reshape(D, 8 * 16)),
        lb=np.ascontiguousarray(
            np.asarray(inputs["L0_ffn_lora_B"], f32).reshape(8 * 16, D)),
        rw=np.ascontiguousarray(np.asarray(inputs["L0_ffn_router_w"], f32).T),
        rb=np.ascontiguousarray(np.asarray(inputs["L0_ffn_router_b"], f32)
                                .reshape(1, G)),
        pvec=pvec, cns=cns, ident=ident, expand=expand, onesrow=onesrow,
    )
    in_maps = []
    for cid in range(NCORES):
        b, q = cid // 4, cid % 4
        m = dict(shared)
        m["x"] = np.ascontiguousarray(x[b, :, q * SP:(q + 1) * SP])
        msel0 = np.zeros((P, G), f32)
        if q > 0:
            msel0[:, cid - 1] = 1.0
        m["msel0"] = msel0
        selA = np.zeros((P, 3 * G), f32)
        selB = np.zeros((P, 3 * G), f32)
        mbits = np.zeros((P, 8), f32)
        for j in range(3):
            if j < q:
                selA[:, j * G + (b * 4 + j)] = 1.0
        mbits[:, 0] = 1.0 if 1 < q or 1 == q else 0.0
        mbits[:, 0] = 1.0 if 1 < q else 0.0
        mbits[:, 1] = 1.0 - mbits[:, 0]
        mbits[:, 2] = 1.0 if 2 < q else 0.0
        mbits[:, 3] = 1.0 - mbits[:, 2]
        for j in range(3):
            if j < q - 1:
                selB[:, j * G + (b * 4 + j)] = 1.0
        mbits[:, 4] = 1.0 if 1 < q - 1 else 0.0
        mbits[:, 5] = 1.0 - mbits[:, 4]
        mbits[:, 6] = 1.0 if 2 < q - 1 else 0.0
        mbits[:, 7] = 1.0 - mbits[:, 6]
        m["selA"] = selA
        m["selB"] = selB
        m["mbits"] = mbits
        mh = np.zeros((P, 2), f32)
        mh[:, 0] = 0.0 if q == 0 else 1.0
        mh[:, 1] = 1.0 - mh[:, 0]
        m["mhalo"] = mh
        in_maps.append(m)
    return in_maps


_NC_CACHE = {}


def kernel(**inputs):
    in_maps = _host_inputs(inputs)
    if "nc" not in _NC_CACHE:
        _NC_CACHE["nc"] = build_program()
    nc = _NC_CACHE["nc"]
    res = run_bass_kernel_spmd(nc, in_maps, core_ids=list(range(NCORES)))
    out = np.empty((2, D, 8192), np.float32)
    for cid in range(NCORES):
        b, q = cid // 4, cid % 4
        out[b, :, q * SP:(q + 1) * SP] = res.results[cid]["y"]
    return np.ascontiguousarray(np.swapaxes(out, 1, 2))
